# version 13
# baseline (speedup 1.0000x reference)
import zlib
import numpy as np
import ml_dtypes
import concourse.bass as bass
import concourse.mybir as mybir
from concourse.bass_utils import run_bass_kernel_spmd
from contextlib import ExitStack

F32 = mybir.dt.float32
BF16 = mybir.dt.bfloat16
FP8 = mybir.dt.float8e4
AF = mybir.ActivationFunctionType
OP = mybir.AluOpType
AX = mybir.AxisListType
bfd = ml_dtypes.bfloat16

D_MODEL = 2048
D_INNER = 4096
CHUNK = 128
KFOR = 4
T_OWN = 2048
T_FOR = KFOR * CHUNK
HALO = 3
T_ALL = T_OWN + T_FOR + HALO
T_CONV = T_OWN + T_FOR
NCH = T_CONV // CHUNK
XB = 35
EPS = 1e-5
BIG = 8192.0

ENGINES = ("pe", "dve", "act", "pool", "sp")
NSUBQ = 6
QUEUES = tuple(f"qsp_{i}" for i in range(NSUBQ)) + ("qact",) + tuple(f"qpool_{i}" for i in range(NSUBQ))
Q_OWNER = {**{f"qsp_{i}": "sp" for i in range(NSUBQ)}, "qact": "act",
           **{f"qpool_{i}": "pool" for i in range(NSUBQ)}}


class Em:

    def __init__(self, nc):
        self.nc = nc
        self.sems = {}
        self.cnt = {s: 0 for s in ENGINES + QUEUES}
        self.pend = {e: [] for e in ENGINES}
        self.last_w = {}
        self.readers = {}
        self.wm = {}

    def init_sems(self, stack):
        for s in ENGINES + QUEUES:
            self.sems[s] = stack.enter_context(self.nc.semaphore("sem_" + s))

    SAME_ENGINE_EDGES = True

    def _need(self, eng, reads, writes):
        need = {}
        se = self.SAME_ENGINE_EDGES
        for k in reads:
            lw = self.last_w.get(k)
            if lw is not None and (se or lw[0] != eng):
                need[lw[0]] = max(need.get(lw[0], 0), lw[1])
        for k in writes:
            lw = self.last_w.get(k)
            if lw is not None and (se or lw[0] != eng):
                need[lw[0]] = max(need.get(lw[0], 0), lw[1])
            for st, c in self.readers.get(k, {}).items():
                if se or st != eng:
                    need[st] = max(need.get(st, 0), c)
        out = {}
        for st, c in need.items():
            if st in QUEUES:
                c = self.cnt[st]
            if st == eng:
                if c >= self.cnt[eng] + 1:
                    continue
            if self.wm.get((eng, st), 0) < c:
                out[st] = c
                self.wm[(eng, st)] = c
        return out

    def _track(self, stream, cnt, reads, writes):
        for k in reads:
            self.readers.setdefault(k, {})[stream] = cnt
        for k in writes:
            self.last_w[k] = (stream, cnt)
            self.readers[k] = {}

    def op(self, eng, fn, reads=(), writes=()):
        need = self._need(eng, reads, writes)
        self.cnt[eng] += 1
        cnt = self.cnt[eng]
        sems = self.sems

        def closure(e, need=need, fn=fn):
            for st, c in need.items():
                e.wait_ge(sems[st], c)
            fn(e).then_inc(sems[eng], 1)
        self.pend[eng].append(closure)
        self._track(eng, cnt, reads, writes)

    def _subq(self, q, writes):
        if q in ("qpool", "qsp"):
            tag = writes[0] if writes else "x"
            return f"{q}_{zlib.crc32(str(tag).encode()) % NSUBQ}"
        return q

    def dma(self, q, out, in_, reads=(), writes=(), **kw):
        q = self._subq(q, writes)
        eng = Q_OWNER[q]
        need = self._need(eng, reads, writes)
        self.cnt[q] += 16
        qc = self.cnt[q]
        sems = self.sems

        def closure(e, need=need, out=out, in_=in_, kw=kw):
            for st, c in need.items():
                e.wait_ge(sems[st], c)
            e.dma_start(out=out, in_=in_, **kw).then_inc(sems[q], 16)
        self.pend[eng].append(closure)
        self._track(q, qc, reads, writes)

    def dma_t(self, q, out, in_, reads=(), writes=()):
        q = self._subq(q, writes)
        eng = Q_OWNER[q]
        need = self._need(eng, reads, writes)
        self.cnt[q] += 16
        qc = self.cnt[q]
        sems = self.sems

        def closure(e, need=need, out=out, in_=in_):
            for st, c in need.items():
                e.wait_ge(sems[st], c)
            e.dma_start_transpose(out, in_).then_inc(sems[q], 16)
        self.pend[eng].append(closure)
        self._track(q, qc, reads, writes)

    def wait_all(self, eng):
        sems = self.sems
        vals = [(st, self.cnt[st]) for st in ENGINES + QUEUES
                if st != eng and self.cnt[st] > 0]

        def closure(e, vals=vals):
            for st, c in vals:
                e.wait_ge(sems[st], c)
        self.pend[eng].append(closure)
        for st, c in vals:
            self.wm[(eng, st)] = max(self.wm.get((eng, st), 0), c)

    def barrier(self):
        for e in ENGINES:
            self.wait_all(e)
        self.last_w = {}
        self.readers = {}

    def flush(self, block):
        for e, lst in self.pend.items():
            if not lst:
                continue
            body = {"pe": block.tensor, "dve": block.vector, "act": block.scalar,
                    "pool": block.gpsimd, "sp": block.sync}[e]

            def run(engine, lst=tuple(lst)):
                for c in lst:
                    c(engine)
            body(run)
            self.pend[e] = []


def build_program(debug=False):
    nc = bass.Bass("TRN2", target_bir_lowering=False)
    SCR = "ExternalOutput" if debug else "Internal"
    u_t = nc.dram_tensor("u_t", [D_MODEL, T_ALL], F32, kind="ExternalInput").ap()
    u_own = nc.dram_tensor("u_own", [T_OWN, D_MODEL], F32, kind="ExternalInput").ap()
    w_in_x = nc.dram_tensor("w_in_x", [XB, 16, 128, 128], F32, kind="ExternalInput").ap()
    w_in_z = nc.dram_tensor("w_in_z", [D_MODEL, D_INNER], F32, kind="ExternalInput").ap()
    w_out = nc.dram_tensor("w_out", [D_INNER, D_MODEL], F32, kind="ExternalInput").ap()
    w1t = nc.dram_tensor("w1t", [D_MODEL, 8192], F32, kind="ExternalInput").ap()
    w2t = nc.dram_tensor("w2t", [8192, D_MODEL], F32, kind="ExternalInput").ap()
    conv_wt = nc.dram_tensor("conv_wt", [XB * 128, 4], F32, kind="ExternalInput").ap()
    conv_bt = nc.dram_tensor("conv_bt", [XB * 128], F32, kind="ExternalInput").ap()
    small_c = nc.dram_tensor("small_c", [64, 4], F32, kind="ExternalInput").ap()
    small_r = nc.dram_tensor("small_r", [4, 64], F32, kind="ExternalInput").ap()
    norm_w = nc.dram_tensor("norm_w", [D_INNER], F32, kind="ExternalInput").ap()
    seq_mask = nc.dram_tensor("seq_mask", [128, 1], F32, kind="ExternalInput").ap()
    out_t = nc.dram_tensor("out", [T_OWN, D_MODEL], F32, kind="ExternalOutput").ap()

    x_sp = nc.dram_tensor("x_sp", [D_INNER, T_CONV], BF16, kind=SCR).ap()
    zsil_sp = nc.dram_tensor("zsil_sp", [8, T_OWN, 512], BF16, kind=SCR).ap()
    y_sp = nc.dram_tensor("y_sp", [T_OWN, D_INNER], BF16, kind=SCR).ap()
    x1_sp = nc.dram_tensor("x1_sp", [T_OWN, D_MODEL], BF16, kind=SCR).ap()
    woutb_sp = nc.dram_tensor("woutb_sp", [D_INNER, D_MODEL], BF16, kind=SCR).ap()
    w1q_sp = nc.dram_tensor("w1q_sp", [16, 128, 8192], FP8, kind=SCR).ap()
    w2q_sp = nc.dram_tensor("w2q_sp", [4, 64, 128, 512], FP8, kind=SCR).ap()
    hqT_sp = nc.dram_tensor("hqT_sp", [8192, T_OWN], BF16, kind=SCR).ap()
    tn_sp = nc.dram_tensor("tn_sp", [T_OWN, D_INNER], BF16, kind="Internal").ap()
    dbg_a = nc.dram_tensor("dbg_a", [128, 4096], F32, kind=SCR).ap()
    dbg_b = nc.dram_tensor("dbg_b", [128, 8192], BF16, kind=SCR).ap()
    dbg_c = nc.dram_tensor("dbg_c", [128, 8192], BF16, kind=SCR).ap()
    dbg_d = nc.dram_tensor("dbg_d", [128, 4096], F32, kind=SCR).ap()

    ii = np.arange(128)
    c_ident = nc.inline_tensor(np.eye(128, dtype=bfd), "c_ident").ap()
    c_ident32 = nc.inline_tensor(np.eye(128, dtype=np.float32), "c_ident32").ap()
    c_ones = nc.inline_tensor(np.ones((128, 128), dtype=bfd), "c_ones").ap()
    c_ones32 = nc.inline_tensor(np.ones((1, 128), dtype=np.float32), "c_ones32").ap()
    c_tri = nc.inline_tensor((ii[:, None] <= ii[None, :]).astype(bfd), "c_tri").ap()
    c_ntril = nc.inline_tensor((-(ii[:, None] <= ii[None, :]).astype(np.float32)).astype(bfd), "c_ntril").ap()
    c_nbig = nc.inline_tensor((-BIG * (ii[:, None] < ii[None, :]).astype(np.float32)).astype(bfd), "c_nbig").ap()

    stack = ExitStack()
    em = Em(nc)

    with nc.Block() as block:
        em.init_sems(stack)
        ident = stack.enter_context(nc.sbuf_tensor("ident", [128, 128], BF16))
        ident32 = stack.enter_context(nc.sbuf_tensor("ident32", [128, 128], F32))
        onesb = stack.enter_context(nc.sbuf_tensor("onesb", [128, 128], BF16))
        ones32 = stack.enter_context(nc.sbuf_tensor("ones32", [1, 128], F32))
        trib = stack.enter_context(nc.sbuf_tensor("trib", [128, 128], BF16))
        ntrilb = stack.enter_context(nc.sbuf_tensor("ntrilb", [128, 128], BF16))
        nbigb = stack.enter_context(nc.sbuf_tensor("nbigb", [128, 128], BF16))
        small = stack.enter_context(nc.sbuf_tensor("small", [128, 24], F32))
        dbc = stack.enter_context(nc.sbuf_tensor("dbc", [128, 64], BF16))
        drow = stack.enter_context(nc.sbuf_tensor("drow", [1, 64], BF16))
        convw = stack.enter_context(nc.sbuf_tensor("convw", [128, XB * 4], F32))
        convb = stack.enter_context(nc.sbuf_tensor("convb", [128, XB], F32))

        em.dma("qsp", ident[:, :], c_ident[:, :], writes=("ident",))
        em.dma("qsp", ident32[:, :], c_ident32[:, :], writes=("ident32",))
        em.dma("qsp", onesb[:, :], c_ones[:, :], writes=("onesb",))
        em.dma("qsp", ones32[:, :], c_ones32[:, :], writes=("ones32",))
        em.dma("qsp", trib[:, :], c_tri[:, :], writes=("trib",))
        em.dma("qsp", ntrilb[:, :], c_ntril[:, :], writes=("ntrilb",))
        em.dma("qsp", nbigb[:, :], c_nbig[:, :], writes=("nbigb",))
        em.dma("qsp", small[0:64, 0:2], small_c[:, 0:2], writes=("small",))
        em.dma("qsp", small[:, 8:9], seq_mask[:, :], writes=("small",))
        em.dma("qpool", drow[0:1, :], small_r[2:3, :], writes=("drow",))
        em.dma("qpool", convw.ap().rearrange("p (c k) -> p c k", k=4),
               conv_wt.rearrange("(c p) k -> p c k", p=128), writes=("convw",))
        em.dma("qsp", convb[:, :],
               conv_bt.rearrange("(c p) -> p c", p=128), writes=("convb",),
               allow_slow_non_contiguous=True)
        for col, val in ((4, 1.0), (5, 0.5), (6, 127.0), (7, EPS), (9, -0.5),
                         (14, 0.0), (15, 0.0), (18, 0.0)):
            em.op("dve", lambda v, col=col, val=val: v.memset(small[:, col:col + 1], val),
                  writes=("small",))
        em.op("act", lambda s: s.activation(small[0:64, 3:4], small[0:64, 1:2], AF.Exp),
              reads=("small",), writes=("small",))
        em.op("dve", lambda v: v.tensor_scalar_mul(small[0:64, 3:4], small[0:64, 3:4], -1.0),
              reads=("small",), writes=("small",))
        with ExitStack() as sc0:
            p0 = sc0.enter_context(nc.psum_tensor("p0t", [128, 64], F32))
            em.op("pe", lambda t: t.matmul(p0[:, 0:64], onesb[0:1, :], drow[0:1, :],
                                           start=True, stop=True),
                  reads=("onesb", "drow"), writes=("p0t",))
            em.op("dve", lambda v: v.tensor_copy(dbc[:, :], p0[:, 0:64]),
                  reads=("p0t",), writes=("dbc",))

        em.barrier()

        scAP3 = ExitStack()
        dbcr = scAP3.enter_context(nc.sbuf_tensor("dbcr", [128, 4096], BF16))
        em.op("dve", lambda v: v.tensor_copy(
            dbcr.ap().rearrange("p (h q) -> p h q", h=64),
            dbc.ap().unsqueeze(2).broadcast_to([128, 64, 64])),
            reads=("dbc",), writes=("dbcr",))
        b_res = scAP3.enter_context(nc.sbuf_tensor("b_res", [128, T_CONV], BF16))
        c_res = scAP3.enter_context(nc.sbuf_tensor("c_res", [128, T_CONV], BF16))
        dt_res = scAP3.enter_context(nc.sbuf_tensor("dt_res", [64, T_CONV], F32))
        with ExitStack() as scA:
            utb = scA.enter_context(nc.sbuf_tensor("utb", [128, 16 * T_ALL], BF16))
            for part in range(4):
                em.dma("qpool",
                       utb.ap().rearrange("p (k t) -> p k t", k=16)[:, 4 * part:4 * (part + 1), :],
                       u_t.rearrange("(k p) t -> p k t", p=128)[:, 4 * part:4 * (part + 1), :],
                       writes=("utb",))
            wxb = [scA.enter_context(nc.sbuf_tensor(f"wxb{i}", [128, 2048], BF16)) for i in range(2)]
            slab = [scA.enter_context(nc.sbuf_tensor(f"slab{i}", [128, T_ALL], BF16)) for i in range(2)]
            acc = [scA.enter_context(nc.sbuf_tensor(f"acc{i}", [128, T_CONV], BF16)) for i in range(2)]
            xact = [scA.enter_context(nc.sbuf_tensor(f"xact{i}", [128, T_CONV], BF16)) for i in range(2)]
            stat = [scA.enter_context(nc.sbuf_tensor(f"stat{i}", [128, 1024], F32)) for i in range(4)]
            statq = scA.enter_context(nc.sbuf_tensor("statq", [128, 1024], BF16))
            statacc = scA.enter_context(nc.sbuf_tensor("statacc", [128, 256], F32))
            em.op("dve", lambda v: v.memset(statacc[:, :], 0.0), writes=("statacc",))
            rmsrow = scA.enter_context(nc.sbuf_tensor("rmsrow", [2, 4], F32))
            p2p = [scA.enter_context(nc.psum_tensor(f"p2p{i}", [128, 512], F32)) for i in range(2)]
            p_st = scA.enter_context(nc.psum_tensor("p_st", [128, 8], F32))

            stat_jobs = []
            for r in range(16):
                for qr in range(8):
                    stat_jobs.append((w1t[128 * r:128 * (r + 1), 1024 * qr:1024 * (qr + 1)],
                                      r * 8 + qr))
            for r in range(64):
                for hh in range(2):
                    stat_jobs.append((w2t[128 * r:128 * (r + 1), 1024 * hh:1024 * (hh + 1)],
                                      128 + r * 2 + hh))

            def stat_dma(idx):
                src_ap, col = stat_jobs[idx]
                em.dma("qsp", stat[idx % 4][:, :], src_ap, writes=(f"stat{idx % 4}",))

            def stat_sq(idx):
                _, col = stat_jobs[idx]
                sb, sk = stat[idx % 4], f"stat{idx % 4}"
                em.op("act", lambda s, sb=sb, col=col: s.activation(
                    statq[:, :], sb[:, :], AF.Square, accum_out=statacc[:, col:col + 1]),
                    reads=(sk,), writes=("statq", "statacc"))

            stat_d = 0
            stat_s = 0

            subs = [(i * 512, 512) for i in range(5)] + [(2560, 3)]
            for c in range(XB):
                want_s = c * 256 // XB
                while stat_s < min(want_s, len(stat_jobs)):
                    while stat_d < min(stat_s + 4, len(stat_jobs)):
                        stat_dma(stat_d)
                        stat_d += 1
                    stat_sq(stat_s)
                    stat_s += 1
                wb, wkey = wxb[c % 2], f"wxb{c % 2}"
                em.dma("qpool", wb.ap().rearrange("p (k n) -> p k n", k=16),
                       w_in_x[c, :, :, :].rearrange("k p n -> p k n"), writes=(wkey,))
                sl, skey = slab[c % 2], f"slab{c % 2}"
                for si, (o, w) in enumerate(subs):
                    pb, pkey = p2p[si % 2], f"p2p{si % 2}"

                    def mmg(t, wb=wb, o=o, w=w, pb=pb):
                        for k in range(16):
                            ins = t.matmul(pb[:, 0:w], wb[:, 128 * k:128 * (k + 1)],
                                           utb[:, k * T_ALL + o:k * T_ALL + o + w],
                                           start=(k == 0), stop=(k == 15))
                        return ins
                    em.op("pe", mmg, reads=(wkey, "utb", pkey), writes=(pkey,))
                    if c == 34:
                        lo = max(o, HALO)
                        if lo < o + w:
                            em.op("dve", lambda v, pb=pb, o=o, w=w, lo=lo: v.tensor_copy(
                                dt_res[0:64, lo - HALO:o + w - HALO], pb[0:64, lo - o:w]),
                                reads=(pkey,), writes=("dt_res",))
                    else:
                        em.op("dve", lambda v, pb=pb, sl=sl, o=o, w=w: v.tensor_copy(
                            sl[:, o:o + w], pb[:, 0:w]),
                            reads=(pkey,), writes=(skey,))
                if c == 34:
                    continue
                ac, ackey = acc[c % 2], f"acc{c % 2}"
                xa, xakey = xact[c % 2], f"xact{c % 2}"
                em.op("dve", lambda v, sl=sl, ac=ac, c=c: v.tensor_scalar_mul(
                    ac[:, :], sl[:, 0:T_CONV], convw[:, 4 * c:4 * c + 1]),
                    reads=(skey, "convw"), writes=(ackey,))
                for k in range(1, 4):
                    em.op("dve", lambda v, sl=sl, ac=ac, c=c, k=k: v.scalar_tensor_tensor(
                        ac[:, :], sl[:, k:k + T_CONV], convw[:, 4 * c + k:4 * c + k + 1],
                        ac[:, :], op0=OP.mult, op1=OP.add),
                        reads=(skey, "convw", ackey), writes=(ackey,))
                em.op("act", lambda s, ac=ac, xa=xa, c=c: s.activation(
                    xa[:, :], ac[:, :], AF.Silu, bias=convb[:, c:c + 1], scale=1.0),
                    reads=(ackey, "convb"), writes=(xakey,))
                if c < 32:
                    em.dma("qsp", x_sp[128 * c:128 * (c + 1), :], xa[:, :],
                           reads=(xakey,), writes=("x_sp",))
                elif c == 32:
                    em.op("dve", lambda v, xa=xa: v.tensor_copy(b_res[:, :], xa[:, :]),
                          reads=(xakey,), writes=("b_res",))
                else:
                    em.op("dve", lambda v, xa=xa: v.tensor_copy(c_res[:, :], xa[:, :]),
                          reads=(xakey,), writes=("c_res",))

            while stat_s < len(stat_jobs):
                while stat_d < min(stat_s + 4, len(stat_jobs)):
                    stat_dma(stat_d)
                    stat_d += 1
                stat_sq(stat_s)
                stat_s += 1
            em.op("dve", lambda v: v.reduce_sum(small[:, 14:15], statacc[:, 0:128], axis=AX.X),
                  reads=("statacc",), writes=("small",))
            em.op("dve", lambda v: v.reduce_sum(small[:, 15:16], statacc[:, 128:256], axis=AX.X),
                  reads=("statacc",), writes=("small",))
            em.op("pe", lambda t: t.matmul(p_st[0:1, 0:1], small[:, 14:15],
                                           small[:, 4:5], start=True, stop=True),
                  reads=("small", "p_st"), writes=("p_st",))
            em.op("pe", lambda t: t.matmul(p_st[0:1, 1:2], small[:, 15:16],
                                           small[:, 4:5], start=False, stop=True),
                  reads=("small", "p_st"), writes=("p_st",))
            NW = float(2048 * 8192)
            em.op("dve", lambda v: v.tensor_scalar(rmsrow[0:1, 0:2], p_st[0:1, 0:2],
                                                   1.0 / NW, EPS, op0=OP.mult, op1=OP.max),
                  reads=("p_st",), writes=("rmsrow",))
            em.op("act", lambda s: s.activation(rmsrow[0:1, 0:2], rmsrow[0:1, 0:2], AF.Ln),
                  reads=("rmsrow",), writes=("rmsrow",))
            em.op("dve", lambda v: v.tensor_scalar_mul(rmsrow[0:1, 0:2], rmsrow[0:1, 0:2], -0.5),
                  reads=("rmsrow",), writes=("rmsrow",))
            em.op("act", lambda s: s.activation(rmsrow[0:1, 0:2], rmsrow[0:1, 0:2], AF.Exp),
                  reads=("rmsrow",), writes=("rmsrow",))
            em.op("dve", lambda v: v.tensor_scalar_mul(rmsrow[0:1, 2:4], rmsrow[0:1, 0:2], 254.0),
                  reads=("rmsrow",), writes=("rmsrow",))
            em.op("dve", lambda v: v.reciprocal(rmsrow[0:1, 2:4], rmsrow[0:1, 2:4]),
                  reads=("rmsrow",), writes=("rmsrow",))
            em.op("pe", lambda t: t.matmul(p_st[:, 4:8], ones32[0:1, :],
                                           rmsrow[0:1, 0:4], start=True, stop=True),
                  reads=("rmsrow", "ones32", "p_st"), writes=("p_st2",))
            em.op("dve", lambda v: v.tensor_copy(small[:, 10:12], p_st[:, 4:6]),
                  reads=("p_st2",), writes=("small",))
            em.op("dve", lambda v: v.tensor_copy(small[:, 12:14], p_st[:, 6:8]),
                  reads=("p_st2",), writes=("small",))

            wzb = [scA.enter_context(nc.sbuf_tensor(f"wzb{i}", [128, 16 * 512], BF16)) for i in range(2)]
            zst = [scA.enter_context(nc.sbuf_tensor(f"zst{i}", [128, 512], BF16)) for i in range(2)]
            for zc in range(8):
                wz, wzk = wzb[zc % 2], f"wzb{zc % 2}"
                for part in range(4):
                    em.dma("qpool",
                           wz.ap().rearrange("p (k n) -> p k n", k=16)[:, 4 * part:4 * (part + 1), :],
                           w_in_z.rearrange("(k p) n -> p k n", p=128)
                           [:, 4 * part:4 * (part + 1), 512 * zc:512 * (zc + 1)],
                           writes=(wzk,))
                for m in range(16):
                    pb, pkey = p2p[m % 2], f"p2p{m % 2}"

                    def mmz(t, wz=wz, m=m, pb=pb):
                        for k in range(16):
                            o = k * T_ALL + HALO + T_FOR + 128 * m
                            ins = t.matmul(pb[:, :], utb[:, o:o + 128],
                                           wz[:, 512 * k:512 * (k + 1)],
                                           start=(k == 0), stop=(k == 15))
                        return ins
                    em.op("pe", mmz, reads=(wzk, "utb", pkey), writes=(pkey,))
                    zb, zkey = zst[m % 2], f"zst{m % 2}"
                    em.op("act", lambda s, pb=pb, zb=zb: s.activation(zb[:, :], pb[:, :], AF.Silu),
                          reads=(pkey,), writes=(zkey,))
                    em.dma("qsp", zsil_sp[zc, 128 * m:128 * (m + 1), :], zb[:, :],
                           reads=(zkey,), writes=("zsil_sp",))

        em.barrier()

        with ExitStack() as sc3:
            s_st = sc3.enter_context(nc.sbuf_tensor("s_st", [128, 4096], F32))
            em.op("dve", lambda v: v.memset(s_st[:, :], 0.0), writes=("s_st",))
            xtok = [sc3.enter_context(nc.sbuf_tensor(f"xtok{i}", [128, 4096], BF16)) for i in range(2)]
            xch = [sc3.enter_context(nc.sbuf_tensor("xch0", [128, 4096], BF16))]
            ybuf = [sc3.enter_context(nc.sbuf_tensor("ybuf0", [128, 4096], BF16))]
            sbf = sc3.enter_context(nc.sbuf_tensor("sbf", [128, 4096], BF16))
            xd = sc3.enter_context(nc.sbuf_tensor("xd", [128, 4096], BF16))
            xw = sc3.enter_context(nc.sbuf_tensor("xw", [128, 4096], BF16))
            mpp = sc3.enter_context(nc.sbuf_tensor("mpp", [128, 8192], BF16))
            dtap = sc3.enter_context(nc.sbuf_tensor("dtap", [128, 8192], BF16))
            decb = [sc3.enter_context(nc.sbuf_tensor(f"decb{i}", [128, 512], BF16)) for i in range(2)]
            cbsb = sc3.enter_context(nc.sbuf_tensor("cbsb", [128, 128], BF16))
            btok = sc3.enter_context(nc.sbuf_tensor("btok", [128, 128], BF16))
            dch = sc3.enter_context(nc.sbuf_tensor("dch", [64, 128 * 8], F32))
            dchb = sc3.enter_context(nc.sbuf_tensor("dchb", [64, 128], BF16))
            dtok = sc3.enter_context(nc.sbuf_tensor("dtok", [128, 64 * 4], F32))
            dtokb = sc3.enter_context(nc.sbuf_tensor("dtokb", [128, 128], BF16))
            arow = sc3.enter_context(nc.sbuf_tensor("arow", [1, 64], F32))
            evtmp = sc3.enter_context(nc.sbuf_tensor("evtmp", [128, 512], F32))
            wrep = sc3.enter_context(nc.sbuf_tensor("wrep", [128, 4096], BF16))
            arep = sc3.enter_context(nc.sbuf_tensor("arep", [128, 4096], BF16))
            erep = sc3.enter_context(nc.sbuf_tensor("erep", [128, 4096], BF16))
            sgbuf = [sc3.enter_context(nc.sbuf_tensor(f"sgbuf{i}", [128, 2048], F32)) for i in range(2)]
            sgout = sc3.enter_context(nc.sbuf_tensor("sgout", [128, 2048], BF16))
            sg2 = sc3.enter_context(nc.sbuf_tensor("sg2", [128, 2048], BF16))
            sgq = sc3.enter_context(nc.sbuf_tensor("sgq", [128, 2048], FP8))

            p_dec = [sc3.enter_context(nc.psum_tensor(f"p_dec{i}", [128, 512], F32)) for i in range(2)]
            p_y1 = [sc3.enter_context(nc.psum_tensor(f"p_y1{i}", [128, 512], F32)) for i in range(2)]
            p_y2 = [sc3.enter_context(nc.psum_tensor(f"p_y2{i}", [128, 512], F32)) for i in range(2)]
            p_s3 = sc3.enter_context(nc.psum_tensor("p_s3", [128, 512], F32))
            p_aux = sc3.enter_context(nc.psum_tensor("p_aux", [128, 128], F32))
            p_auxb = p_aux.ap().bitcast(BF16)

            em.op("dve", lambda v: v.memset(dch[:, 128 * 7:128 * 8], 0.0), writes=("dchz",))
            DTF, DTA, LND, DLS, WDT, CUM, SCR, DZ = [
                dch[0:64, 128 * i:128 * (i + 1)] for i in range(8)]

            sign_jobs = []
            for fc in range(16):
                for qf in range(4):
                    k0 = 4 * qf
                    src = w1t.rearrange("(k p) f -> p k f", p=128)[:, k0:k0 + 4,
                                                                  512 * fc:512 * (fc + 1)]
                    sign_jobs.append((src, w1q_sp[fc, :, 2048 * qf:2048 * (qf + 1)], 10))
            for d2c in range(4):
                for ks in range(4):
                    for qf in range(4):
                        k0 = 16 * ks + 4 * qf
                        src = w2t.rearrange("(k p) f -> p k f", p=128)[:, k0:k0 + 4,
                                                                      512 * d2c:512 * (d2c + 1)]
                        sign_jobs.append((src, w2q_sp[d2c, 16 * ks + 4 * qf:16 * ks + 4 * (qf + 1), :, :]
                                          .rearrange("k p f -> p k f"), 11))

            def sign_dma(idx):
                src, _, _ = sign_jobs[idx]
                em.dma("qpool", sgbuf[idx % 2].ap().rearrange("p (k f) -> p k f", k=4),
                       src, writes=(f"sgbuf{idx % 2}",))

            def sign_emit(idx):
                _, dst_ap, rms_col = sign_jobs[idx]
                sb, sk = sgbuf[idx % 2], f"sgbuf{idx % 2}"
                em.op("act", lambda s, sb=sb, rms_col=rms_col: s.activation(
                    sgout[:, :], sb[:, :], AF.Sign, bias=small[:, 9:10],
                    scale=small[:, rms_col:rms_col + 1]),
                    reads=(sk, "small"), writes=("sgout",))
                em.op("act", lambda s, sb=sb, rms_col=rms_col: s.activation(
                    sg2[:, :], sb[:, :], AF.Sign, bias=small[:, 5:6],
                    scale=small[:, rms_col:rms_col + 1]),
                    reads=(sk, "small"), writes=("sg2",))
                em.op("dve", lambda v: v.tensor_tensor(sgq[:, :], sgout[:, :], sg2[:, :],
                                                       op=OP.add),
                      reads=("sgout", "sg2"), writes=("sgq",))
                em.dma("qpool", dst_ap,
                       sgq.ap().rearrange("p (k f) -> p k f", k=4) if dst_ap.ndim == 3
                       else sgq[:, :],
                       reads=("sgq",), writes=("wq_sp",))

            sign_d = 0
            sign_i = 0

            for c in range(NCH):
                own = c >= KFOR
                xt, xk = xtok[c % 2], f"xtok{c % 2}"
                xc, xck = xch[0], "xch0"
                em.dma("qsp", xc.ap().rearrange("p (b t) -> p b t", b=32),
                       x_sp.rearrange("(b p) t -> p b t", p=128)[:, :, 128 * c:128 * (c + 1)],
                       reads=("x_sp",), writes=(xck,))
                for g in range(8):
                    pd, pdk = p_dec[g % 2], f"p_dec{g % 2}"

                    def xtg(t, g=g, xc=xc, pd=pd):
                        pb = pd.ap().bitcast(BF16)
                        for qq in range(4):
                            j = 4 * g + qq
                            ins = t.transpose(pb[:, 128 * qq:128 * (qq + 1)],
                                              xc[:, 128 * j:128 * (j + 1)], ident[:, :])
                        return ins
                    em.op("pe", xtg, reads=(xck, "ident", pdk), writes=(pdk,))
                    em.op("act", lambda s, g=g, pd=pd: s.copy(
                        xt[:, 512 * g:512 * (g + 1)], pd.ap().bitcast(BF16)[:, 0:512]),
                        reads=(pdk,), writes=(xk,))
                em.op("pe", lambda t, c=c: t.transpose(p_auxb[:, 0:128],
                                                       b_res[:, 128 * c:128 * (c + 1)], ident[:, :]),
                      reads=("b_res", "ident"), writes=("p_aux",))
                em.op("dve", lambda v: v.tensor_copy(btok[:, :], p_auxb[:, 0:128]),
                      reads=("p_aux",), writes=("btok",))
                em.op("dve", lambda v, c=c: v.tensor_scalar_add(
                    SCR, dt_res[0:64, 128 * c:128 * (c + 1)], small[0:64, 0:1]),
                    reads=("dt_res", "small"), writes=("dscr",))
                em.op("act", lambda s: s.activation(DTF, SCR, AF.Abs), reads=("dscr",), writes=("dtf",))
                em.op("act", lambda s: s.activation(DTF, DTF, AF.Exp, scale=-1.0),
                      reads=("dtf",), writes=("dtf",))
                em.op("act", lambda s: s.activation(DTF, DTF, AF.Ln, bias=small[0:64, 4:5], scale=1.0),
                      reads=("dtf", "small"), writes=("dtf",))
                em.op("dve", lambda v: v.tensor_scalar_max(SCR, SCR, 0.0), reads=("dscr",), writes=("dscr",))
                em.op("dve", lambda v: v.tensor_tensor(DTF, DTF, SCR, op=OP.add),
                      reads=("dtf", "dscr"), writes=("dtf",))
                em.op("dve", lambda v: v.tensor_scalar_mul(DTA, DTF, small[0:64, 3:4]),
                      reads=("dtf", "small"), writes=("dta",))
                em.op("dve", lambda v: v.tensor_tensor_scan(CUM, DTA, DZ, 0.0, op0=OP.add, op1=OP.add),
                      reads=("dta", "dchz"), writes=("cum",))
                em.op("dve", lambda v: v.tensor_scalar_max(SCR, DTF, 1e-38),
                      reads=("dtf",), writes=("dscr",))
                em.op("act", lambda s: s.activation(LND, SCR, AF.Ln), reads=("dscr",), writes=("lnd",))
                em.op("dve", lambda v: v.tensor_scalar_max(LND, LND, -50.0), reads=("lnd",), writes=("lnd",))
                em.op("dve", lambda v: v.tensor_scalar(DLS, CUM, dch[0:64, 128 * 5 + 127:128 * 5 + 128],
                                                       None, op0=OP.subtract),
                      reads=("cum",), writes=("dls",))
                em.op("act", lambda s: s.activation(DLS, DLS, AF.Exp, scale=-1.0),
                      reads=("dls",), writes=("dls",))
                em.op("dve", lambda v: v.tensor_tensor(WDT, DLS, DTF, op=OP.mult),
                      reads=("dls", "dtf"), writes=("wdt",))
                em.op("act", lambda s: s.activation(SCR[:, 0:1], dch[0:64, 128 * 5 + 127:128 * 5 + 128],
                                                    AF.Exp),
                      reads=("cum",), writes=("dscr",))
                em.op("pe", lambda t: t.transpose(p_aux[0:1, 0:64], SCR[:, 0:1], ident32[0:64, 0:64]),
                      reads=("dscr", "ident32"), writes=("p_aux",))
                em.op("act", lambda s: s.copy(arow[0:1, :], p_aux[0:1, 0:64]),
                      reads=("p_aux",), writes=("arow",))
                em.op("pe", lambda t: t.matmul(p_aux[:, 0:64], ones32[0:1, :], arow[0:1, :],
                                               start=True, stop=True),
                      reads=("ones32", "arow", "p_aux"), writes=("p_aux",))
                em.op("dve", lambda v: v.tensor_copy(dtok[:, 128:192], p_aux[:, 0:64]),
                      reads=("p_aux",), writes=("dtok2",))
                em.op("pe", lambda t: t.transpose(p_aux[:, 0:64], CUM, ident32[0:64, 0:64]),
                      reads=("cum", "ident32"), writes=("p_aux",))
                em.op("dve", lambda v: v.tensor_copy(dtok[:, 0:64], p_aux[:, 0:64]),
                      reads=("p_aux",), writes=("dtok0",))
                em.op("pe", lambda t: t.transpose(p_aux[:, 0:64], WDT, ident32[0:64, 0:64]),
                      reads=("wdt", "ident32"), writes=("p_aux",))
                em.op("dve", lambda v: v.tensor_copy(dtok[:, 192:256], p_aux[:, 0:64]),
                      reads=("p_aux",), writes=("dtok3",))
                em.op("act", lambda s: s.activation(dtok[:, 64:128], dtok[:, 0:64], AF.Exp),
                      reads=("dtok0",), writes=("dtok1",))
                em.op("dve", lambda v: v.tensor_copy(dchb[0:64, :], DTA), reads=("dta",), writes=("dchb",))
                em.op("pe", lambda t: t.transpose(p_auxb[:, 0:64], dchb[0:64, :], ident[0:64, 0:64]),
                      reads=("dchb", "ident"), writes=("p_aux",))
                em.op("dve", lambda v: v.tensor_copy(dtokb[:, 0:64], p_auxb[:, 0:64]),
                      reads=("p_aux",), writes=("dtokb0",))
                em.op("dve", lambda v: v.tensor_copy(dchb[0:64, :], LND), reads=("lnd",), writes=("dchb",))
                em.op("pe", lambda t: t.transpose(p_auxb[:, 0:64], dchb[0:64, :], ident[0:64, 0:64]),
                      reads=("dchb", "ident"), writes=("p_aux",))
                em.op("dve", lambda v: v.tensor_copy(dtokb[:, 64:128], p_auxb[:, 0:64]),
                      reads=("p_aux",), writes=("dtokb1",))
                em.op("dve", lambda v: v.tensor_copy(
                    wrep.ap().rearrange("p (h q) -> p h q", h=64),
                    dtok[:, 192:256].unsqueeze(2).broadcast_to([128, 64, 64])),
                    reads=("dtok3",), writes=("wrep",))
                em.op("dve", lambda v: v.tensor_copy(
                    arep.ap().rearrange("p (h q) -> p h q", h=64),
                    dtok[:, 128:192].unsqueeze(2).broadcast_to([128, 64, 64])),
                    reads=("dtok2",), writes=("arep",))
                em.op("dve", lambda v: v.tensor_tensor(xw[:, :], xt[:, :], wrep[:, :],
                                                       op=OP.mult),
                    reads=(xk, "wrep"), writes=("xw",))
                if debug and c == KFOR:
                    em.dma("qpool", dbg_d[:, 0:2048], wrep[:, 0:2048], reads=("wrep",), writes=("dbg_d",))
                    em.dma("qpool", dbg_d[:, 2048:4096], xw[:, 0:2048], reads=("xw",), writes=("dbg_d",))
                if own:
                    em.op("dve", lambda v: v.tensor_copy(
                        erep.ap().rearrange("p (h q) -> p h q", h=64),
                        dtok[:, 64:128].unsqueeze(2).broadcast_to([128, 64, 64])),
                        reads=("dtok1",), writes=("erep",))
                    em.op("dve", lambda v: v.tensor_copy(sbf[:, :], s_st[:, :]),
                          reads=("s_st",), writes=("sbf",))
                    em.op("pe", lambda t, c=c: t.matmul(p_aux[:, :], b_res[:, 128 * c:128 * (c + 1)],
                                                        c_res[:, 128 * c:128 * (c + 1)],
                                                        start=True, stop=True),
                          reads=("b_res", "c_res", "p_aux"), writes=("p_aux",))
                    em.op("act", lambda s: s.copy(cbsb[:, :], p_aux[:, :]),
                          reads=("p_aux",), writes=("cbsb",))
                    em.op("dve", lambda v: v.tensor_tensor(xd[:, :], xt[:, :], dbcr[:, :],
                                                           op=OP.mult),
                        reads=(xk, "dbcr"), writes=("xd",))
                    em.op("dve", lambda v: v.tensor_tensor(
                        dtap.ap().rearrange("p (h i) -> p h i", h=64),
                        dtokb[:, 0:64].unsqueeze(2).broadcast_to([128, 64, 128]),
                        trib.ap().unsqueeze(1).broadcast_to([128, 64, 128]), op=OP.mult),
                        reads=("dtokb0", "trib"), writes=("dtap",))
                    for g4 in range(16):
                        pd, pdk = p_dec[g4 % 2], f"p_dec{g4 % 2}"

                        def decmm(t, g4=g4, pd=pd):
                            t.matmul(pd[:, :], onesb[:, :], dtap[:, 512 * g4:512 * (g4 + 1)],
                                     start=True, stop=False)
                            t.matmul(pd[:, :], ntrilb[:, :],
                                     dtokb[:, 4 * g4:4 * g4 + 4].unsqueeze(2)
                                     .broadcast_to([128, 4, 128]), start=False, stop=False)
                            t.matmul(pd[:, :], ident[:, :],
                                     dtokb[:, 64 + 4 * g4:64 + 4 * g4 + 4].unsqueeze(2)
                                     .broadcast_to([128, 4, 128]), start=False, stop=False)
                            return t.matmul(pd[:, :], nbigb[:, :],
                                            ident.ap().unsqueeze(1).broadcast_to([128, 4, 128]),
                                            start=False, stop=True)
                        em.op("pe", decmm, reads=("onesb", "dtap", "ntrilb", "dtokb0",
                                                  "dtokb1", "ident", "nbigb", pdk), writes=(pdk,))
                        db, dbk = decb[g4 % 2], f"decb{g4 % 2}"
                        em.op("act", lambda s, pd=pd, db=db: s.activation(db[:, :], pd[:, :], AF.Exp),
                              reads=(pdk,), writes=(dbk,))
                        em.op("dve", lambda v, db=db, g4=g4: v.tensor_tensor(
                            mpp[:, 512 * g4:512 * (g4 + 1)].rearrange("p (h i) -> p h i", h=4),
                            db.ap().rearrange("p (h i) -> p h i", h=4),
                            cbsb.ap().unsqueeze(1).broadcast_to([128, 4, 128]), op=OP.mult),
                            reads=(dbk, "cbsb"), writes=("mpp",))
                for g8 in range(8):
                    if own:
                        py1, k1 = p_y1[g8 % 2], f"p_y1{g8 % 2}"
                        py2, k2 = p_y2[g8 % 2], f"p_y2{g8 % 2}"

                        def ymm(t, g8=g8, py1=py1):
                            t.matmul(py1[:, :], ident[:, :], xd[:, 512 * g8:512 * (g8 + 1)],
                                     start=True, stop=False)
                            for hl in range(8):
                                h = 8 * g8 + hl
                                ins = t.matmul(py1[:, 64 * hl:64 * (hl + 1)],
                                               mpp[:, 128 * h:128 * (h + 1)],
                                               xt[:, 64 * h:64 * (h + 1)],
                                               start=False, stop=(hl == 7))
                            return ins
                        em.op("pe", ymm, reads=("ident", "xd", "mpp", xk, k1), writes=(k1,))
                        em.op("pe", lambda t, py2=py2, g8=g8, c=c: t.matmul(
                            py2[:, :], c_res[:, 128 * c:128 * (c + 1)],
                            sbf[:, 512 * g8:512 * (g8 + 1)], start=True, stop=True),
                            reads=("c_res", "sbf", k2), writes=(k2,))
                        yb, ybk = ybuf[0], "ybuf0"
                        em.op("dve", lambda v, py2=py2, g8=g8: v.tensor_tensor(
                            evtmp[:, :], py2[:, :], erep[:, 512 * g8:512 * (g8 + 1)],
                            op=OP.mult),
                            reads=(k2, "erep"), writes=("evtmp",))
                        em.op("dve", lambda v, py1=py1, yb=yb, g8=g8: v.tensor_tensor(
                            yb[:, 512 * g8:512 * (g8 + 1)], evtmp[:, :], py1[:, :], op=OP.add),
                            reads=("evtmp", k1), writes=(ybk,))
                    em.op("pe", lambda t, g8=g8: t.matmul(
                        p_s3[:, :], btok[:, :], xw[:, 512 * g8:512 * (g8 + 1)],
                        start=True, stop=True),
                        reads=("btok", "xw", "p_s3"), writes=("p_s3",))
                    em.op("dve", lambda v, g8=g8: v.tensor_tensor(
                        s_st[:, 512 * g8:512 * (g8 + 1)], s_st[:, 512 * g8:512 * (g8 + 1)],
                        arep[:, 512 * g8:512 * (g8 + 1)], op=OP.mult),
                        reads=("arep", "s_st"), writes=("s_st",))
                    em.op("dve", lambda v, g8=g8: v.tensor_tensor(
                        s_st[:, 512 * g8:512 * (g8 + 1)], s_st[:, 512 * g8:512 * (g8 + 1)],
                        p_s3[:, :], op=OP.add),
                        reads=("p_s3", "s_st"), writes=("s_st",))
                if own:
                    em.dma("qsp", y_sp[128 * (c - KFOR):128 * (c - KFOR + 1), :],
                           ybuf[0][:, :], reads=("ybuf0",), writes=("y_sp",))
                if debug and c == KFOR:
                    em.op("dve", lambda v: v.tensor_copy(evtmp[:, 0:64], dbc[:, :]),
                          reads=("dbc",), writes=("evtmp",))
                    em.op("dve", lambda v: v.tensor_copy(
                        evtmp.ap()[:, 64:256].rearrange("p (h q) -> p h q", q=3),
                        dtok[:, 192:256].unsqueeze(2).broadcast_to([128, 64, 3])),
                        reads=("dtok3",), writes=("evtmp",))
                    em.dma("qsp", dbg_a[:, 256:512], evtmp[:, 0:256], reads=("evtmp",), writes=("dbg_a",))
                    em.dma("qsp", dbg_a[:, 0:256], dtok[:, :],
                           reads=("dtok0", "dtok1", "dtok2", "dtok3"), writes=("dbg_a",))
                    em.dma("qsp", dbg_a[0:64, 1024:2048], dch[0:64, :],
                           reads=("dtf", "dta", "lnd", "dls", "wdt", "cum"), writes=("dbg_a",))
                    em.dma("qsp", dbg_b[:, 0:4096], xtok[c % 2][:, :], reads=(f"xtok{c % 2}",), writes=("dbg_b",))
                    em.dma("qsp", dbg_b[:, 4096:8192], xd[:, :], reads=("xd",), writes=("dbg_b",))
                    em.dma("qsp", dbg_c[:, 0:4096], xw[:, :], reads=("xw",), writes=("dbg_c",))
                    em.dma("qsp", dbg_c[:, 4096:4224], cbsb[:, :], reads=("cbsb",), writes=("dbg_c",))
                    em.dma("qsp", dbg_c[:, 4224:4352], btok[:, :], reads=("btok",), writes=("dbg_c",))
                    em.dma("qsp", dbg_c[:, 4352:4480], dtokb[:, :], reads=("dtokb0", "dtokb1"), writes=("dbg_c",))

                if c == KFOR - 1:
                    em.op("dve", lambda v: v.tensor_scalar_mul(s_st[:, :], s_st[:, :],
                                                               small[:, 8:9]),
                          reads=("small", "s_st"), writes=("s_st",))
                want_s = (c + 1) * len(sign_jobs) // NCH
                while sign_i < want_s:
                    while sign_d < min(sign_i + 2, len(sign_jobs)):
                        sign_dma(sign_d)
                        sign_d += 1
                    sign_emit(sign_i)
                    sign_i += 1

            for part in range(8):
                em.dma("qpool", woutb_sp[512 * part:512 * (part + 1), :],
                       w_out[512 * part:512 * (part + 1), :], writes=("woutb_sp",))
            while sign_i < len(sign_jobs):
                while sign_d < min(sign_i + 2, len(sign_jobs)):
                    sign_dma(sign_d)
                    sign_d += 1
                sign_emit(sign_i)
                sign_i += 1

        scAP3.close()
        em.barrier()

        with ExitStack() as sc4a:
            nwb = sc4a.enter_context(nc.sbuf_tensor("nwb", [128, 4096], BF16))
            nwrow = sc4a.enter_context(nc.sbuf_tensor("nwrow", [1, 512], F32))
            p_nw = sc4a.enter_context(nc.psum_tensor("p_nw", [128, 512], F32))
            for j in range(8):
                em.dma("qsp", nwrow[0:1, :], norm_w[512 * j:512 * (j + 1)].unsqueeze(0),
                       writes=("nwrow",))
                em.op("pe", lambda t, j=j: t.matmul(p_nw[:, :], ones32[0:1, :],
                                                    nwrow[0:1, :], start=True, stop=True),
                      reads=("ones32", "nwrow", "p_nw"), writes=("p_nw",))
                em.op("dve", lambda v, j=j: v.tensor_copy(nwb[:, 512 * j:512 * (j + 1)],
                                                          p_nw[:, :]),
                      reads=("p_nw",), writes=("nwb",))
            yb4 = [sc4a.enter_context(nc.sbuf_tensor(f"y4_{i}", [128, 4096], BF16)) for i in range(2)]
            zb4 = [sc4a.enter_context(nc.sbuf_tensor(f"z4_{i}", [128, 4096], BF16)) for i in range(2)]
            tb4 = [sc4a.enter_context(nc.sbuf_tensor(f"tb_{i}", [128, 4096], BF16)) for i in range(2)]
            tsq4 = [sc4a.enter_context(nc.sbuf_tensor(f"tsq_{i}", [128, 4096], BF16)) for i in range(2)]
            tn4 = [sc4a.enter_context(nc.sbuf_tensor(f"tn_{i}", [128, 4096], BF16)) for i in range(2)]
            cla = sc4a.enter_context(nc.sbuf_tensor("cla", [128, 8], F32))
            for m in range(16):
                i2 = m % 2
                yb, zb, tb, tsq, tn = yb4[i2], zb4[i2], tb4[i2], tsq4[i2], tn4[i2]
                yk, zk, tbk, tqk, tnk = (f"y4_{i2}", f"z4_{i2}", f"tb_{i2}", f"tsq_{i2}", f"tn_{i2}")
                c0 = 4 * i2
                ck = f"cla{i2}"
                em.dma("qsp", yb[:, :], y_sp[128 * m:128 * (m + 1), :],
                       reads=("y_sp",), writes=(yk,))
                for zc in range(8):
                    em.dma("qsp", zb[:, 512 * zc:512 * (zc + 1)],
                           zsil_sp[zc, 128 * m:128 * (m + 1), :],
                           reads=("zsil_sp",), writes=(zk,))
                em.op("dve", lambda v, yb=yb, zb=zb, tb=tb: v.tensor_tensor(
                    tb[:, :], yb[:, :], zb[:, :], op=OP.mult),
                    reads=(yk, zk), writes=(tbk,))
                em.op("act", lambda s, tb=tb, tsq=tsq, c0=c0: s.activation(
                    tsq[:, :], tb[:, :], AF.Square, accum_out=cla[:, c0:c0 + 1]),
                    reads=(tbk,), writes=(tqk, ck))
                em.op("act", lambda s, c0=c0: s.activation(cla[:, c0 + 1:c0 + 2],
                                                           cla[:, c0:c0 + 1], AF.Ln,
                                                           bias=small[:, 7:8],
                                                           scale=1.0 / D_INNER),
                      reads=(ck, "small"), writes=(ck,))
                em.op("dve", lambda v, c0=c0: v.tensor_scalar_mul(cla[:, c0 + 1:c0 + 2],
                                                                  cla[:, c0 + 1:c0 + 2], -0.5),
                      reads=(ck,), writes=(ck,))
                em.op("act", lambda s, c0=c0: s.activation(cla[:, c0 + 2:c0 + 3],
                                                           cla[:, c0 + 1:c0 + 2], AF.Exp),
                      reads=(ck,), writes=(ck,))
                em.op("dve", lambda v, tb=tb, tn=tn, c0=c0: v.scalar_tensor_tensor(
                    tn[:, :], tb[:, :], cla[:, c0 + 2:c0 + 3], nwb[:, :],
                    op0=OP.mult, op1=OP.mult),
                    reads=(tbk, ck, "nwb"), writes=(tnk,))
                em.dma("qsp", tn_sp[128 * m:128 * (m + 1), :], tn[:, :],
                       reads=(tnk,), writes=("tn_sp",))

        em.barrier()

        with ExitStack() as sc4:
            woutb = sc4.enter_context(nc.sbuf_tensor("woutb", [128, 32 * 2048], BF16))
            for part in range(8):
                em.dma("qsp",
                       woutb.ap().rearrange("p (k n) -> p k n", k=32)[:, 4 * part:4 * (part + 1), :],
                       woutb_sp.rearrange("(k p) n -> p k n", p=128)[:, 4 * part:4 * (part + 1), :],
                       reads=("woutb_sp",), writes=("woutb",))
            tnl = [sc4.enter_context(nc.sbuf_tensor(f"tnl{i}", [128, 4096], BF16)) for i in range(2)]
            ub4 = [sc4.enter_context(nc.sbuf_tensor(f"u4_{i}", [128, 2048], BF16)) for i in range(2)]
            tT4 = [sc4.enter_context(nc.sbuf_tensor(f"tT{i}", [128, 4096], BF16)) for i in range(2)]
            x1p4 = [sc4.enter_context(nc.sbuf_tensor(f"x1p{i}", [128, 2048], F32)) for i in range(2)]
            sqb = [sc4.enter_context(nc.sbuf_tensor(f"sqb{i}", [128, 2048], BF16)) for i in range(2)]
            x1b4 = [sc4.enter_context(nc.sbuf_tensor(f"x1b{i}", [128, 2048], BF16)) for i in range(2)]
            clb = sc4.enter_context(nc.sbuf_tensor("clb", [128, 12], F32))
            p_w = [sc4.enter_context(nc.psum_tensor(f"p_w{i}", [128, 512], F32)) for i in range(2)]
            p_tr32 = [sc4.enter_context(nc.psum_tensor(f"p_tr{i}", [128, 512], F32)) for i in range(2)]
            p_tr = [p.ap().bitcast(BF16) for p in p_tr32]

            for m in range(16):
                i2 = m % 2
                tn, ub, tT, x1pre, x1b, sq = tnl[i2], ub4[i2], tT4[i2], x1p4[i2], x1b4[i2], sqb[i2]
                tnk, uk, tTk, xpk, xbk, sqk = (f"tnl{i2}", f"u4_{i2}", f"tT{i2}",
                                               f"x1p{i2}", f"x1b{i2}", f"sqb{i2}")
                c0 = 6 * i2
                ck = f"clb{i2}"
                em.dma("qsp", tn[:, :], tn_sp[128 * m:128 * (m + 1), :],
                       reads=("tn_sp",), writes=(tnk,))
                em.dma("qpool", ub[:, :], u_own[128 * m:128 * (m + 1), :], writes=(uk,))
                for tt in range(8):
                    ptr, pk = p_tr[tt % 2], f"p_tr{tt % 2}"

                    def trg(t, tt=tt, ptr=ptr, tn=tn):
                        for qq in range(4):
                            j = 4 * tt + qq
                            ins = t.transpose(ptr[:, 128 * qq:128 * (qq + 1)],
                                              tn[:, 128 * j:128 * (j + 1)], ident[:, :])
                        return ins
                    em.op("pe", trg, reads=(tnk, "ident", pk), writes=(pk,))
                    em.op("act", lambda s, tt=tt, ptr=ptr, tT=tT: s.copy(
                        tT[:, 512 * tt:512 * (tt + 1)], ptr[:, 0:512]),
                        reads=(pk,), writes=(tTk,))
                for nchk in range(4):
                    pw, pk = p_w[nchk % 2], f"p_w{nchk % 2}"

                    def wmm(t, nchk=nchk, pw=pw, tT=tT):
                        for k in range(32):
                            ins = t.matmul(pw[:, :], tT[:, 128 * k:128 * (k + 1)],
                                           woutb[:, 2048 * k + 512 * nchk:2048 * k + 512 * (nchk + 1)],
                                           start=(k == 0), stop=(k == 31))
                        return ins
                    em.op("pe", wmm, reads=(tTk, "woutb", pk), writes=(pk,))
                    em.op("dve", lambda v, nchk=nchk, pw=pw, ub=ub, x1pre=x1pre: v.tensor_tensor(
                        x1pre[:, 512 * nchk:512 * (nchk + 1)], pw[:, :],
                        ub[:, 512 * nchk:512 * (nchk + 1)], op=OP.add),
                        reads=(pk, uk), writes=(xpk,))
                em.op("dve", lambda v, x1pre=x1pre, c0=c0: v.tensor_reduce(
                    clb[:, c0:c0 + 1], x1pre[:, :], axis=AX.X, op=OP.add),
                    reads=(xpk,), writes=(ck,))
                em.op("dve", lambda v, c0=c0: v.tensor_scalar_mul(clb[:, c0:c0 + 1],
                                                                  clb[:, c0:c0 + 1], 1.0 / D_MODEL),
                      reads=(ck,), writes=(ck,))
                em.op("act", lambda s, x1pre=x1pre, sq=sq, c0=c0: s.activation(
                    sq[:, :], x1pre[:, :], AF.Square, accum_out=clb[:, c0 + 1:c0 + 2]),
                    reads=(xpk,), writes=(sqk, ck))
                em.op("dve", lambda v, c0=c0: v.tensor_scalar_mul(clb[:, c0 + 1:c0 + 2],
                                                                  clb[:, c0 + 1:c0 + 2], 1.0 / D_MODEL),
                      reads=(ck,), writes=(ck,))
                em.op("dve", lambda v, c0=c0: v.scalar_tensor_tensor(
                    clb[:, c0 + 2:c0 + 3], clb[:, c0:c0 + 1], clb[:, c0:c0 + 1],
                    clb[:, c0 + 1:c0 + 2], op0=OP.mult, op1=OP.subtract),
                    reads=(ck,), writes=(ck,))
                em.op("dve", lambda v, c0=c0: v.tensor_scalar_mul(clb[:, c0 + 2:c0 + 3],
                                                                  clb[:, c0 + 2:c0 + 3], -1.0),
                      reads=(ck,), writes=(ck,))
                em.op("act", lambda s, c0=c0: s.activation(clb[:, c0 + 3:c0 + 4],
                                                           clb[:, c0 + 2:c0 + 3], AF.Ln,
                                                           bias=small[:, 7:8], scale=1.0),
                      reads=(ck, "small"), writes=(ck,))
                em.op("dve", lambda v, c0=c0: v.tensor_scalar_mul(clb[:, c0 + 3:c0 + 4],
                                                                  clb[:, c0 + 3:c0 + 4], -0.5),
                      reads=(ck,), writes=(ck,))
                em.op("act", lambda s, c0=c0: s.activation(clb[:, c0 + 3:c0 + 4],
                                                           clb[:, c0 + 3:c0 + 4], AF.Exp),
                      reads=(ck,), writes=(ck,))
                em.op("dve", lambda v, c0=c0: v.scalar_tensor_tensor(
                    clb[:, c0 + 4:c0 + 5], clb[:, c0:c0 + 1], -1.0, clb[:, c0 + 3:c0 + 4],
                    op0=OP.mult, op1=OP.mult),
                    reads=(ck,), writes=(ck,))
                em.op("dve", lambda v, x1pre=x1pre, x1b=x1b, c0=c0: v.tensor_scalar(
                    x1b[:, :], x1pre[:, :], clb[:, c0 + 3:c0 + 4], clb[:, c0 + 4:c0 + 5],
                    op0=OP.mult, op1=OP.add),
                    reads=(xpk, ck), writes=(xbk,))
                em.dma("qsp", x1_sp[128 * m:128 * (m + 1), :], x1b[:, :], reads=(xbk,),
                       writes=("x1_sp",))

        em.barrier()

        qcols = stack.enter_context(nc.sbuf_tensor("qcols", [128, 64], F32))
        p_f1 = [stack.enter_context(nc.psum_tensor(f"p_f1{i}", [128, 512], F32)) for i in range(4)]
        p_tq32 = [stack.enter_context(nc.psum_tensor(f"p_tq{i}", [128, 512], F32)) for i in range(2)]
        p_tq = [p.ap().bitcast(BF16) for p in p_tq32]

        with ExitStack() as scE1:
            xqT = scE1.enter_context(nc.sbuf_tensor("xqT", [128, 16 * 2048], BF16))
            x1l = [scE1.enter_context(nc.sbuf_tensor(f"x1l{i}", [128, 2048], BF16)) for i in range(2)]
            hgrp = [scE1.enter_context(nc.sbuf_tensor(f"hgrp{i}", [128, 8192], BF16)) for i in range(4)]
            w1qb = [scE1.enter_context(nc.sbuf_tensor("w1qb0", [128, 8192], BF16))]
            y2b = scE1.enter_context(nc.sbuf_tensor("y2b", [128, 2048], F32))
            mfr = scE1.enter_context(nc.sbuf_tensor("mfr", [128, 2048], F32))
            ibuf = scE1.enter_context(nc.sbuf_tensor("ibuf", [128, 2048], mybir.dt.int32))
            qfb = scE1.enter_context(nc.sbuf_tensor("qfb", [128, 2048], F32))
            hqs = scE1.enter_context(nc.sbuf_tensor("hqs", [128, 2048], BF16))


            y2c = y2b.ap().bitcast(BF16)
            mfc = mfr.ap().bitcast(BF16)

            def emit_round(in_ap, in_key, sc_key, sc_ap, dst_ap, dst_key):
                em.op("dve", lambda v: v.tensor_scalar_mul(y2b[:, :], in_ap, sc_ap),
                      reads=(in_key, sc_key), writes=("y2b",))
                em.op("dve", lambda v: v.tensor_copy(ibuf[:, :], y2b[:, :]),
                      reads=("y2b",), writes=("ibuf",))
                em.op("dve", lambda v: v.tensor_copy(qfb[:, :], ibuf[:, :]),
                      reads=("ibuf",), writes=("qfb",))
                em.op("dve", lambda v: v.scalar_tensor_tensor(mfr[:, :], qfb[:, :], -1.0,
                                                              y2b[:, :], op0=OP.mult, op1=OP.add),
                      reads=("qfb", "y2b"), writes=("mfr",))
                em.op("act", lambda s: s.activation(y2c[:, 0:2048], mfr[:, :], AF.Sign,
                                                    bias=small[:, 9:10], scale=1.0),
                      reads=("mfr", "small"), writes=("y2b",))
                em.op("act", lambda s: s.activation(y2c[:, 2048:4096], mfr[:, :], AF.Sign,
                                                    bias=small[:, 5:6], scale=1.0),
                      reads=("mfr", "small"), writes=("y2b",))
                em.op("dve", lambda v: v.tensor_tensor(mfc[:, 0:2048], y2c[:, 0:2048],
                                                       y2c[:, 2048:4096], op=OP.add),
                      reads=("y2b",), writes=("mfr",))
                em.op("dve", lambda v: v.scalar_tensor_tensor(dst_ap, mfc[:, 0:2048], 0.5,
                                                              qfb[:, :], op0=OP.mult, op1=OP.add),
                      reads=("mfr", "qfb"), writes=(dst_key,))

            for q in range(4):
                for mi in range(4):
                    m = 4 * q + mi
                    xb, xk = x1l[mi % 2], f"x1l{mi % 2}"
                    em.dma("qsp", xb[:, :], x1_sp[128 * m:128 * (m + 1), :],
                           reads=("x1_sp",), writes=(xk,))
                    em.op("dve", lambda v, xb=xb, m=m: v.reduce_max(
                        qcols[:, m:m + 1], xb[:, :], axis=AX.X,
                        apply_absolute_value=True),
                        reads=(xk,), writes=("qamax",))
                    em.op("dve", lambda v, m=m: v.tensor_scalar_max(
                        qcols[:, 48:49], qcols[:, m:m + 1], EPS),
                        reads=("qamax",), writes=("q48",))
                    em.op("dve", lambda v: v.reciprocal(qcols[:, 49:50], qcols[:, 48:49]),
                          reads=("q48",), writes=("q49",))
                    em.op("dve", lambda v: v.tensor_scalar_mul(qcols[:, 49:50],
                                                               qcols[:, 49:50], 127.0),
                          reads=("q49",), writes=("q49",))
                    em.op("dve", lambda v, m=m: v.tensor_tensor(
                        qcols[:, 16 + m:17 + m], qcols[:, 48:49], small[:, 12:13], op=OP.mult),
                        reads=("q48", "small"), writes=("qc1",))
                    emit_round(xb[:, :], xk, "q49", qcols[:, 49:50], hqs[:, :],
                               "hqsAB")
                    for tt in range(4):
                        ptq, pk = p_tq[tt % 2], f"p_tq{tt % 2}"

                        def xtr(t, tt=tt, ptq=ptq):
                            for qq in range(4):
                                j = 4 * tt + qq
                                ins = t.transpose(ptq[:, 128 * qq:128 * (qq + 1)],
                                                  hqs[:, 128 * j:128 * (j + 1)], ident[:, :])
                            return ins
                        em.op("pe", xtr, reads=("hqsAB", "ident", pk), writes=(pk,))

                        def xev(v, tt=tt, ptq=ptq, m=m):
                            for qq in range(4):
                                k = 4 * tt + qq
                                ins = v.tensor_copy(
                                    xqT[:, 2048 * k + 128 * m:2048 * k + 128 * (m + 1)],
                                    ptq[:, 128 * qq:128 * (qq + 1)])
                            return ins
                        em.op("dve", xev, reads=(pk,), writes=("xqT",))
                for fc in range(16):
                    wq, wk = w1qb[0], "w1qb0"
                    em.dma("qpool", wq[:, :], w1q_sp[fc, :, :], reads=("wq_sp",), writes=(wk,))
                    for mi in range(4):
                        m = 4 * q + mi
                        pf, pk = p_f1[mi], f"p_f1{mi}"

                        def f1mm(t, wq=wq, m=m, pf=pf):
                            for k in range(16):
                                ins = t.matmul(pf[:, :],
                                               xqT[:, 2048 * k + 128 * m:2048 * k + 128 * (m + 1)],
                                               wq[:, 512 * k:512 * (k + 1)],
                                               start=(k == 0), stop=(k == 15))
                            return ins
                        em.op("pe", f1mm, reads=("xqT", wk, pk), writes=(pk,))
                        em.op("act", lambda s, pf=pf, mi=mi, m=m, fc=fc: s.activation(
                            hgrp[mi][:, 512 * fc:512 * (fc + 1)], pf[:, :], AF.Gelu,
                            bias=small[:, 18:19], scale=qcols[:, 16 + m:17 + m]),
                            reads=(pk, "qc1", "small"), writes=(f"hgrp{mi}",))
                if debug and q == 3:
                    em.dma("qsp", dbg_b[:, :], hgrp[0][:, :], reads=("hgrp0",), writes=("dbg_b",))
                    em.dma("qsp", dbg_a[:, 0:64], qcols[:, :], reads=("qamax", "qc1", "qc2", "q48", "q49", "q50", "q51"), writes=("dbg_a",))
                for mi in range(4):
                    m = 4 * q + mi
                    hg, hk = hgrp[mi], f"hgrp{mi}"
                    em.op("dve", lambda v, hg=hg: v.reduce_max(
                        qcols[:, 50:51], hg[:, :], axis=AX.X,
                        apply_absolute_value=True),
                        reads=(hk,), writes=("q50",))
                    em.op("dve", lambda v: v.tensor_scalar_max(qcols[:, 50:51], qcols[:, 50:51],
                                                               EPS),
                          reads=("q50",), writes=("q50",))
                    em.op("dve", lambda v: v.reciprocal(qcols[:, 51:52], qcols[:, 50:51]),
                          reads=("q50",), writes=("q51",))
                    em.op("dve", lambda v: v.tensor_scalar_mul(qcols[:, 51:52],
                                                               qcols[:, 51:52], 127.0),
                          reads=("q51",), writes=("q51",))
                    em.op("dve", lambda v, m=m: v.tensor_tensor(
                        qcols[:, 32 + m:33 + m], qcols[:, 50:51], small[:, 13:14], op=OP.mult),
                        reads=("q50", "small"), writes=("qc2",))
                    for piece in range(4):
                        sl = slice(2048 * piece, 2048 * (piece + 1))
                        emit_round(hg[:, sl], hk, "q51", qcols[:, 51:52], hg[:, sl], hk)
                    for tt in range(16):
                        ptq, pk = p_tq[tt % 2], f"p_tq{tt % 2}"

                        def htr(t, tt=tt, ptq=ptq, hg=hg):
                            for qq in range(4):
                                j = 4 * tt + qq
                                ins = t.transpose(ptq[:, 128 * qq:128 * (qq + 1)],
                                                  hg[:, 128 * j:128 * (j + 1)], ident[:, :])
                            return ins
                        em.op("pe", htr, reads=(hk, "ident", pk), writes=(pk,))
                        st_off = 512 * (tt % 2)
                        em.op("dve", lambda v, ptq=ptq, st_off=st_off: v.tensor_copy(
                            hqs[:, st_off:st_off + 512], ptq[:, 0:512]),
                            reads=(pk,), writes=("hqsAB",))
                        em.dma("qsp",
                               hqT_sp.rearrange("(b p) t -> b p t", p=128)
                               [4 * tt:4 * (tt + 1), :, 128 * m:128 * (m + 1)]
                               .transpose([1, 0, 2]),
                               hqs.ap().rearrange("p (b t) -> p b t", b=16)
                               [:, 4 * (tt % 2):4 * (tt % 2 + 1), :],
                               reads=("hqsAB",), writes=("hqT_sp",))

        em.barrier()

        with ExitStack() as scE2:
            hqTl = scE2.enter_context(nc.sbuf_tensor("hqTl", [128, 64 * 512], BF16))
            w2qb = [scE2.enter_context(nc.sbuf_tensor(f"w2qb{i}", [128, 16 * 512], BF16)) for i in range(2)]
            x2pre = [scE2.enter_context(nc.sbuf_tensor(f"x2pre{i}", [128, 2048], F32)) for i in range(4)]
            x1q = [scE2.enter_context(nc.sbuf_tensor(f"x1q{i}", [128, 2048], BF16)) for i in range(4)]
            outb = scE2.enter_context(nc.sbuf_tensor("outb", [128, 2048], F32))
            sq2 = scE2.enter_context(nc.sbuf_tensor("sq2", [128, 2048], BF16))
            for q in range(4):
                for k in range(64):
                    em.dma("qsp", hqTl[:, 512 * k:512 * (k + 1)],
                           hqT_sp[128 * k:128 * (k + 1), 512 * q:512 * (q + 1)],
                           reads=("hqT_sp",), writes=("hqTl",))
                for mi in range(4):
                    m = 4 * q + mi
                    em.dma("qsp", x1q[mi][:, :], x1_sp[128 * m:128 * (m + 1), :],
                           reads=("x1_sp",), writes=(f"x1q{mi}",))
                for d2c in range(4):
                    for ks in range(4):
                        wq, wk = w2qb[ks % 2], f"w2qb{ks % 2}"
                        em.dma("qpool", wq.ap().rearrange("p (k f) -> p k f", k=16),
                               w2q_sp[d2c, 16 * ks:16 * (ks + 1), :, :].rearrange("k p f -> p k f"),
                               reads=("wq_sp",), writes=(wk,))
                        for mi in range(4):
                            pf, pk = p_f1[mi], f"p_f1{mi}"

                            def f2mm(t, wq=wq, ks=ks, mi=mi, pf=pf):
                                for kk in range(16):
                                    k = 16 * ks + kk
                                    ins = t.matmul(
                                        pf[:, :],
                                        hqTl[:, 512 * k + 128 * mi:512 * k + 128 * (mi + 1)],
                                        wq[:, 512 * kk:512 * (kk + 1)],
                                        start=(k == 0), stop=(k == 63))
                                return ins
                            em.op("pe", f2mm, reads=("hqTl", wk, pk), writes=(pk,))
                    for mi in range(4):
                        m = 4 * q + mi
                        pf, pk = p_f1[mi], f"p_f1{mi}"
                        em.op("dve", lambda v, pf=pf, mi=mi, d2c=d2c, m=m: v.scalar_tensor_tensor(
                            x2pre[mi][:, 512 * d2c:512 * (d2c + 1)], pf[:, :],
                            qcols[:, 32 + m:33 + m], x1q[mi][:, 512 * d2c:512 * (d2c + 1)],
                            op0=OP.mult, op1=OP.add),
                            reads=(pk, "qc2", f"x1q{mi}"), writes=(f"x2pre{mi}",))
                for mi in range(4):
                    m = 4 * q + mi
                    xp, xk = x2pre[mi], f"x2pre{mi}"
                    em.op("dve", lambda v, xp=xp: v.tensor_reduce(qcols[:, 52:53], xp[:, :],
                                                                  axis=AX.X, op=OP.add),
                          reads=(xk,), writes=("q52",))
                    em.op("dve", lambda v: v.tensor_scalar_mul(qcols[:, 52:53], qcols[:, 52:53],
                                                               1.0 / D_MODEL),
                          reads=("q52",), writes=("q52",))
                    em.op("act", lambda s, xp=xp: s.activation(sq2[:, :], xp[:, :], AF.Square,
                                                               accum_out=qcols[:, 53:54]),
                          reads=(xk,), writes=("sq2", "q53"))
                    em.op("dve", lambda v: v.tensor_scalar_mul(qcols[:, 53:54], qcols[:, 53:54],
                                                               1.0 / D_MODEL),
                          reads=("q53",), writes=("q53",))
                    em.op("dve", lambda v: v.scalar_tensor_tensor(qcols[:, 54:55], qcols[:, 52:53],
                                                                  qcols[:, 52:53], qcols[:, 53:54],
                                                                  op0=OP.mult, op1=OP.subtract),
                          reads=("q52", "q53"), writes=("q54",))
                    em.op("dve", lambda v: v.tensor_scalar_mul(qcols[:, 54:55], qcols[:, 54:55],
                                                               -1.0),
                          reads=("q54",), writes=("q54",))
                    em.op("act", lambda s: s.activation(qcols[:, 54:55], qcols[:, 54:55], AF.Ln,
                                                        bias=small[:, 7:8], scale=1.0),
                          reads=("q54", "small"), writes=("q54",))
                    em.op("dve", lambda v: v.tensor_scalar_mul(qcols[:, 54:55], qcols[:, 54:55],
                                                               -0.5),
                          reads=("q54",), writes=("q54",))
                    em.op("act", lambda s: s.activation(qcols[:, 54:55], qcols[:, 54:55], AF.Exp),
                          reads=("q54",), writes=("q54",))
                    em.op("dve", lambda v: v.scalar_tensor_tensor(qcols[:, 55:56], qcols[:, 52:53],
                                                                  -1.0, qcols[:, 54:55],
                                                                  op0=OP.mult, op1=OP.mult),
                          reads=("q52", "q54"), writes=("q55",))
                    em.op("dve", lambda v, xp=xp: v.tensor_scalar(outb[:, :], xp[:, :],
                                                                  qcols[:, 54:55], qcols[:, 55:56],
                                                                  op0=OP.mult, op1=OP.add),
                          reads=(xk, "q54", "q55"), writes=("outb",))
                    em.dma("qsp", out_t[128 * m:128 * (m + 1), :], outb[:, :], reads=("outb",),
                           writes=("out_t",))

        em.wait_all("sp")
        em.wait_all("pool")
        em.flush(block)
    stack.close()
    return nc


_NC_CACHE = {}


def kernel(**inputs):
    u = np.ascontiguousarray(np.asarray(inputs["L0_attn_u"], np.float32))
    W_in = np.ascontiguousarray(np.asarray(inputs["L0_attn_W_in"], np.float32))
    conv_w = np.asarray(inputs["L0_attn_conv_w"], np.float32)
    conv_b = np.asarray(inputs["L0_attn_conv_b"], np.float32)
    dt_bias = np.asarray(inputs["L0_attn_dt_bias"], np.float32)
    A_log = np.asarray(inputs["L0_attn_A_log"], np.float32)
    Dv = np.asarray(inputs["L0_attn_D"], np.float32)
    norm_w = np.asarray(inputs["L0_attn_norm_w"], np.float32)
    W_out = np.ascontiguousarray(np.asarray(inputs["L0_attn_W_out"], np.float32))
    w1 = np.asarray(inputs["L0_ffn_w1"], np.float32)
    w2 = np.asarray(inputs["L0_ffn_w2"], np.float32)

    wx_pad = np.zeros((D_MODEL, XB * 128), np.float32)
    wx_pad[:, :4416] = W_in[:, D_INNER:]
    w_in_x = np.ascontiguousarray(
        wx_pad.reshape(16, 128, XB, 128).transpose(2, 0, 1, 3))
    w_in_z = np.ascontiguousarray(W_in[:, :D_INNER])
    w1t = np.ascontiguousarray(w1.T)
    w2t = np.ascontiguousarray(w2.T)
    convw_pad = np.zeros((XB * 128, 4), np.float32)
    convw_pad[:4352] = conv_w
    convb_pad = np.zeros((XB * 128,), np.float32)
    convb_pad[:4352] = conv_b
    small_c = np.zeros((64, 4), np.float32)
    small_c[:, 0] = dt_bias
    small_c[:, 1] = A_log
    small_r = np.zeros((4, 64), np.float32)
    small_r[2] = Dv

    in_maps = []
    for c in range(8):
        b, h = c // 2, c % 2
        g0 = h * T_OWN
        ut = np.zeros((D_MODEL, T_ALL), np.float32)
        lo = g0 - (T_FOR + HALO)
        src_lo = max(lo, 0)
        ut[:, src_lo - lo:T_FOR + HALO] = u[b, src_lo:g0].T
        ut[:, T_FOR + HALO:] = u[b, g0:g0 + T_OWN].T
        mask = np.full((128, 1), 0.0 if h == 0 else 1.0, np.float32)
        in_maps.append(dict(
            u_t=np.ascontiguousarray(ut),
            u_own=np.ascontiguousarray(u[b, g0:g0 + T_OWN]),
            w_in_x=w_in_x, w_in_z=w_in_z, w_out=W_out, w1t=w1t, w2t=w2t,
            conv_wt=convw_pad, conv_bt=convb_pad, small_c=small_c,
            small_r=small_r, norm_w=norm_w, seq_mask=mask))

    if "nc" not in _NC_CACHE:
        _NC_CACHE["nc"] = build_program()
    nc = _NC_CACHE["nc"]
    res = run_bass_kernel_spmd(nc, in_maps, core_ids=list(range(8)))
    out = np.zeros((4, 4096, D_MODEL), np.float32)
    for c in range(8):
        b, h = c // 2, c % 2
        out[b, h * T_OWN:(h + 1) * T_OWN] = res.results[c]["out"]
    return out


# version 14
# speedup vs baseline: 1.0531x; 1.0531x over previous
import zlib
import numpy as np
import ml_dtypes
import concourse.bass as bass
import concourse.mybir as mybir
from concourse.bass_utils import run_bass_kernel_spmd
from contextlib import ExitStack

F32 = mybir.dt.float32
BF16 = mybir.dt.bfloat16
FP8 = mybir.dt.float8e4
AF = mybir.ActivationFunctionType
OP = mybir.AluOpType
AX = mybir.AxisListType
bfd = ml_dtypes.bfloat16

D_MODEL = 2048
D_INNER = 4096
CHUNK = 128
KFOR = 4
T_OWN = 2048
T_FOR = KFOR * CHUNK
HALO = 3
T_ALL = T_OWN + T_FOR + HALO
T_CONV = T_OWN + T_FOR
NCH = T_CONV // CHUNK
XB = 35
EPS = 1e-5
BIG = 8192.0

ENGINES = ("pe", "dve", "act", "pool", "sp")
NSUBQ = 6
QUEUES = tuple(f"qsp_{i}" for i in range(NSUBQ)) + ("qact",) + tuple(f"qpool_{i}" for i in range(NSUBQ))
Q_OWNER = {**{f"qsp_{i}": "sp" for i in range(NSUBQ)}, "qact": "act",
           **{f"qpool_{i}": "pool" for i in range(NSUBQ)}}


class Em:

    def __init__(self, nc):
        self.nc = nc
        self.sems = {}
        self.cnt = {s: 0 for s in ENGINES + QUEUES}
        self.pend = {e: [] for e in ENGINES}
        self.last_w = {}
        self.readers = {}
        self.wm = {}

    def init_sems(self, stack):
        for s in ENGINES + QUEUES:
            self.sems[s] = stack.enter_context(self.nc.semaphore("sem_" + s))

    SAME_ENGINE_EDGES = True

    def _need(self, eng, reads, writes):
        need = {}
        se = self.SAME_ENGINE_EDGES
        for k in reads:
            lw = self.last_w.get(k)
            if lw is not None and (se or lw[0] != eng):
                need[lw[0]] = max(need.get(lw[0], 0), lw[1])
        for k in writes:
            lw = self.last_w.get(k)
            if lw is not None and (se or lw[0] != eng):
                need[lw[0]] = max(need.get(lw[0], 0), lw[1])
            for st, c in self.readers.get(k, {}).items():
                if se or st != eng:
                    need[st] = max(need.get(st, 0), c)
        out = {}
        for st, c in need.items():
            if st in QUEUES:
                c = self.cnt[st]
            if st == eng:
                if c >= self.cnt[eng] + 1:
                    continue
            if self.wm.get((eng, st), 0) < c:
                out[st] = c
                self.wm[(eng, st)] = c
        return out

    def _track(self, stream, cnt, reads, writes):
        for k in reads:
            self.readers.setdefault(k, {})[stream] = cnt
        for k in writes:
            self.last_w[k] = (stream, cnt)
            self.readers[k] = {}

    def op(self, eng, fn, reads=(), writes=()):
        need = self._need(eng, reads, writes)
        self.cnt[eng] += 1
        cnt = self.cnt[eng]
        sems = self.sems

        def closure(e, need=need, fn=fn):
            for st, c in need.items():
                e.wait_ge(sems[st], c)
            fn(e).then_inc(sems[eng], 1)
        self.pend[eng].append(closure)
        self._track(eng, cnt, reads, writes)

    def _subq(self, q, writes):
        if q in ("qpool", "qsp"):
            tag = writes[0] if writes else "x"
            return f"{q}_{zlib.crc32(str(tag).encode()) % NSUBQ}"
        return q

    def dma(self, q, out, in_, reads=(), writes=(), **kw):
        q = self._subq(q, writes)
        eng = Q_OWNER[q]
        need = self._need(eng, reads, writes)
        self.cnt[q] += 16
        qc = self.cnt[q]
        sems = self.sems

        def closure(e, need=need, out=out, in_=in_, kw=kw):
            for st, c in need.items():
                e.wait_ge(sems[st], c)
            e.dma_start(out=out, in_=in_, **kw).then_inc(sems[q], 16)
        self.pend[eng].append(closure)
        self._track(q, qc, reads, writes)

    def dma_t(self, q, out, in_, reads=(), writes=()):
        q = self._subq(q, writes)
        eng = Q_OWNER[q]
        need = self._need(eng, reads, writes)
        self.cnt[q] += 16
        qc = self.cnt[q]
        sems = self.sems

        def closure(e, need=need, out=out, in_=in_):
            for st, c in need.items():
                e.wait_ge(sems[st], c)
            e.dma_start_transpose(out, in_).then_inc(sems[q], 16)
        self.pend[eng].append(closure)
        self._track(q, qc, reads, writes)

    def wait_all(self, eng):
        sems = self.sems
        vals = [(st, self.cnt[st]) for st in ENGINES + QUEUES
                if st != eng and self.cnt[st] > 0]

        def closure(e, vals=vals):
            for st, c in vals:
                e.wait_ge(sems[st], c)
        self.pend[eng].append(closure)
        for st, c in vals:
            self.wm[(eng, st)] = max(self.wm.get((eng, st), 0), c)

    def barrier(self):
        for e in ENGINES:
            self.wait_all(e)
        self.last_w = {}
        self.readers = {}

    def flush(self, block):
        for e, lst in self.pend.items():
            if not lst:
                continue
            body = {"pe": block.tensor, "dve": block.vector, "act": block.scalar,
                    "pool": block.gpsimd, "sp": block.sync}[e]

            def run(engine, lst=tuple(lst)):
                for c in lst:
                    c(engine)
            body(run)
            self.pend[e] = []


def build_program(debug=False):
    nc = bass.Bass("TRN2", target_bir_lowering=False)
    SCR = "ExternalOutput" if debug else "Internal"
    u_t = nc.dram_tensor("u_t", [D_MODEL, T_ALL], F32, kind="ExternalInput").ap()
    u_own = nc.dram_tensor("u_own", [T_OWN, D_MODEL], F32, kind="ExternalInput").ap()
    w_in_x = nc.dram_tensor("w_in_x", [XB, 16, 128, 128], F32, kind="ExternalInput").ap()
    w_in_z = nc.dram_tensor("w_in_z", [D_MODEL, D_INNER], F32, kind="ExternalInput").ap()
    w_out = nc.dram_tensor("w_out", [D_INNER, D_MODEL], F32, kind="ExternalInput").ap()
    w1t = nc.dram_tensor("w1t", [D_MODEL, 8192], F32, kind="ExternalInput").ap()
    w2t = nc.dram_tensor("w2t", [8192, D_MODEL], F32, kind="ExternalInput").ap()
    conv_wt = nc.dram_tensor("conv_wt", [XB * 128, 4], F32, kind="ExternalInput").ap()
    conv_bt = nc.dram_tensor("conv_bt", [XB * 128], F32, kind="ExternalInput").ap()
    small_c = nc.dram_tensor("small_c", [64, 4], F32, kind="ExternalInput").ap()
    small_r = nc.dram_tensor("small_r", [4, 64], F32, kind="ExternalInput").ap()
    norm_w = nc.dram_tensor("norm_w", [D_INNER], F32, kind="ExternalInput").ap()
    seq_mask = nc.dram_tensor("seq_mask", [128, 1], F32, kind="ExternalInput").ap()
    out_t = nc.dram_tensor("out", [T_OWN, D_MODEL], F32, kind="ExternalOutput").ap()

    x_sp = nc.dram_tensor("x_sp", [D_INNER, T_CONV], BF16, kind=SCR).ap()
    zsil_sp = nc.dram_tensor("zsil_sp", [8, T_OWN, 512], BF16, kind=SCR).ap()
    y_sp = nc.dram_tensor("y_sp", [T_OWN, D_INNER], BF16, kind=SCR).ap()
    x1_sp = nc.dram_tensor("x1_sp", [T_OWN, D_MODEL], BF16, kind=SCR).ap()
    woutb_sp = nc.dram_tensor("woutb_sp", [D_INNER, D_MODEL], BF16, kind=SCR).ap()
    w1q_sp = nc.dram_tensor("w1q_sp", [16, 128, 8192], FP8, kind=SCR).ap()
    w2q_sp = nc.dram_tensor("w2q_sp", [4, 64, 128, 512], FP8, kind=SCR).ap()
    hqT_sp = nc.dram_tensor("hqT_sp", [8192, T_OWN], BF16, kind=SCR).ap()
    tn_sp = nc.dram_tensor("tn_sp", [T_OWN, D_INNER], BF16, kind="Internal").ap()
    dbg_a = nc.dram_tensor("dbg_a", [128, 4096], F32, kind=SCR).ap()
    dbg_b = nc.dram_tensor("dbg_b", [128, 8192], BF16, kind=SCR).ap()
    dbg_c = nc.dram_tensor("dbg_c", [128, 8192], BF16, kind=SCR).ap()
    dbg_d = nc.dram_tensor("dbg_d", [128, 4096], F32, kind=SCR).ap()

    ii = np.arange(128)
    c_ident = nc.inline_tensor(np.eye(128, dtype=bfd), "c_ident").ap()
    c_ident32 = nc.inline_tensor(np.eye(128, dtype=np.float32), "c_ident32").ap()
    c_ones = nc.inline_tensor(np.ones((128, 128), dtype=bfd), "c_ones").ap()
    c_ones32 = nc.inline_tensor(np.ones((1, 128), dtype=np.float32), "c_ones32").ap()
    c_tri = nc.inline_tensor((ii[:, None] <= ii[None, :]).astype(bfd), "c_tri").ap()
    c_ntril = nc.inline_tensor((-(ii[:, None] <= ii[None, :]).astype(np.float32)).astype(bfd), "c_ntril").ap()
    c_nbig = nc.inline_tensor((-BIG * (ii[:, None] < ii[None, :]).astype(np.float32)).astype(bfd), "c_nbig").ap()

    stack = ExitStack()
    em = Em(nc)

    with nc.Block() as block:
        em.init_sems(stack)
        ident = stack.enter_context(nc.sbuf_tensor("ident", [128, 128], BF16))
        ident32 = stack.enter_context(nc.sbuf_tensor("ident32", [128, 128], F32))
        onesb = stack.enter_context(nc.sbuf_tensor("onesb", [128, 128], BF16))
        ones32 = stack.enter_context(nc.sbuf_tensor("ones32", [1, 128], F32))
        trib = stack.enter_context(nc.sbuf_tensor("trib", [128, 128], BF16))
        ntrilb = stack.enter_context(nc.sbuf_tensor("ntrilb", [128, 128], BF16))
        nbigb = stack.enter_context(nc.sbuf_tensor("nbigb", [128, 128], BF16))
        small = stack.enter_context(nc.sbuf_tensor("small", [128, 24], F32))
        dbc = stack.enter_context(nc.sbuf_tensor("dbc", [128, 64], BF16))
        drow = stack.enter_context(nc.sbuf_tensor("drow", [1, 64], BF16))
        convw = stack.enter_context(nc.sbuf_tensor("convw", [128, XB * 4], F32))
        convb = stack.enter_context(nc.sbuf_tensor("convb", [128, XB], F32))

        em.dma("qsp", ident[:, :], c_ident[:, :], writes=("ident",))
        em.dma("qsp", ident32[:, :], c_ident32[:, :], writes=("ident32",))
        em.dma("qsp", onesb[:, :], c_ones[:, :], writes=("onesb",))
        em.dma("qsp", ones32[:, :], c_ones32[:, :], writes=("ones32",))
        em.dma("qsp", trib[:, :], c_tri[:, :], writes=("trib",))
        em.dma("qsp", ntrilb[:, :], c_ntril[:, :], writes=("ntrilb",))
        em.dma("qsp", nbigb[:, :], c_nbig[:, :], writes=("nbigb",))
        em.dma("qsp", small[0:64, 0:2], small_c[:, 0:2], writes=("small",))
        em.dma("qsp", small[:, 8:9], seq_mask[:, :], writes=("small",))
        em.dma("qpool", drow[0:1, :], small_r[2:3, :], writes=("drow",))
        em.dma("qpool", convw.ap().rearrange("p (c k) -> p c k", k=4),
               conv_wt.rearrange("(c p) k -> p c k", p=128), writes=("convw",))
        em.dma("qsp", convb[:, :],
               conv_bt.rearrange("(c p) -> p c", p=128), writes=("convb",),
               allow_slow_non_contiguous=True)
        for col, val in ((4, 1.0), (5, 0.5), (6, 127.0), (7, EPS), (9, -0.5),
                         (14, 0.0), (15, 0.0), (18, 0.0)):
            em.op("dve", lambda v, col=col, val=val: v.memset(small[:, col:col + 1], val),
                  writes=("small",))
        em.op("act", lambda s: s.activation(small[0:64, 3:4], small[0:64, 1:2], AF.Exp),
              reads=("small",), writes=("small",))
        em.op("dve", lambda v: v.tensor_scalar_mul(small[0:64, 3:4], small[0:64, 3:4], -1.0),
              reads=("small",), writes=("small",))
        with ExitStack() as sc0:
            p0 = sc0.enter_context(nc.psum_tensor("p0t", [128, 64], F32))
            em.op("pe", lambda t: t.matmul(p0[:, 0:64], onesb[0:1, :], drow[0:1, :],
                                           start=True, stop=True),
                  reads=("onesb", "drow"), writes=("p0t",))
            em.op("dve", lambda v: v.tensor_copy(dbc[:, :], p0[:, 0:64]),
                  reads=("p0t",), writes=("dbc",))

        em.barrier()

        scAP3 = ExitStack()
        dbcr = scAP3.enter_context(nc.sbuf_tensor("dbcr", [128, 4096], BF16))
        em.op("dve", lambda v: v.tensor_copy(
            dbcr.ap().rearrange("p (h q) -> p h q", h=64),
            dbc.ap().unsqueeze(2).broadcast_to([128, 64, 64])),
            reads=("dbc",), writes=("dbcr",))
        b_res = scAP3.enter_context(nc.sbuf_tensor("b_res", [128, T_CONV], BF16))
        c_res = scAP3.enter_context(nc.sbuf_tensor("c_res", [128, T_CONV], BF16))
        dt_res = scAP3.enter_context(nc.sbuf_tensor("dt_res", [64, T_CONV], F32))
        with ExitStack() as scA:
            utb = scA.enter_context(nc.sbuf_tensor("utb", [128, 16 * T_ALL], BF16))
            for part in range(4):
                em.dma("qpool",
                       utb.ap().rearrange("p (k t) -> p k t", k=16)[:, 4 * part:4 * (part + 1), :],
                       u_t.rearrange("(k p) t -> p k t", p=128)[:, 4 * part:4 * (part + 1), :],
                       writes=("utb",))
            wxb = [scA.enter_context(nc.sbuf_tensor(f"wxb{i}", [128, 2048], BF16)) for i in range(2)]
            slab = [scA.enter_context(nc.sbuf_tensor(f"slab{i}", [128, T_ALL], BF16)) for i in range(2)]
            acc = [scA.enter_context(nc.sbuf_tensor(f"acc{i}", [128, T_CONV], BF16)) for i in range(2)]
            xact = [scA.enter_context(nc.sbuf_tensor(f"xact{i}", [128, T_CONV], BF16)) for i in range(2)]
            stat = [scA.enter_context(nc.sbuf_tensor(f"stat{i}", [128, 1024], F32)) for i in range(4)]
            statq = scA.enter_context(nc.sbuf_tensor("statq", [128, 1024], BF16))
            statacc = scA.enter_context(nc.sbuf_tensor("statacc", [128, 256], F32))
            em.op("dve", lambda v: v.memset(statacc[:, :], 0.0), writes=("statacc",))
            rmsrow = scA.enter_context(nc.sbuf_tensor("rmsrow", [2, 4], F32))
            p2p = [scA.enter_context(nc.psum_tensor(f"p2p{i}", [128, 512], F32)) for i in range(2)]
            p_st = scA.enter_context(nc.psum_tensor("p_st", [128, 8], F32))

            stat_jobs = []
            for r in range(16):
                for qr in range(8):
                    stat_jobs.append((w1t[128 * r:128 * (r + 1), 1024 * qr:1024 * (qr + 1)],
                                      r * 8 + qr))
            for r in range(64):
                for hh in range(2):
                    stat_jobs.append((w2t[128 * r:128 * (r + 1), 1024 * hh:1024 * (hh + 1)],
                                      128 + r * 2 + hh))

            def stat_dma(idx):
                src_ap, col = stat_jobs[idx]
                em.dma("qsp", stat[idx % 4][:, :], src_ap, writes=(f"stat{idx % 4}",))

            def stat_sq(idx):
                _, col = stat_jobs[idx]
                sb, sk = stat[idx % 4], f"stat{idx % 4}"
                em.op("act", lambda s, sb=sb, col=col: s.activation(
                    statq[:, :], sb[:, :], AF.Square, accum_out=statacc[:, col:col + 1]),
                    reads=(sk,), writes=("statq", "statacc"))

            stat_d = 0
            stat_s = 0

            subs = [(i * 512, 512) for i in range(5)] + [(2560, 3)]
            for c in range(XB):
                want_s = c * 256 // XB
                while stat_s < min(want_s, len(stat_jobs)):
                    while stat_d < min(stat_s + 4, len(stat_jobs)):
                        stat_dma(stat_d)
                        stat_d += 1
                    stat_sq(stat_s)
                    stat_s += 1
                wb, wkey = wxb[c % 2], f"wxb{c % 2}"
                em.dma("qpool", wb.ap().rearrange("p (k n) -> p k n", k=16),
                       w_in_x[c, :, :, :].rearrange("k p n -> p k n"), writes=(wkey,))
                sl, skey = slab[c % 2], f"slab{c % 2}"
                for si, (o, w) in enumerate(subs):
                    pb, pkey = p2p[si % 2], f"p2p{si % 2}"

                    def mmg(t, wb=wb, o=o, w=w, pb=pb):
                        for k in range(16):
                            ins = t.matmul(pb[:, 0:w], wb[:, 128 * k:128 * (k + 1)],
                                           utb[:, k * T_ALL + o:k * T_ALL + o + w],
                                           start=(k == 0), stop=(k == 15))
                        return ins
                    em.op("pe", mmg, reads=(wkey, "utb", pkey), writes=(pkey,))
                    if c == 34:
                        lo = max(o, HALO)
                        if lo < o + w:
                            em.op("dve", lambda v, pb=pb, o=o, w=w, lo=lo: v.tensor_copy(
                                dt_res[0:64, lo - HALO:o + w - HALO], pb[0:64, lo - o:w]),
                                reads=(pkey,), writes=("dt_res",))
                    else:
                        em.op("dve", lambda v, pb=pb, sl=sl, o=o, w=w: v.tensor_copy(
                            sl[:, o:o + w], pb[:, 0:w]),
                            reads=(pkey,), writes=(skey,))
                if c == 34:
                    continue
                ac, ackey = acc[c % 2], f"acc{c % 2}"
                xa, xakey = xact[c % 2], f"xact{c % 2}"
                em.op("dve", lambda v, sl=sl, ac=ac, c=c: v.tensor_scalar_mul(
                    ac[:, :], sl[:, 0:T_CONV], convw[:, 4 * c:4 * c + 1]),
                    reads=(skey, "convw"), writes=(ackey,))
                for k in range(1, 4):
                    em.op("dve", lambda v, sl=sl, ac=ac, c=c, k=k: v.scalar_tensor_tensor(
                        ac[:, :], sl[:, k:k + T_CONV], convw[:, 4 * c + k:4 * c + k + 1],
                        ac[:, :], op0=OP.mult, op1=OP.add),
                        reads=(skey, "convw", ackey), writes=(ackey,))
                em.op("act", lambda s, ac=ac, xa=xa, c=c: s.activation(
                    xa[:, :], ac[:, :], AF.Silu, bias=convb[:, c:c + 1], scale=1.0),
                    reads=(ackey, "convb"), writes=(xakey,))
                if c < 32:
                    em.dma("qact", x_sp[128 * c:128 * (c + 1), :], xa[:, :],
                           reads=(xakey,), writes=("x_sp",))
                elif c == 32:
                    em.op("dve", lambda v, xa=xa: v.tensor_copy(b_res[:, :], xa[:, :]),
                          reads=(xakey,), writes=("b_res",))
                else:
                    em.op("dve", lambda v, xa=xa: v.tensor_copy(c_res[:, :], xa[:, :]),
                          reads=(xakey,), writes=("c_res",))

            while stat_s < len(stat_jobs):
                while stat_d < min(stat_s + 4, len(stat_jobs)):
                    stat_dma(stat_d)
                    stat_d += 1
                stat_sq(stat_s)
                stat_s += 1
            em.op("dve", lambda v: v.reduce_sum(small[:, 14:15], statacc[:, 0:128], axis=AX.X),
                  reads=("statacc",), writes=("small",))
            em.op("dve", lambda v: v.reduce_sum(small[:, 15:16], statacc[:, 128:256], axis=AX.X),
                  reads=("statacc",), writes=("small",))
            em.op("pe", lambda t: t.matmul(p_st[0:1, 0:1], small[:, 14:15],
                                           small[:, 4:5], start=True, stop=True),
                  reads=("small", "p_st"), writes=("p_st",))
            em.op("pe", lambda t: t.matmul(p_st[0:1, 1:2], small[:, 15:16],
                                           small[:, 4:5], start=False, stop=True),
                  reads=("small", "p_st"), writes=("p_st",))
            NW = float(2048 * 8192)
            em.op("dve", lambda v: v.tensor_scalar(rmsrow[0:1, 0:2], p_st[0:1, 0:2],
                                                   1.0 / NW, EPS, op0=OP.mult, op1=OP.max),
                  reads=("p_st",), writes=("rmsrow",))
            em.op("act", lambda s: s.activation(rmsrow[0:1, 0:2], rmsrow[0:1, 0:2], AF.Ln),
                  reads=("rmsrow",), writes=("rmsrow",))
            em.op("dve", lambda v: v.tensor_scalar_mul(rmsrow[0:1, 0:2], rmsrow[0:1, 0:2], -0.5),
                  reads=("rmsrow",), writes=("rmsrow",))
            em.op("act", lambda s: s.activation(rmsrow[0:1, 0:2], rmsrow[0:1, 0:2], AF.Exp),
                  reads=("rmsrow",), writes=("rmsrow",))
            em.op("dve", lambda v: v.tensor_scalar_mul(rmsrow[0:1, 2:4], rmsrow[0:1, 0:2], 254.0),
                  reads=("rmsrow",), writes=("rmsrow",))
            em.op("dve", lambda v: v.reciprocal(rmsrow[0:1, 2:4], rmsrow[0:1, 2:4]),
                  reads=("rmsrow",), writes=("rmsrow",))
            em.op("pe", lambda t: t.matmul(p_st[:, 4:8], ones32[0:1, :],
                                           rmsrow[0:1, 0:4], start=True, stop=True),
                  reads=("rmsrow", "ones32", "p_st"), writes=("p_st2",))
            em.op("dve", lambda v: v.tensor_copy(small[:, 10:12], p_st[:, 4:6]),
                  reads=("p_st2",), writes=("small",))
            em.op("dve", lambda v: v.tensor_copy(small[:, 12:14], p_st[:, 6:8]),
                  reads=("p_st2",), writes=("small",))

            wzb = [scA.enter_context(nc.sbuf_tensor(f"wzb{i}", [128, 16 * 512], BF16)) for i in range(2)]
            zst = [scA.enter_context(nc.sbuf_tensor(f"zst{i}", [128, 512], BF16)) for i in range(2)]
            for zc in range(8):
                wz, wzk = wzb[zc % 2], f"wzb{zc % 2}"
                for part in range(4):
                    em.dma("qpool",
                           wz.ap().rearrange("p (k n) -> p k n", k=16)[:, 4 * part:4 * (part + 1), :],
                           w_in_z.rearrange("(k p) n -> p k n", p=128)
                           [:, 4 * part:4 * (part + 1), 512 * zc:512 * (zc + 1)],
                           writes=(wzk,))
                for m in range(16):
                    pb, pkey = p2p[m % 2], f"p2p{m % 2}"

                    def mmz(t, wz=wz, m=m, pb=pb):
                        for k in range(16):
                            o = k * T_ALL + HALO + T_FOR + 128 * m
                            ins = t.matmul(pb[:, :], utb[:, o:o + 128],
                                           wz[:, 512 * k:512 * (k + 1)],
                                           start=(k == 0), stop=(k == 15))
                        return ins
                    em.op("pe", mmz, reads=(wzk, "utb", pkey), writes=(pkey,))
                    zb, zkey = zst[m % 2], f"zst{m % 2}"
                    em.op("act", lambda s, pb=pb, zb=zb: s.activation(zb[:, :], pb[:, :], AF.Silu),
                          reads=(pkey,), writes=(zkey,))
                    em.dma("qact", zsil_sp[zc, 128 * m:128 * (m + 1), :], zb[:, :],
                           reads=(zkey,), writes=("zsil_sp",))

        em.barrier()

        with ExitStack() as sc3:
            s_st = sc3.enter_context(nc.sbuf_tensor("s_st", [128, 4096], F32))
            em.op("dve", lambda v: v.memset(s_st[:, :], 0.0), writes=("s_st",))
            xtok = [sc3.enter_context(nc.sbuf_tensor(f"xtok{i}", [128, 4096], BF16)) for i in range(2)]
            xch = [sc3.enter_context(nc.sbuf_tensor("xch0", [128, 4096], BF16))]
            ybuf = [sc3.enter_context(nc.sbuf_tensor("ybuf0", [128, 4096], BF16))]
            sbf = sc3.enter_context(nc.sbuf_tensor("sbf", [128, 4096], BF16))
            xd = sc3.enter_context(nc.sbuf_tensor("xd", [128, 4096], BF16))
            xw = sc3.enter_context(nc.sbuf_tensor("xw", [128, 4096], BF16))
            mpp = sc3.enter_context(nc.sbuf_tensor("mpp", [128, 8192], BF16))
            dtap = sc3.enter_context(nc.sbuf_tensor("dtap", [128, 8192], BF16))
            decb = [sc3.enter_context(nc.sbuf_tensor(f"decb{i}", [128, 512], BF16)) for i in range(2)]
            cbsb = sc3.enter_context(nc.sbuf_tensor("cbsb", [128, 128], BF16))
            btok = sc3.enter_context(nc.sbuf_tensor("btok", [128, 128], BF16))
            dch = sc3.enter_context(nc.sbuf_tensor("dch", [64, 128 * 8], F32))
            dchb = sc3.enter_context(nc.sbuf_tensor("dchb", [64, 128], BF16))
            dtok = sc3.enter_context(nc.sbuf_tensor("dtok", [128, 64 * 4], F32))
            dtokb = sc3.enter_context(nc.sbuf_tensor("dtokb", [128, 128], BF16))
            arow = sc3.enter_context(nc.sbuf_tensor("arow", [1, 64], F32))
            evtmp = sc3.enter_context(nc.sbuf_tensor("evtmp", [128, 512], F32))
            wrep = sc3.enter_context(nc.sbuf_tensor("wrep", [128, 4096], BF16))
            arep = sc3.enter_context(nc.sbuf_tensor("arep", [128, 4096], BF16))
            erep = sc3.enter_context(nc.sbuf_tensor("erep", [128, 4096], BF16))
            sgbuf = [sc3.enter_context(nc.sbuf_tensor(f"sgbuf{i}", [128, 2048], F32)) for i in range(2)]
            sgout = sc3.enter_context(nc.sbuf_tensor("sgout", [128, 2048], BF16))
            sg2 = sc3.enter_context(nc.sbuf_tensor("sg2", [128, 2048], BF16))
            sgq = sc3.enter_context(nc.sbuf_tensor("sgq", [128, 2048], FP8))

            p_dec = [sc3.enter_context(nc.psum_tensor(f"p_dec{i}", [128, 512], F32)) for i in range(2)]
            p_y1 = [sc3.enter_context(nc.psum_tensor(f"p_y1{i}", [128, 512], F32)) for i in range(2)]
            p_y2 = [sc3.enter_context(nc.psum_tensor(f"p_y2{i}", [128, 512], F32)) for i in range(2)]
            p_s3 = sc3.enter_context(nc.psum_tensor("p_s3", [128, 512], F32))
            p_aux = sc3.enter_context(nc.psum_tensor("p_aux", [128, 128], F32))
            p_auxb = p_aux.ap().bitcast(BF16)

            em.op("dve", lambda v: v.memset(dch[:, 128 * 7:128 * 8], 0.0), writes=("dchz",))
            DTF, DTA, LND, DLS, WDT, CUM, SCR, DZ = [
                dch[0:64, 128 * i:128 * (i + 1)] for i in range(8)]

            sign_jobs = []
            for fc in range(16):
                for qf in range(4):
                    k0 = 4 * qf
                    src = w1t.rearrange("(k p) f -> p k f", p=128)[:, k0:k0 + 4,
                                                                  512 * fc:512 * (fc + 1)]
                    sign_jobs.append((src, w1q_sp[fc, :, 2048 * qf:2048 * (qf + 1)], 10))
            for d2c in range(4):
                for ks in range(4):
                    for qf in range(4):
                        k0 = 16 * ks + 4 * qf
                        src = w2t.rearrange("(k p) f -> p k f", p=128)[:, k0:k0 + 4,
                                                                      512 * d2c:512 * (d2c + 1)]
                        sign_jobs.append((src, w2q_sp[d2c, 16 * ks + 4 * qf:16 * ks + 4 * (qf + 1), :, :]
                                          .rearrange("k p f -> p k f"), 11))

            def sign_dma(idx):
                src, _, _ = sign_jobs[idx]
                em.dma("qpool", sgbuf[idx % 2].ap().rearrange("p (k f) -> p k f", k=4),
                       src, writes=(f"sgbuf{idx % 2}",))

            def sign_emit(idx):
                _, dst_ap, rms_col = sign_jobs[idx]
                sb, sk = sgbuf[idx % 2], f"sgbuf{idx % 2}"
                em.op("act", lambda s, sb=sb, rms_col=rms_col: s.activation(
                    sgout[:, :], sb[:, :], AF.Sign, bias=small[:, 9:10],
                    scale=small[:, rms_col:rms_col + 1]),
                    reads=(sk, "small"), writes=("sgout",))
                em.op("act", lambda s, sb=sb, rms_col=rms_col: s.activation(
                    sg2[:, :], sb[:, :], AF.Sign, bias=small[:, 5:6],
                    scale=small[:, rms_col:rms_col + 1]),
                    reads=(sk, "small"), writes=("sg2",))
                em.op("dve", lambda v: v.tensor_tensor(sgq[:, :], sgout[:, :], sg2[:, :],
                                                       op=OP.add),
                      reads=("sgout", "sg2"), writes=("sgq",))
                em.dma("qpool", dst_ap,
                       sgq.ap().rearrange("p (k f) -> p k f", k=4) if dst_ap.ndim == 3
                       else sgq[:, :],
                       reads=("sgq",), writes=("wq_sp",))

            sign_d = 0
            sign_i = 0

            for c in range(NCH):
                own = c >= KFOR
                xt, xk = xtok[c % 2], f"xtok{c % 2}"
                xc, xck = xch[0], "xch0"
                em.dma("qsp", xc.ap().rearrange("p (b t) -> p b t", b=32),
                       x_sp.rearrange("(b p) t -> p b t", p=128)[:, :, 128 * c:128 * (c + 1)],
                       reads=("x_sp",), writes=(xck,))
                for g in range(8):
                    pd, pdk = p_dec[g % 2], f"p_dec{g % 2}"

                    def xtg(t, g=g, xc=xc, pd=pd):
                        pb = pd.ap().bitcast(BF16)
                        for qq in range(4):
                            j = 4 * g + qq
                            ins = t.transpose(pb[:, 128 * qq:128 * (qq + 1)],
                                              xc[:, 128 * j:128 * (j + 1)], ident[:, :])
                        return ins
                    em.op("pe", xtg, reads=(xck, "ident", pdk), writes=(pdk,))
                    em.op("act", lambda s, g=g, pd=pd: s.copy(
                        xt[:, 512 * g:512 * (g + 1)], pd.ap().bitcast(BF16)[:, 0:512]),
                        reads=(pdk,), writes=(xk,))
                em.op("pe", lambda t, c=c: t.transpose(p_auxb[:, 0:128],
                                                       b_res[:, 128 * c:128 * (c + 1)], ident[:, :]),
                      reads=("b_res", "ident"), writes=("p_aux",))
                em.op("dve", lambda v: v.tensor_copy(btok[:, :], p_auxb[:, 0:128]),
                      reads=("p_aux",), writes=("btok",))
                em.op("dve", lambda v, c=c: v.tensor_scalar_add(
                    SCR, dt_res[0:64, 128 * c:128 * (c + 1)], small[0:64, 0:1]),
                    reads=("dt_res", "small"), writes=("dscr",))
                em.op("act", lambda s: s.activation(DTF, SCR, AF.Abs), reads=("dscr",), writes=("dtf",))
                em.op("act", lambda s: s.activation(DTF, DTF, AF.Exp, scale=-1.0),
                      reads=("dtf",), writes=("dtf",))
                em.op("act", lambda s: s.activation(DTF, DTF, AF.Ln, bias=small[0:64, 4:5], scale=1.0),
                      reads=("dtf", "small"), writes=("dtf",))
                em.op("dve", lambda v: v.tensor_scalar_max(SCR, SCR, 0.0), reads=("dscr",), writes=("dscr",))
                em.op("dve", lambda v: v.tensor_tensor(DTF, DTF, SCR, op=OP.add),
                      reads=("dtf", "dscr"), writes=("dtf",))
                em.op("dve", lambda v: v.tensor_scalar_mul(DTA, DTF, small[0:64, 3:4]),
                      reads=("dtf", "small"), writes=("dta",))
                em.op("dve", lambda v: v.tensor_tensor_scan(CUM, DTA, DZ, 0.0, op0=OP.add, op1=OP.add),
                      reads=("dta", "dchz"), writes=("cum",))
                em.op("dve", lambda v: v.tensor_scalar_max(SCR, DTF, 1e-38),
                      reads=("dtf",), writes=("dscr",))
                em.op("act", lambda s: s.activation(LND, SCR, AF.Ln), reads=("dscr",), writes=("lnd",))
                em.op("dve", lambda v: v.tensor_scalar_max(LND, LND, -50.0), reads=("lnd",), writes=("lnd",))
                em.op("dve", lambda v: v.tensor_scalar(DLS, CUM, dch[0:64, 128 * 5 + 127:128 * 5 + 128],
                                                       None, op0=OP.subtract),
                      reads=("cum",), writes=("dls",))
                em.op("act", lambda s: s.activation(DLS, DLS, AF.Exp, scale=-1.0),
                      reads=("dls",), writes=("dls",))
                em.op("dve", lambda v: v.tensor_tensor(WDT, DLS, DTF, op=OP.mult),
                      reads=("dls", "dtf"), writes=("wdt",))
                em.op("act", lambda s: s.activation(SCR[:, 0:1], dch[0:64, 128 * 5 + 127:128 * 5 + 128],
                                                    AF.Exp),
                      reads=("cum",), writes=("dscr",))
                em.op("pe", lambda t: t.transpose(p_aux[0:1, 0:64], SCR[:, 0:1], ident32[0:64, 0:64]),
                      reads=("dscr", "ident32"), writes=("p_aux",))
                em.op("act", lambda s: s.copy(arow[0:1, :], p_aux[0:1, 0:64]),
                      reads=("p_aux",), writes=("arow",))
                em.op("pe", lambda t: t.matmul(p_aux[:, 0:64], ones32[0:1, :], arow[0:1, :],
                                               start=True, stop=True),
                      reads=("ones32", "arow", "p_aux"), writes=("p_aux",))
                em.op("dve", lambda v: v.tensor_copy(dtok[:, 128:192], p_aux[:, 0:64]),
                      reads=("p_aux",), writes=("dtok2",))
                em.op("pe", lambda t: t.transpose(p_aux[:, 0:64], CUM, ident32[0:64, 0:64]),
                      reads=("cum", "ident32"), writes=("p_aux",))
                em.op("dve", lambda v: v.tensor_copy(dtok[:, 0:64], p_aux[:, 0:64]),
                      reads=("p_aux",), writes=("dtok0",))
                em.op("pe", lambda t: t.transpose(p_aux[:, 0:64], WDT, ident32[0:64, 0:64]),
                      reads=("wdt", "ident32"), writes=("p_aux",))
                em.op("dve", lambda v: v.tensor_copy(dtok[:, 192:256], p_aux[:, 0:64]),
                      reads=("p_aux",), writes=("dtok3",))
                em.op("act", lambda s: s.activation(dtok[:, 64:128], dtok[:, 0:64], AF.Exp),
                      reads=("dtok0",), writes=("dtok1",))
                em.op("dve", lambda v: v.tensor_copy(dchb[0:64, :], DTA), reads=("dta",), writes=("dchb",))
                em.op("pe", lambda t: t.transpose(p_auxb[:, 0:64], dchb[0:64, :], ident[0:64, 0:64]),
                      reads=("dchb", "ident"), writes=("p_aux",))
                em.op("dve", lambda v: v.tensor_copy(dtokb[:, 0:64], p_auxb[:, 0:64]),
                      reads=("p_aux",), writes=("dtokb0",))
                em.op("dve", lambda v: v.tensor_copy(dchb[0:64, :], LND), reads=("lnd",), writes=("dchb",))
                em.op("pe", lambda t: t.transpose(p_auxb[:, 0:64], dchb[0:64, :], ident[0:64, 0:64]),
                      reads=("dchb", "ident"), writes=("p_aux",))
                em.op("dve", lambda v: v.tensor_copy(dtokb[:, 64:128], p_auxb[:, 0:64]),
                      reads=("p_aux",), writes=("dtokb1",))
                em.op("dve", lambda v: v.tensor_copy(
                    wrep.ap().rearrange("p (h q) -> p h q", h=64),
                    dtok[:, 192:256].unsqueeze(2).broadcast_to([128, 64, 64])),
                    reads=("dtok3",), writes=("wrep",))
                em.op("dve", lambda v: v.tensor_copy(
                    arep.ap().rearrange("p (h q) -> p h q", h=64),
                    dtok[:, 128:192].unsqueeze(2).broadcast_to([128, 64, 64])),
                    reads=("dtok2",), writes=("arep",))
                em.op("dve", lambda v: v.tensor_tensor(xw[:, :], xt[:, :], wrep[:, :],
                                                       op=OP.mult),
                    reads=(xk, "wrep"), writes=("xw",))
                if debug and c == KFOR:
                    em.dma("qpool", dbg_d[:, 0:2048], wrep[:, 0:2048], reads=("wrep",), writes=("dbg_d",))
                    em.dma("qpool", dbg_d[:, 2048:4096], xw[:, 0:2048], reads=("xw",), writes=("dbg_d",))
                if own:
                    em.op("dve", lambda v: v.tensor_copy(
                        erep.ap().rearrange("p (h q) -> p h q", h=64),
                        dtok[:, 64:128].unsqueeze(2).broadcast_to([128, 64, 64])),
                        reads=("dtok1",), writes=("erep",))
                    em.op("dve", lambda v: v.tensor_copy(sbf[:, :], s_st[:, :]),
                          reads=("s_st",), writes=("sbf",))
                    em.op("pe", lambda t, c=c: t.matmul(p_aux[:, :], b_res[:, 128 * c:128 * (c + 1)],
                                                        c_res[:, 128 * c:128 * (c + 1)],
                                                        start=True, stop=True),
                          reads=("b_res", "c_res", "p_aux"), writes=("p_aux",))
                    em.op("act", lambda s: s.copy(cbsb[:, :], p_aux[:, :]),
                          reads=("p_aux",), writes=("cbsb",))
                    em.op("dve", lambda v: v.tensor_tensor(xd[:, :], xt[:, :], dbcr[:, :],
                                                           op=OP.mult),
                        reads=(xk, "dbcr"), writes=("xd",))
                    em.op("dve", lambda v: v.tensor_tensor(
                        dtap.ap().rearrange("p (h i) -> p h i", h=64),
                        dtokb[:, 0:64].unsqueeze(2).broadcast_to([128, 64, 128]),
                        trib.ap().unsqueeze(1).broadcast_to([128, 64, 128]), op=OP.mult),
                        reads=("dtokb0", "trib"), writes=("dtap",))
                    for g4 in range(16):
                        pd, pdk = p_dec[g4 % 2], f"p_dec{g4 % 2}"

                        def decmm(t, g4=g4, pd=pd):
                            t.matmul(pd[:, :], onesb[:, :], dtap[:, 512 * g4:512 * (g4 + 1)],
                                     start=True, stop=False)
                            t.matmul(pd[:, :], ntrilb[:, :],
                                     dtokb[:, 4 * g4:4 * g4 + 4].unsqueeze(2)
                                     .broadcast_to([128, 4, 128]), start=False, stop=False)
                            t.matmul(pd[:, :], ident[:, :],
                                     dtokb[:, 64 + 4 * g4:64 + 4 * g4 + 4].unsqueeze(2)
                                     .broadcast_to([128, 4, 128]), start=False, stop=False)
                            return t.matmul(pd[:, :], nbigb[:, :],
                                            ident.ap().unsqueeze(1).broadcast_to([128, 4, 128]),
                                            start=False, stop=True)
                        em.op("pe", decmm, reads=("onesb", "dtap", "ntrilb", "dtokb0",
                                                  "dtokb1", "ident", "nbigb", pdk), writes=(pdk,))
                        db, dbk = decb[g4 % 2], f"decb{g4 % 2}"
                        em.op("act", lambda s, pd=pd, db=db: s.activation(db[:, :], pd[:, :], AF.Exp),
                              reads=(pdk,), writes=(dbk,))
                        em.op("dve", lambda v, db=db, g4=g4: v.tensor_tensor(
                            mpp[:, 512 * g4:512 * (g4 + 1)].rearrange("p (h i) -> p h i", h=4),
                            db.ap().rearrange("p (h i) -> p h i", h=4),
                            cbsb.ap().unsqueeze(1).broadcast_to([128, 4, 128]), op=OP.mult),
                            reads=(dbk, "cbsb"), writes=("mpp",))
                for g8 in range(8):
                    if own:
                        py1, k1 = p_y1[g8 % 2], f"p_y1{g8 % 2}"
                        py2, k2 = p_y2[g8 % 2], f"p_y2{g8 % 2}"

                        def ymm(t, g8=g8, py1=py1):
                            t.matmul(py1[:, :], ident[:, :], xd[:, 512 * g8:512 * (g8 + 1)],
                                     start=True, stop=False)
                            for hl in range(8):
                                h = 8 * g8 + hl
                                ins = t.matmul(py1[:, 64 * hl:64 * (hl + 1)],
                                               mpp[:, 128 * h:128 * (h + 1)],
                                               xt[:, 64 * h:64 * (h + 1)],
                                               start=False, stop=(hl == 7))
                            return ins
                        em.op("pe", ymm, reads=("ident", "xd", "mpp", xk, k1), writes=(k1,))
                        em.op("pe", lambda t, py2=py2, g8=g8, c=c: t.matmul(
                            py2[:, :], c_res[:, 128 * c:128 * (c + 1)],
                            sbf[:, 512 * g8:512 * (g8 + 1)], start=True, stop=True),
                            reads=("c_res", "sbf", k2), writes=(k2,))
                        yb, ybk = ybuf[0], "ybuf0"
                        em.op("dve", lambda v, py2=py2, g8=g8: v.tensor_tensor(
                            evtmp[:, :], py2[:, :], erep[:, 512 * g8:512 * (g8 + 1)],
                            op=OP.mult),
                            reads=(k2, "erep"), writes=("evtmp",))
                        em.op("dve", lambda v, py1=py1, yb=yb, g8=g8: v.tensor_tensor(
                            yb[:, 512 * g8:512 * (g8 + 1)], evtmp[:, :], py1[:, :], op=OP.add),
                            reads=("evtmp", k1), writes=(ybk,))
                    em.op("pe", lambda t, g8=g8: t.matmul(
                        p_s3[:, :], btok[:, :], xw[:, 512 * g8:512 * (g8 + 1)],
                        start=True, stop=True),
                        reads=("btok", "xw", "p_s3"), writes=("p_s3",))
                    em.op("dve", lambda v, g8=g8: v.tensor_tensor(
                        s_st[:, 512 * g8:512 * (g8 + 1)], s_st[:, 512 * g8:512 * (g8 + 1)],
                        arep[:, 512 * g8:512 * (g8 + 1)], op=OP.mult),
                        reads=("arep", "s_st"), writes=("s_st",))
                    em.op("dve", lambda v, g8=g8: v.tensor_tensor(
                        s_st[:, 512 * g8:512 * (g8 + 1)], s_st[:, 512 * g8:512 * (g8 + 1)],
                        p_s3[:, :], op=OP.add),
                        reads=("p_s3", "s_st"), writes=("s_st",))
                if own:
                    em.dma("qact", y_sp[128 * (c - KFOR):128 * (c - KFOR + 1), :],
                           ybuf[0][:, :], reads=("ybuf0",), writes=("y_sp",))
                if debug and c == KFOR:
                    em.op("dve", lambda v: v.tensor_copy(evtmp[:, 0:64], dbc[:, :]),
                          reads=("dbc",), writes=("evtmp",))
                    em.op("dve", lambda v: v.tensor_copy(
                        evtmp.ap()[:, 64:256].rearrange("p (h q) -> p h q", q=3),
                        dtok[:, 192:256].unsqueeze(2).broadcast_to([128, 64, 3])),
                        reads=("dtok3",), writes=("evtmp",))
                    em.dma("qsp", dbg_a[:, 256:512], evtmp[:, 0:256], reads=("evtmp",), writes=("dbg_a",))
                    em.dma("qsp", dbg_a[:, 0:256], dtok[:, :],
                           reads=("dtok0", "dtok1", "dtok2", "dtok3"), writes=("dbg_a",))
                    em.dma("qsp", dbg_a[0:64, 1024:2048], dch[0:64, :],
                           reads=("dtf", "dta", "lnd", "dls", "wdt", "cum"), writes=("dbg_a",))
                    em.dma("qsp", dbg_b[:, 0:4096], xtok[c % 2][:, :], reads=(f"xtok{c % 2}",), writes=("dbg_b",))
                    em.dma("qsp", dbg_b[:, 4096:8192], xd[:, :], reads=("xd",), writes=("dbg_b",))
                    em.dma("qsp", dbg_c[:, 0:4096], xw[:, :], reads=("xw",), writes=("dbg_c",))
                    em.dma("qsp", dbg_c[:, 4096:4224], cbsb[:, :], reads=("cbsb",), writes=("dbg_c",))
                    em.dma("qsp", dbg_c[:, 4224:4352], btok[:, :], reads=("btok",), writes=("dbg_c",))
                    em.dma("qsp", dbg_c[:, 4352:4480], dtokb[:, :], reads=("dtokb0", "dtokb1"), writes=("dbg_c",))

                if c == KFOR - 1:
                    em.op("dve", lambda v: v.tensor_scalar_mul(s_st[:, :], s_st[:, :],
                                                               small[:, 8:9]),
                          reads=("small", "s_st"), writes=("s_st",))
                want_s = (c + 1) * len(sign_jobs) // NCH
                while sign_i < want_s:
                    while sign_d < min(sign_i + 2, len(sign_jobs)):
                        sign_dma(sign_d)
                        sign_d += 1
                    sign_emit(sign_i)
                    sign_i += 1

            for part in range(8):
                em.dma("qpool", woutb_sp[512 * part:512 * (part + 1), :],
                       w_out[512 * part:512 * (part + 1), :], writes=("woutb_sp",))
            while sign_i < len(sign_jobs):
                while sign_d < min(sign_i + 2, len(sign_jobs)):
                    sign_dma(sign_d)
                    sign_d += 1
                sign_emit(sign_i)
                sign_i += 1

        scAP3.close()
        em.barrier()

        with ExitStack() as sc4a:
            nwb = sc4a.enter_context(nc.sbuf_tensor("nwb", [128, 4096], BF16))
            nwrow = sc4a.enter_context(nc.sbuf_tensor("nwrow", [1, 512], F32))
            p_nw = sc4a.enter_context(nc.psum_tensor("p_nw", [128, 512], F32))
            for j in range(8):
                em.dma("qsp", nwrow[0:1, :], norm_w[512 * j:512 * (j + 1)].unsqueeze(0),
                       writes=("nwrow",))
                em.op("pe", lambda t, j=j: t.matmul(p_nw[:, :], ones32[0:1, :],
                                                    nwrow[0:1, :], start=True, stop=True),
                      reads=("ones32", "nwrow", "p_nw"), writes=("p_nw",))
                em.op("dve", lambda v, j=j: v.tensor_copy(nwb[:, 512 * j:512 * (j + 1)],
                                                          p_nw[:, :]),
                      reads=("p_nw",), writes=("nwb",))
            yb4 = [sc4a.enter_context(nc.sbuf_tensor(f"y4_{i}", [128, 4096], BF16)) for i in range(2)]
            zb4 = [sc4a.enter_context(nc.sbuf_tensor(f"z4_{i}", [128, 4096], BF16)) for i in range(2)]
            tb4 = [sc4a.enter_context(nc.sbuf_tensor(f"tb_{i}", [128, 4096], BF16)) for i in range(2)]
            tsq4 = [sc4a.enter_context(nc.sbuf_tensor(f"tsq_{i}", [128, 4096], BF16)) for i in range(2)]
            tn4 = [sc4a.enter_context(nc.sbuf_tensor(f"tn_{i}", [128, 4096], BF16)) for i in range(2)]
            cla = sc4a.enter_context(nc.sbuf_tensor("cla", [128, 8], F32))
            for m in range(16):
                i2 = m % 2
                yb, zb, tb, tsq, tn = yb4[i2], zb4[i2], tb4[i2], tsq4[i2], tn4[i2]
                yk, zk, tbk, tqk, tnk = (f"y4_{i2}", f"z4_{i2}", f"tb_{i2}", f"tsq_{i2}", f"tn_{i2}")
                c0 = 4 * i2
                ck = f"cla{i2}"
                em.dma("qsp", yb[:, :], y_sp[128 * m:128 * (m + 1), :],
                       reads=("y_sp",), writes=(yk,))
                for zc in range(8):
                    em.dma("qsp", zb[:, 512 * zc:512 * (zc + 1)],
                           zsil_sp[zc, 128 * m:128 * (m + 1), :],
                           reads=("zsil_sp",), writes=(zk,))
                em.op("dve", lambda v, yb=yb, zb=zb, tb=tb: v.tensor_tensor(
                    tb[:, :], yb[:, :], zb[:, :], op=OP.mult),
                    reads=(yk, zk), writes=(tbk,))
                em.op("act", lambda s, tb=tb, tsq=tsq, c0=c0: s.activation(
                    tsq[:, :], tb[:, :], AF.Square, accum_out=cla[:, c0:c0 + 1]),
                    reads=(tbk,), writes=(tqk, ck))
                em.op("act", lambda s, c0=c0: s.activation(cla[:, c0 + 1:c0 + 2],
                                                           cla[:, c0:c0 + 1], AF.Ln,
                                                           bias=small[:, 7:8],
                                                           scale=1.0 / D_INNER),
                      reads=(ck, "small"), writes=(ck,))
                em.op("dve", lambda v, c0=c0: v.tensor_scalar_mul(cla[:, c0 + 1:c0 + 2],
                                                                  cla[:, c0 + 1:c0 + 2], -0.5),
                      reads=(ck,), writes=(ck,))
                em.op("act", lambda s, c0=c0: s.activation(cla[:, c0 + 2:c0 + 3],
                                                           cla[:, c0 + 1:c0 + 2], AF.Exp),
                      reads=(ck,), writes=(ck,))
                em.op("dve", lambda v, tb=tb, tn=tn, c0=c0: v.scalar_tensor_tensor(
                    tn[:, :], tb[:, :], cla[:, c0 + 2:c0 + 3], nwb[:, :],
                    op0=OP.mult, op1=OP.mult),
                    reads=(tbk, ck, "nwb"), writes=(tnk,))
                em.dma("qact", tn_sp[128 * m:128 * (m + 1), :], tn[:, :],
                       reads=(tnk,), writes=("tn_sp",))

        em.barrier()

        with ExitStack() as sc4:
            woutb = sc4.enter_context(nc.sbuf_tensor("woutb", [128, 32 * 2048], BF16))
            for part in range(8):
                em.dma("qsp",
                       woutb.ap().rearrange("p (k n) -> p k n", k=32)[:, 4 * part:4 * (part + 1), :],
                       woutb_sp.rearrange("(k p) n -> p k n", p=128)[:, 4 * part:4 * (part + 1), :],
                       reads=("woutb_sp",), writes=("woutb",))
            tnl = [sc4.enter_context(nc.sbuf_tensor(f"tnl{i}", [128, 4096], BF16)) for i in range(2)]
            ub4 = [sc4.enter_context(nc.sbuf_tensor(f"u4_{i}", [128, 2048], BF16)) for i in range(2)]
            tT4 = [sc4.enter_context(nc.sbuf_tensor(f"tT{i}", [128, 4096], BF16)) for i in range(2)]
            x1p4 = [sc4.enter_context(nc.sbuf_tensor(f"x1p{i}", [128, 2048], F32)) for i in range(2)]
            sqb = [sc4.enter_context(nc.sbuf_tensor(f"sqb{i}", [128, 2048], BF16)) for i in range(2)]
            x1b4 = [sc4.enter_context(nc.sbuf_tensor(f"x1b{i}", [128, 2048], BF16)) for i in range(2)]
            clb = sc4.enter_context(nc.sbuf_tensor("clb", [128, 12], F32))
            p_w = [sc4.enter_context(nc.psum_tensor(f"p_w{i}", [128, 512], F32)) for i in range(2)]
            p_tr32 = [sc4.enter_context(nc.psum_tensor(f"p_tr{i}", [128, 512], F32)) for i in range(2)]
            p_tr = [p.ap().bitcast(BF16) for p in p_tr32]

            for m in range(16):
                i2 = m % 2
                tn, ub, tT, x1pre, x1b, sq = tnl[i2], ub4[i2], tT4[i2], x1p4[i2], x1b4[i2], sqb[i2]
                tnk, uk, tTk, xpk, xbk, sqk = (f"tnl{i2}", f"u4_{i2}", f"tT{i2}",
                                               f"x1p{i2}", f"x1b{i2}", f"sqb{i2}")
                c0 = 6 * i2
                ck = f"clb{i2}"
                em.dma("qsp", tn[:, :], tn_sp[128 * m:128 * (m + 1), :],
                       reads=("tn_sp",), writes=(tnk,))
                em.dma("qpool", ub[:, :], u_own[128 * m:128 * (m + 1), :], writes=(uk,))
                for tt in range(8):
                    ptr, pk = p_tr[tt % 2], f"p_tr{tt % 2}"

                    def trg(t, tt=tt, ptr=ptr, tn=tn):
                        for qq in range(4):
                            j = 4 * tt + qq
                            ins = t.transpose(ptr[:, 128 * qq:128 * (qq + 1)],
                                              tn[:, 128 * j:128 * (j + 1)], ident[:, :])
                        return ins
                    em.op("pe", trg, reads=(tnk, "ident", pk), writes=(pk,))
                    em.op("act", lambda s, tt=tt, ptr=ptr, tT=tT: s.copy(
                        tT[:, 512 * tt:512 * (tt + 1)], ptr[:, 0:512]),
                        reads=(pk,), writes=(tTk,))
                for nchk in range(4):
                    pw, pk = p_w[nchk % 2], f"p_w{nchk % 2}"

                    def wmm(t, nchk=nchk, pw=pw, tT=tT):
                        for k in range(32):
                            ins = t.matmul(pw[:, :], tT[:, 128 * k:128 * (k + 1)],
                                           woutb[:, 2048 * k + 512 * nchk:2048 * k + 512 * (nchk + 1)],
                                           start=(k == 0), stop=(k == 31))
                        return ins
                    em.op("pe", wmm, reads=(tTk, "woutb", pk), writes=(pk,))
                    em.op("dve", lambda v, nchk=nchk, pw=pw, ub=ub, x1pre=x1pre: v.tensor_tensor(
                        x1pre[:, 512 * nchk:512 * (nchk + 1)], pw[:, :],
                        ub[:, 512 * nchk:512 * (nchk + 1)], op=OP.add),
                        reads=(pk, uk), writes=(xpk,))
                em.op("dve", lambda v, x1pre=x1pre, c0=c0: v.tensor_reduce(
                    clb[:, c0:c0 + 1], x1pre[:, :], axis=AX.X, op=OP.add),
                    reads=(xpk,), writes=(ck,))
                em.op("dve", lambda v, c0=c0: v.tensor_scalar_mul(clb[:, c0:c0 + 1],
                                                                  clb[:, c0:c0 + 1], 1.0 / D_MODEL),
                      reads=(ck,), writes=(ck,))
                em.op("act", lambda s, x1pre=x1pre, sq=sq, c0=c0: s.activation(
                    sq[:, :], x1pre[:, :], AF.Square, accum_out=clb[:, c0 + 1:c0 + 2]),
                    reads=(xpk,), writes=(sqk, ck))
                em.op("dve", lambda v, c0=c0: v.tensor_scalar_mul(clb[:, c0 + 1:c0 + 2],
                                                                  clb[:, c0 + 1:c0 + 2], 1.0 / D_MODEL),
                      reads=(ck,), writes=(ck,))
                em.op("dve", lambda v, c0=c0: v.scalar_tensor_tensor(
                    clb[:, c0 + 2:c0 + 3], clb[:, c0:c0 + 1], clb[:, c0:c0 + 1],
                    clb[:, c0 + 1:c0 + 2], op0=OP.mult, op1=OP.subtract),
                    reads=(ck,), writes=(ck,))
                em.op("dve", lambda v, c0=c0: v.tensor_scalar_mul(clb[:, c0 + 2:c0 + 3],
                                                                  clb[:, c0 + 2:c0 + 3], -1.0),
                      reads=(ck,), writes=(ck,))
                em.op("act", lambda s, c0=c0: s.activation(clb[:, c0 + 3:c0 + 4],
                                                           clb[:, c0 + 2:c0 + 3], AF.Ln,
                                                           bias=small[:, 7:8], scale=1.0),
                      reads=(ck, "small"), writes=(ck,))
                em.op("dve", lambda v, c0=c0: v.tensor_scalar_mul(clb[:, c0 + 3:c0 + 4],
                                                                  clb[:, c0 + 3:c0 + 4], -0.5),
                      reads=(ck,), writes=(ck,))
                em.op("act", lambda s, c0=c0: s.activation(clb[:, c0 + 3:c0 + 4],
                                                           clb[:, c0 + 3:c0 + 4], AF.Exp),
                      reads=(ck,), writes=(ck,))
                em.op("dve", lambda v, c0=c0: v.scalar_tensor_tensor(
                    clb[:, c0 + 4:c0 + 5], clb[:, c0:c0 + 1], -1.0, clb[:, c0 + 3:c0 + 4],
                    op0=OP.mult, op1=OP.mult),
                    reads=(ck,), writes=(ck,))
                em.op("dve", lambda v, x1pre=x1pre, x1b=x1b, c0=c0: v.tensor_scalar(
                    x1b[:, :], x1pre[:, :], clb[:, c0 + 3:c0 + 4], clb[:, c0 + 4:c0 + 5],
                    op0=OP.mult, op1=OP.add),
                    reads=(xpk, ck), writes=(xbk,))
                em.dma("qact", x1_sp[128 * m:128 * (m + 1), :], x1b[:, :], reads=(xbk,),
                       writes=("x1_sp",))

        em.barrier()

        qcols = stack.enter_context(nc.sbuf_tensor("qcols", [128, 64], F32))
        p_f1 = [stack.enter_context(nc.psum_tensor(f"p_f1{i}", [128, 512], F32)) for i in range(4)]
        p_tq32 = [stack.enter_context(nc.psum_tensor(f"p_tq{i}", [128, 512], F32)) for i in range(2)]
        p_tq = [p.ap().bitcast(BF16) for p in p_tq32]

        with ExitStack() as scE1:
            xqT = scE1.enter_context(nc.sbuf_tensor("xqT", [128, 16 * 2048], BF16))
            x1l = [scE1.enter_context(nc.sbuf_tensor(f"x1l{i}", [128, 2048], BF16)) for i in range(2)]
            hgrp = [scE1.enter_context(nc.sbuf_tensor(f"hgrp{i}", [128, 8192], BF16)) for i in range(4)]
            w1qb = [scE1.enter_context(nc.sbuf_tensor("w1qb0", [128, 8192], BF16))]
            y2b = scE1.enter_context(nc.sbuf_tensor("y2b", [128, 2048], F32))
            mfr = scE1.enter_context(nc.sbuf_tensor("mfr", [128, 2048], F32))
            ibuf = scE1.enter_context(nc.sbuf_tensor("ibuf", [128, 2048], mybir.dt.int32))
            qfb = scE1.enter_context(nc.sbuf_tensor("qfb", [128, 2048], F32))
            hqs = scE1.enter_context(nc.sbuf_tensor("hqs", [128, 2048], BF16))


            y2c = y2b.ap().bitcast(BF16)
            mfc = mfr.ap().bitcast(BF16)

            def emit_round(in_ap, in_key, sc_key, sc_ap, dst_ap, dst_key):
                em.op("dve", lambda v: v.tensor_scalar_mul(y2b[:, :], in_ap, sc_ap),
                      reads=(in_key, sc_key), writes=("y2b",))
                em.op("dve", lambda v: v.tensor_copy(ibuf[:, :], y2b[:, :]),
                      reads=("y2b",), writes=("ibuf",))
                em.op("dve", lambda v: v.tensor_copy(qfb[:, :], ibuf[:, :]),
                      reads=("ibuf",), writes=("qfb",))
                em.op("dve", lambda v: v.scalar_tensor_tensor(mfr[:, :], qfb[:, :], -1.0,
                                                              y2b[:, :], op0=OP.mult, op1=OP.add),
                      reads=("qfb", "y2b"), writes=("mfr",))
                em.op("act", lambda s: s.activation(y2c[:, 0:2048], mfr[:, :], AF.Sign,
                                                    bias=small[:, 9:10], scale=1.0),
                      reads=("mfr", "small"), writes=("y2b",))
                em.op("act", lambda s: s.activation(y2c[:, 2048:4096], mfr[:, :], AF.Sign,
                                                    bias=small[:, 5:6], scale=1.0),
                      reads=("mfr", "small"), writes=("y2b",))
                em.op("dve", lambda v: v.tensor_tensor(mfc[:, 0:2048], y2c[:, 0:2048],
                                                       y2c[:, 2048:4096], op=OP.add),
                      reads=("y2b",), writes=("mfr",))
                em.op("dve", lambda v: v.scalar_tensor_tensor(dst_ap, mfc[:, 0:2048], 0.5,
                                                              qfb[:, :], op0=OP.mult, op1=OP.add),
                      reads=("mfr", "qfb"), writes=(dst_key,))

            for q in range(4):
                for mi in range(4):
                    m = 4 * q + mi
                    xb, xk = x1l[mi % 2], f"x1l{mi % 2}"
                    em.dma("qsp", xb[:, :], x1_sp[128 * m:128 * (m + 1), :],
                           reads=("x1_sp",), writes=(xk,))
                    em.op("dve", lambda v, xb=xb, m=m: v.reduce_max(
                        qcols[:, m:m + 1], xb[:, :], axis=AX.X,
                        apply_absolute_value=True),
                        reads=(xk,), writes=("qamax",))
                    em.op("dve", lambda v, m=m: v.tensor_scalar_max(
                        qcols[:, 48:49], qcols[:, m:m + 1], EPS),
                        reads=("qamax",), writes=("q48",))
                    em.op("dve", lambda v: v.reciprocal(qcols[:, 49:50], qcols[:, 48:49]),
                          reads=("q48",), writes=("q49",))
                    em.op("dve", lambda v: v.tensor_scalar_mul(qcols[:, 49:50],
                                                               qcols[:, 49:50], 127.0),
                          reads=("q49",), writes=("q49",))
                    em.op("dve", lambda v, m=m: v.tensor_tensor(
                        qcols[:, 16 + m:17 + m], qcols[:, 48:49], small[:, 12:13], op=OP.mult),
                        reads=("q48", "small"), writes=("qc1",))
                    emit_round(xb[:, :], xk, "q49", qcols[:, 49:50], hqs[:, :],
                               "hqsAB")
                    for tt in range(4):
                        ptq, pk = p_tq[tt % 2], f"p_tq{tt % 2}"

                        def xtr(t, tt=tt, ptq=ptq):
                            for qq in range(4):
                                j = 4 * tt + qq
                                ins = t.transpose(ptq[:, 128 * qq:128 * (qq + 1)],
                                                  hqs[:, 128 * j:128 * (j + 1)], ident[:, :])
                            return ins
                        em.op("pe", xtr, reads=("hqsAB", "ident", pk), writes=(pk,))

                        def xev(v, tt=tt, ptq=ptq, m=m):
                            for qq in range(4):
                                k = 4 * tt + qq
                                ins = v.tensor_copy(
                                    xqT[:, 2048 * k + 128 * m:2048 * k + 128 * (m + 1)],
                                    ptq[:, 128 * qq:128 * (qq + 1)])
                            return ins
                        em.op("dve", xev, reads=(pk,), writes=("xqT",))
                for fc in range(16):
                    wq, wk = w1qb[0], "w1qb0"
                    em.dma("qpool", wq[:, :], w1q_sp[fc, :, :], reads=("wq_sp",), writes=(wk,))
                    for mi in range(4):
                        m = 4 * q + mi
                        pf, pk = p_f1[mi], f"p_f1{mi}"

                        def f1mm(t, wq=wq, m=m, pf=pf):
                            for k in range(16):
                                ins = t.matmul(pf[:, :],
                                               xqT[:, 2048 * k + 128 * m:2048 * k + 128 * (m + 1)],
                                               wq[:, 512 * k:512 * (k + 1)],
                                               start=(k == 0), stop=(k == 15))
                            return ins
                        em.op("pe", f1mm, reads=("xqT", wk, pk), writes=(pk,))
                        em.op("act", lambda s, pf=pf, mi=mi, m=m, fc=fc: s.activation(
                            hgrp[mi][:, 512 * fc:512 * (fc + 1)], pf[:, :], AF.Gelu,
                            bias=small[:, 18:19], scale=qcols[:, 16 + m:17 + m]),
                            reads=(pk, "qc1", "small"), writes=(f"hgrp{mi}",))
                if debug and q == 3:
                    em.dma("qsp", dbg_b[:, :], hgrp[0][:, :], reads=("hgrp0",), writes=("dbg_b",))
                    em.dma("qsp", dbg_a[:, 0:64], qcols[:, :], reads=("qamax", "qc1", "qc2", "q48", "q49", "q50", "q51"), writes=("dbg_a",))
                for mi in range(4):
                    m = 4 * q + mi
                    hg, hk = hgrp[mi], f"hgrp{mi}"
                    em.op("dve", lambda v, hg=hg: v.reduce_max(
                        qcols[:, 50:51], hg[:, :], axis=AX.X,
                        apply_absolute_value=True),
                        reads=(hk,), writes=("q50",))
                    em.op("dve", lambda v: v.tensor_scalar_max(qcols[:, 50:51], qcols[:, 50:51],
                                                               EPS),
                          reads=("q50",), writes=("q50",))
                    em.op("dve", lambda v: v.reciprocal(qcols[:, 51:52], qcols[:, 50:51]),
                          reads=("q50",), writes=("q51",))
                    em.op("dve", lambda v: v.tensor_scalar_mul(qcols[:, 51:52],
                                                               qcols[:, 51:52], 127.0),
                          reads=("q51",), writes=("q51",))
                    em.op("dve", lambda v, m=m: v.tensor_tensor(
                        qcols[:, 32 + m:33 + m], qcols[:, 50:51], small[:, 13:14], op=OP.mult),
                        reads=("q50", "small"), writes=("qc2",))
                    for piece in range(4):
                        sl = slice(2048 * piece, 2048 * (piece + 1))
                        emit_round(hg[:, sl], hk, "q51", qcols[:, 51:52], hg[:, sl], hk)
                    for tt in range(16):
                        ptq, pk = p_tq[tt % 2], f"p_tq{tt % 2}"

                        def htr(t, tt=tt, ptq=ptq, hg=hg):
                            for qq in range(4):
                                j = 4 * tt + qq
                                ins = t.transpose(ptq[:, 128 * qq:128 * (qq + 1)],
                                                  hg[:, 128 * j:128 * (j + 1)], ident[:, :])
                            return ins
                        em.op("pe", htr, reads=(hk, "ident", pk), writes=(pk,))
                        st_off = 512 * (tt % 2)
                        em.op("dve", lambda v, ptq=ptq, st_off=st_off: v.tensor_copy(
                            hqs[:, st_off:st_off + 512], ptq[:, 0:512]),
                            reads=(pk,), writes=("hqsAB",))
                        em.dma("qact",
                               hqT_sp.rearrange("(b p) t -> b p t", p=128)
                               [4 * tt:4 * (tt + 1), :, 128 * m:128 * (m + 1)]
                               .transpose([1, 0, 2]),
                               hqs.ap().rearrange("p (b t) -> p b t", b=16)
                               [:, 4 * (tt % 2):4 * (tt % 2 + 1), :],
                               reads=("hqsAB",), writes=("hqT_sp",))

        em.barrier()

        with ExitStack() as scE2:
            hqTl = scE2.enter_context(nc.sbuf_tensor("hqTl", [128, 64 * 512], BF16))
            w2qb = [scE2.enter_context(nc.sbuf_tensor(f"w2qb{i}", [128, 16 * 512], BF16)) for i in range(2)]
            x2pre = [scE2.enter_context(nc.sbuf_tensor(f"x2pre{i}", [128, 2048], F32)) for i in range(4)]
            x1q = [scE2.enter_context(nc.sbuf_tensor(f"x1q{i}", [128, 2048], BF16)) for i in range(4)]
            outb = scE2.enter_context(nc.sbuf_tensor("outb", [128, 2048], F32))
            sq2 = scE2.enter_context(nc.sbuf_tensor("sq2", [128, 2048], BF16))
            for q in range(4):
                for k in range(64):
                    em.dma("qsp", hqTl[:, 512 * k:512 * (k + 1)],
                           hqT_sp[128 * k:128 * (k + 1), 512 * q:512 * (q + 1)],
                           reads=("hqT_sp",), writes=("hqTl",))
                for mi in range(4):
                    m = 4 * q + mi
                    em.dma("qsp", x1q[mi][:, :], x1_sp[128 * m:128 * (m + 1), :],
                           reads=("x1_sp",), writes=(f"x1q{mi}",))
                for d2c in range(4):
                    for ks in range(4):
                        wq, wk = w2qb[ks % 2], f"w2qb{ks % 2}"
                        em.dma("qpool", wq.ap().rearrange("p (k f) -> p k f", k=16),
                               w2q_sp[d2c, 16 * ks:16 * (ks + 1), :, :].rearrange("k p f -> p k f"),
                               reads=("wq_sp",), writes=(wk,))
                        for mi in range(4):
                            pf, pk = p_f1[mi], f"p_f1{mi}"

                            def f2mm(t, wq=wq, ks=ks, mi=mi, pf=pf):
                                for kk in range(16):
                                    k = 16 * ks + kk
                                    ins = t.matmul(
                                        pf[:, :],
                                        hqTl[:, 512 * k + 128 * mi:512 * k + 128 * (mi + 1)],
                                        wq[:, 512 * kk:512 * (kk + 1)],
                                        start=(k == 0), stop=(k == 63))
                                return ins
                            em.op("pe", f2mm, reads=("hqTl", wk, pk), writes=(pk,))
                    for mi in range(4):
                        m = 4 * q + mi
                        pf, pk = p_f1[mi], f"p_f1{mi}"
                        em.op("dve", lambda v, pf=pf, mi=mi, d2c=d2c, m=m: v.scalar_tensor_tensor(
                            x2pre[mi][:, 512 * d2c:512 * (d2c + 1)], pf[:, :],
                            qcols[:, 32 + m:33 + m], x1q[mi][:, 512 * d2c:512 * (d2c + 1)],
                            op0=OP.mult, op1=OP.add),
                            reads=(pk, "qc2", f"x1q{mi}"), writes=(f"x2pre{mi}",))
                for mi in range(4):
                    m = 4 * q + mi
                    xp, xk = x2pre[mi], f"x2pre{mi}"
                    em.op("dve", lambda v, xp=xp: v.tensor_reduce(qcols[:, 52:53], xp[:, :],
                                                                  axis=AX.X, op=OP.add),
                          reads=(xk,), writes=("q52",))
                    em.op("dve", lambda v: v.tensor_scalar_mul(qcols[:, 52:53], qcols[:, 52:53],
                                                               1.0 / D_MODEL),
                          reads=("q52",), writes=("q52",))
                    em.op("act", lambda s, xp=xp: s.activation(sq2[:, :], xp[:, :], AF.Square,
                                                               accum_out=qcols[:, 53:54]),
                          reads=(xk,), writes=("sq2", "q53"))
                    em.op("dve", lambda v: v.tensor_scalar_mul(qcols[:, 53:54], qcols[:, 53:54],
                                                               1.0 / D_MODEL),
                          reads=("q53",), writes=("q53",))
                    em.op("dve", lambda v: v.scalar_tensor_tensor(qcols[:, 54:55], qcols[:, 52:53],
                                                                  qcols[:, 52:53], qcols[:, 53:54],
                                                                  op0=OP.mult, op1=OP.subtract),
                          reads=("q52", "q53"), writes=("q54",))
                    em.op("dve", lambda v: v.tensor_scalar_mul(qcols[:, 54:55], qcols[:, 54:55],
                                                               -1.0),
                          reads=("q54",), writes=("q54",))
                    em.op("act", lambda s: s.activation(qcols[:, 54:55], qcols[:, 54:55], AF.Ln,
                                                        bias=small[:, 7:8], scale=1.0),
                          reads=("q54", "small"), writes=("q54",))
                    em.op("dve", lambda v: v.tensor_scalar_mul(qcols[:, 54:55], qcols[:, 54:55],
                                                               -0.5),
                          reads=("q54",), writes=("q54",))
                    em.op("act", lambda s: s.activation(qcols[:, 54:55], qcols[:, 54:55], AF.Exp),
                          reads=("q54",), writes=("q54",))
                    em.op("dve", lambda v: v.scalar_tensor_tensor(qcols[:, 55:56], qcols[:, 52:53],
                                                                  -1.0, qcols[:, 54:55],
                                                                  op0=OP.mult, op1=OP.mult),
                          reads=("q52", "q54"), writes=("q55",))
                    em.op("dve", lambda v, xp=xp: v.tensor_scalar(outb[:, :], xp[:, :],
                                                                  qcols[:, 54:55], qcols[:, 55:56],
                                                                  op0=OP.mult, op1=OP.add),
                          reads=(xk, "q54", "q55"), writes=("outb",))
                    em.dma("qact", out_t[128 * m:128 * (m + 1), :], outb[:, :], reads=("outb",),
                           writes=("out_t",))

        em.wait_all("sp")
        em.wait_all("pool")
        em.flush(block)
    stack.close()
    return nc


_NC_CACHE = {}


def kernel(**inputs):
    u = np.ascontiguousarray(np.asarray(inputs["L0_attn_u"], np.float32))
    W_in = np.ascontiguousarray(np.asarray(inputs["L0_attn_W_in"], np.float32))
    conv_w = np.asarray(inputs["L0_attn_conv_w"], np.float32)
    conv_b = np.asarray(inputs["L0_attn_conv_b"], np.float32)
    dt_bias = np.asarray(inputs["L0_attn_dt_bias"], np.float32)
    A_log = np.asarray(inputs["L0_attn_A_log"], np.float32)
    Dv = np.asarray(inputs["L0_attn_D"], np.float32)
    norm_w = np.asarray(inputs["L0_attn_norm_w"], np.float32)
    W_out = np.ascontiguousarray(np.asarray(inputs["L0_attn_W_out"], np.float32))
    w1 = np.asarray(inputs["L0_ffn_w1"], np.float32)
    w2 = np.asarray(inputs["L0_ffn_w2"], np.float32)

    wx_pad = np.zeros((D_MODEL, XB * 128), np.float32)
    wx_pad[:, :4416] = W_in[:, D_INNER:]
    w_in_x = np.ascontiguousarray(
        wx_pad.reshape(16, 128, XB, 128).transpose(2, 0, 1, 3))
    w_in_z = np.ascontiguousarray(W_in[:, :D_INNER])
    w1t = np.ascontiguousarray(w1.T)
    w2t = np.ascontiguousarray(w2.T)
    convw_pad = np.zeros((XB * 128, 4), np.float32)
    convw_pad[:4352] = conv_w
    convb_pad = np.zeros((XB * 128,), np.float32)
    convb_pad[:4352] = conv_b
    small_c = np.zeros((64, 4), np.float32)
    small_c[:, 0] = dt_bias
    small_c[:, 1] = A_log
    small_r = np.zeros((4, 64), np.float32)
    small_r[2] = Dv

    in_maps = []
    for c in range(8):
        b, h = c // 2, c % 2
        g0 = h * T_OWN
        ut = np.zeros((D_MODEL, T_ALL), np.float32)
        lo = g0 - (T_FOR + HALO)
        src_lo = max(lo, 0)
        ut[:, src_lo - lo:T_FOR + HALO] = u[b, src_lo:g0].T
        ut[:, T_FOR + HALO:] = u[b, g0:g0 + T_OWN].T
        mask = np.full((128, 1), 0.0 if h == 0 else 1.0, np.float32)
        in_maps.append(dict(
            u_t=np.ascontiguousarray(ut),
            u_own=np.ascontiguousarray(u[b, g0:g0 + T_OWN]),
            w_in_x=w_in_x, w_in_z=w_in_z, w_out=W_out, w1t=w1t, w2t=w2t,
            conv_wt=convw_pad, conv_bt=convb_pad, small_c=small_c,
            small_r=small_r, norm_w=norm_w, seq_mask=mask))

    if "nc" not in _NC_CACHE:
        _NC_CACHE["nc"] = build_program()
    nc = _NC_CACHE["nc"]
    res = run_bass_kernel_spmd(nc, in_maps, core_ids=list(range(8)))
    out = np.zeros((4, 4096, D_MODEL), np.float32)
    for c in range(8):
        b, h = c // 2, c % 2
        out[b, h * T_OWN:(h + 1) * T_OWN] = res.results[c]["out"]
    return out


# version 15
# speedup vs baseline: 1.1030x; 1.0474x over previous
import zlib
import numpy as np
import ml_dtypes
import concourse.bass as bass
import concourse.mybir as mybir
from concourse.bass_utils import run_bass_kernel_spmd
from contextlib import ExitStack

F32 = mybir.dt.float32
BF16 = mybir.dt.bfloat16
FP8 = mybir.dt.float8e4
AF = mybir.ActivationFunctionType
OP = mybir.AluOpType
AX = mybir.AxisListType
bfd = ml_dtypes.bfloat16

D_MODEL = 2048
D_INNER = 4096
CHUNK = 128
KFOR = 4
T_OWN = 2048
T_FOR = KFOR * CHUNK
HALO = 3
T_ALL = T_OWN + T_FOR + HALO
T_CONV = T_OWN + T_FOR
NCH = T_CONV // CHUNK
XB = 35
EPS = 1e-5
BIG = 8192.0

ENGINES = ("pe", "dve", "act", "pool", "sp")
NSUBQ = 6
QUEUES = tuple(f"qsp_{i}" for i in range(NSUBQ)) + ("qact",) + tuple(f"qpool_{i}" for i in range(NSUBQ))
Q_OWNER = {**{f"qsp_{i}": "sp" for i in range(NSUBQ)}, "qact": "act",
           **{f"qpool_{i}": "pool" for i in range(NSUBQ)}}


class Em:

    def __init__(self, nc):
        self.nc = nc
        self.sems = {}
        self.cnt = {s: 0 for s in ENGINES + QUEUES}
        self.pend = {e: [] for e in ENGINES}
        self.last_w = {}
        self.readers = {}
        self.wm = {}

    def init_sems(self, stack):
        for s in ENGINES + QUEUES:
            self.sems[s] = stack.enter_context(self.nc.semaphore("sem_" + s))

    SAME_ENGINE_EDGES = True

    def _need(self, eng, reads, writes):
        need = {}
        se = self.SAME_ENGINE_EDGES
        for k in reads:
            lw = self.last_w.get(k)
            if lw is not None and (se or lw[0] != eng):
                need[lw[0]] = max(need.get(lw[0], 0), lw[1])
        for k in writes:
            lw = self.last_w.get(k)
            if lw is not None and (se or lw[0] != eng):
                need[lw[0]] = max(need.get(lw[0], 0), lw[1])
            for st, c in self.readers.get(k, {}).items():
                if se or st != eng:
                    need[st] = max(need.get(st, 0), c)
        out = {}
        for st, c in need.items():
            if st in QUEUES:
                c = self.cnt[st]
            if st == eng:
                if c >= self.cnt[eng] + 1:
                    continue
            if self.wm.get((eng, st), 0) < c:
                out[st] = c
                self.wm[(eng, st)] = c
        return out

    def _track(self, stream, cnt, reads, writes):
        for k in reads:
            self.readers.setdefault(k, {})[stream] = cnt
        for k in writes:
            self.last_w[k] = (stream, cnt)
            self.readers[k] = {}

    def op(self, eng, fn, reads=(), writes=()):
        need = self._need(eng, reads, writes)
        self.cnt[eng] += 1
        cnt = self.cnt[eng]
        sems = self.sems

        def closure(e, need=need, fn=fn):
            for st, c in need.items():
                e.wait_ge(sems[st], c)
            fn(e).then_inc(sems[eng], 1)
        self.pend[eng].append(closure)
        self._track(eng, cnt, reads, writes)

    def _subq(self, q, writes):
        if q in ("qpool", "qsp"):
            tag = writes[0] if writes else "x"
            return f"{q}_{zlib.crc32(str(tag).encode()) % NSUBQ}"
        return q

    def dma(self, q, out, in_, reads=(), writes=(), **kw):
        q = self._subq(q, writes)
        eng = Q_OWNER[q]
        need = self._need(eng, reads, writes)
        self.cnt[q] += 16
        qc = self.cnt[q]
        sems = self.sems

        def closure(e, need=need, out=out, in_=in_, kw=kw):
            for st, c in need.items():
                e.wait_ge(sems[st], c)
            e.dma_start(out=out, in_=in_, **kw).then_inc(sems[q], 16)
        self.pend[eng].append(closure)
        self._track(q, qc, reads, writes)

    def dma_t(self, q, out, in_, reads=(), writes=()):
        q = self._subq(q, writes)
        eng = Q_OWNER[q]
        need = self._need(eng, reads, writes)
        self.cnt[q] += 16
        qc = self.cnt[q]
        sems = self.sems

        def closure(e, need=need, out=out, in_=in_):
            for st, c in need.items():
                e.wait_ge(sems[st], c)
            e.dma_start_transpose(out, in_).then_inc(sems[q], 16)
        self.pend[eng].append(closure)
        self._track(q, qc, reads, writes)

    def wait_all(self, eng):
        sems = self.sems
        vals = [(st, self.cnt[st]) for st in ENGINES + QUEUES
                if st != eng and self.cnt[st] > 0]

        def closure(e, vals=vals):
            for st, c in vals:
                e.wait_ge(sems[st], c)
        self.pend[eng].append(closure)
        for st, c in vals:
            self.wm[(eng, st)] = max(self.wm.get((eng, st), 0), c)

    def barrier(self):
        for e in ENGINES:
            self.wait_all(e)
        self.last_w = {}
        self.readers = {}

    def flush(self, block):
        for e, lst in self.pend.items():
            if not lst:
                continue
            body = {"pe": block.tensor, "dve": block.vector, "act": block.scalar,
                    "pool": block.gpsimd, "sp": block.sync}[e]

            def run(engine, lst=tuple(lst)):
                for c in lst:
                    c(engine)
            body(run)
            self.pend[e] = []


def build_program(debug=False):
    nc = bass.Bass("TRN2", target_bir_lowering=False)
    SCR = "ExternalOutput" if debug else "Internal"
    u_t = nc.dram_tensor("u_t", [D_MODEL, T_ALL], F32, kind="ExternalInput").ap()
    u_own = nc.dram_tensor("u_own", [T_OWN, D_MODEL], F32, kind="ExternalInput").ap()
    w_in_x = nc.dram_tensor("w_in_x", [XB, 16, 128, 128], F32, kind="ExternalInput").ap()
    w_in_z = nc.dram_tensor("w_in_z", [D_MODEL, D_INNER], F32, kind="ExternalInput").ap()
    w_out = nc.dram_tensor("w_out", [D_INNER, D_MODEL], F32, kind="ExternalInput").ap()
    w1t = nc.dram_tensor("w1t", [D_MODEL, 8192], F32, kind="ExternalInput").ap()
    w2t = nc.dram_tensor("w2t", [8192, D_MODEL], F32, kind="ExternalInput").ap()
    conv_wt = nc.dram_tensor("conv_wt", [XB * 128, 4], F32, kind="ExternalInput").ap()
    conv_bt = nc.dram_tensor("conv_bt", [XB * 128], F32, kind="ExternalInput").ap()
    small_c = nc.dram_tensor("small_c", [64, 4], F32, kind="ExternalInput").ap()
    small_r = nc.dram_tensor("small_r", [4, 64], F32, kind="ExternalInput").ap()
    norm_w = nc.dram_tensor("norm_w", [D_INNER], F32, kind="ExternalInput").ap()
    seq_mask = nc.dram_tensor("seq_mask", [128, 1], F32, kind="ExternalInput").ap()
    out_t = nc.dram_tensor("out", [T_OWN, D_MODEL], F32, kind="ExternalOutput").ap()

    x_sp = nc.dram_tensor("x_sp", [D_INNER, T_CONV], BF16, kind=SCR).ap()
    zsil_sp = nc.dram_tensor("zsil_sp", [8, T_OWN, 512], BF16, kind=SCR).ap()
    y_sp = nc.dram_tensor("y_sp", [T_OWN, D_INNER], BF16, kind=SCR).ap()
    x1_sp = nc.dram_tensor("x1_sp", [T_OWN, D_MODEL], BF16, kind=SCR).ap()
    woutb_sp = nc.dram_tensor("woutb_sp", [D_INNER, D_MODEL], BF16, kind=SCR).ap()
    w1q_sp = nc.dram_tensor("w1q_sp", [16, 128, 8192], FP8, kind=SCR).ap()
    w2q_sp = nc.dram_tensor("w2q_sp", [4, 64, 128, 512], FP8, kind=SCR).ap()
    hqT_sp = nc.dram_tensor("hqT_sp", [8192, T_OWN], BF16, kind=SCR).ap()
    tn_sp = nc.dram_tensor("tn_sp", [T_OWN, D_INNER], BF16, kind="Internal").ap()
    dbg_a = nc.dram_tensor("dbg_a", [128, 4096], F32, kind=SCR).ap()
    dbg_b = nc.dram_tensor("dbg_b", [128, 8192], BF16, kind=SCR).ap()
    dbg_c = nc.dram_tensor("dbg_c", [128, 8192], BF16, kind=SCR).ap()
    dbg_d = nc.dram_tensor("dbg_d", [128, 4096], F32, kind=SCR).ap()

    ii = np.arange(128)
    c_ident = nc.inline_tensor(np.eye(128, dtype=bfd), "c_ident").ap()
    c_ident32 = nc.inline_tensor(np.eye(128, dtype=np.float32), "c_ident32").ap()
    c_ones = nc.inline_tensor(np.ones((128, 128), dtype=bfd), "c_ones").ap()
    c_ones32 = nc.inline_tensor(np.ones((1, 128), dtype=np.float32), "c_ones32").ap()
    c_tri = nc.inline_tensor((ii[:, None] <= ii[None, :]).astype(bfd), "c_tri").ap()
    c_ntril = nc.inline_tensor((-(ii[:, None] <= ii[None, :]).astype(np.float32)).astype(bfd), "c_ntril").ap()
    c_nbig = nc.inline_tensor((-BIG * (ii[:, None] < ii[None, :]).astype(np.float32)).astype(bfd), "c_nbig").ap()

    stack = ExitStack()
    em = Em(nc)

    with nc.Block() as block:
        em.init_sems(stack)
        ident = stack.enter_context(nc.sbuf_tensor("ident", [128, 128], BF16))
        ident32 = stack.enter_context(nc.sbuf_tensor("ident32", [128, 128], F32))
        onesb = stack.enter_context(nc.sbuf_tensor("onesb", [128, 128], BF16))
        ones32 = stack.enter_context(nc.sbuf_tensor("ones32", [1, 128], F32))
        trib = stack.enter_context(nc.sbuf_tensor("trib", [128, 128], BF16))
        ntrilb = stack.enter_context(nc.sbuf_tensor("ntrilb", [128, 128], BF16))
        nbigb = stack.enter_context(nc.sbuf_tensor("nbigb", [128, 128], BF16))
        small = stack.enter_context(nc.sbuf_tensor("small", [128, 24], F32))
        dbc = stack.enter_context(nc.sbuf_tensor("dbc", [128, 64], BF16))
        drow = stack.enter_context(nc.sbuf_tensor("drow", [1, 64], BF16))
        convw = stack.enter_context(nc.sbuf_tensor("convw", [128, XB * 4], F32))
        convb = stack.enter_context(nc.sbuf_tensor("convb", [128, XB], F32))

        em.dma("qsp", ident[:, :], c_ident[:, :], writes=("ident",))
        em.dma("qsp", ident32[:, :], c_ident32[:, :], writes=("ident32",))
        em.dma("qsp", onesb[:, :], c_ones[:, :], writes=("onesb",))
        em.dma("qsp", ones32[:, :], c_ones32[:, :], writes=("ones32",))
        em.dma("qsp", trib[:, :], c_tri[:, :], writes=("trib",))
        em.dma("qsp", ntrilb[:, :], c_ntril[:, :], writes=("ntrilb",))
        em.dma("qsp", nbigb[:, :], c_nbig[:, :], writes=("nbigb",))
        em.dma("qsp", small[0:64, 0:2], small_c[:, 0:2], writes=("small",))
        em.dma("qsp", small[:, 8:9], seq_mask[:, :], writes=("small",))
        em.dma("qpool", drow[0:1, :], small_r[2:3, :], writes=("drow",))
        em.dma("qpool", convw.ap().rearrange("p (c k) -> p c k", k=4),
               conv_wt.rearrange("(c p) k -> p c k", p=128), writes=("convw",))
        em.dma("qsp", convb[:, :],
               conv_bt.rearrange("(c p) -> p c", p=128), writes=("convb",),
               allow_slow_non_contiguous=True)
        for col, val in ((4, 1.0), (5, 0.5), (6, 127.0), (7, EPS), (9, -0.5),
                         (14, 0.0), (15, 0.0), (18, 0.0)):
            em.op("dve", lambda v, col=col, val=val: v.memset(small[:, col:col + 1], val),
                  writes=("small",))
        em.op("act", lambda s: s.activation(small[0:64, 3:4], small[0:64, 1:2], AF.Exp),
              reads=("small",), writes=("small",))
        em.op("dve", lambda v: v.tensor_scalar_mul(small[0:64, 3:4], small[0:64, 3:4], -1.0),
              reads=("small",), writes=("small",))
        with ExitStack() as sc0:
            p0 = sc0.enter_context(nc.psum_tensor("p0t", [128, 64], F32))
            em.op("pe", lambda t: t.matmul(p0[:, 0:64], onesb[0:1, :], drow[0:1, :],
                                           start=True, stop=True),
                  reads=("onesb", "drow"), writes=("p0t",))
            em.op("dve", lambda v: v.tensor_copy(dbc[:, :], p0[:, 0:64]),
                  reads=("p0t",), writes=("dbc",))

        em.barrier()

        scAP3 = ExitStack()
        dbcr = scAP3.enter_context(nc.sbuf_tensor("dbcr", [128, 4096], BF16))
        em.op("dve", lambda v: v.tensor_copy(
            dbcr.ap().rearrange("p (h q) -> p h q", h=64),
            dbc.ap().unsqueeze(2).broadcast_to([128, 64, 64])),
            reads=("dbc",), writes=("dbcr",))
        b_res = scAP3.enter_context(nc.sbuf_tensor("b_res", [128, T_CONV], BF16))
        c_res = scAP3.enter_context(nc.sbuf_tensor("c_res", [128, T_CONV], BF16))
        dt_res = scAP3.enter_context(nc.sbuf_tensor("dt_res", [64, T_CONV], F32))
        with ExitStack() as scA:
            utb = scA.enter_context(nc.sbuf_tensor("utb", [128, 16 * T_ALL], BF16))
            for part in range(4):
                em.dma("qpool",
                       utb.ap().rearrange("p (k t) -> p k t", k=16)[:, 4 * part:4 * (part + 1), :],
                       u_t.rearrange("(k p) t -> p k t", p=128)[:, 4 * part:4 * (part + 1), :],
                       writes=("utb",))
            wxb = [scA.enter_context(nc.sbuf_tensor(f"wxb{i}", [128, 2048], BF16)) for i in range(2)]
            slab = [scA.enter_context(nc.sbuf_tensor(f"slab{i}", [128, T_ALL], BF16)) for i in range(2)]
            acc = [scA.enter_context(nc.sbuf_tensor(f"acc{i}", [128, T_CONV], BF16)) for i in range(2)]
            xact = [scA.enter_context(nc.sbuf_tensor(f"xact{i}", [128, T_CONV], BF16)) for i in range(2)]
            stat = [scA.enter_context(nc.sbuf_tensor(f"stat{i}", [128, 1024], F32)) for i in range(4)]
            statq = scA.enter_context(nc.sbuf_tensor("statq", [128, 1024], BF16))
            statacc = scA.enter_context(nc.sbuf_tensor("statacc", [128, 256], F32))
            em.op("dve", lambda v: v.memset(statacc[:, :], 0.0), writes=("statacc",))
            rmsrow = scA.enter_context(nc.sbuf_tensor("rmsrow", [2, 4], F32))
            p2p = [scA.enter_context(nc.psum_tensor(f"p2p{i}", [128, 512], F32)) for i in range(2)]
            p_st = scA.enter_context(nc.psum_tensor("p_st", [128, 8], F32))

            stat_jobs = []
            for r in range(16):
                for qr in range(8):
                    stat_jobs.append((w1t[128 * r:128 * (r + 1), 1024 * qr:1024 * (qr + 1)],
                                      r * 8 + qr))
            for r in range(64):
                for hh in range(2):
                    stat_jobs.append((w2t[128 * r:128 * (r + 1), 1024 * hh:1024 * (hh + 1)],
                                      128 + r * 2 + hh))

            def stat_dma(idx):
                src_ap, col = stat_jobs[idx]
                em.dma("qsp", stat[idx % 4][:, :], src_ap, writes=(f"stat{idx % 4}",))

            def stat_sq(idx):
                _, col = stat_jobs[idx]
                sb, sk = stat[idx % 4], f"stat{idx % 4}"
                em.op("act", lambda s, sb=sb, col=col: s.activation(
                    statq[:, :], sb[:, :], AF.Square, accum_out=statacc[:, col:col + 1]),
                    reads=(sk,), writes=("statq", "statacc"))

            stat_d = 0
            stat_s = 0

            subs = [(i * 512, 512) for i in range(5)] + [(2560, 3)]
            for c in range(XB):
                want_s = c * 256 // XB
                while stat_s < min(want_s, len(stat_jobs)):
                    while stat_d < min(stat_s + 4, len(stat_jobs)):
                        stat_dma(stat_d)
                        stat_d += 1
                    stat_sq(stat_s)
                    stat_s += 1
                wb, wkey = wxb[c % 2], f"wxb{c % 2}"
                em.dma("qpool", wb.ap().rearrange("p (k n) -> p k n", k=16),
                       w_in_x[c, :, :, :].rearrange("k p n -> p k n"), writes=(wkey,))
                sl, skey = slab[c % 2], f"slab{c % 2}"
                for si, (o, w) in enumerate(subs):
                    pb, pkey = p2p[si % 2], f"p2p{si % 2}"

                    def mmg(t, wb=wb, o=o, w=w, pb=pb):
                        for k in range(16):
                            ins = t.matmul(pb[:, 0:w], wb[:, 128 * k:128 * (k + 1)],
                                           utb[:, k * T_ALL + o:k * T_ALL + o + w],
                                           start=(k == 0), stop=(k == 15))
                        return ins
                    em.op("pe", mmg, reads=(wkey, "utb", pkey), writes=(pkey,))
                    if c == 34:
                        lo = max(o, HALO)
                        if lo < o + w:
                            em.op("dve", lambda v, pb=pb, o=o, w=w, lo=lo: v.tensor_copy(
                                dt_res[0:64, lo - HALO:o + w - HALO], pb[0:64, lo - o:w]),
                                reads=(pkey,), writes=("dt_res",))
                    else:
                        em.op("dve", lambda v, pb=pb, sl=sl, o=o, w=w: v.tensor_copy(
                            sl[:, o:o + w], pb[:, 0:w]),
                            reads=(pkey,), writes=(skey,))
                if c == 34:
                    continue
                ac, ackey = acc[c % 2], f"acc{c % 2}"
                xa, xakey = xact[c % 2], f"xact{c % 2}"
                em.op("dve", lambda v, sl=sl, ac=ac, c=c: v.tensor_scalar_mul(
                    ac[:, :], sl[:, 0:T_CONV], convw[:, 4 * c:4 * c + 1]),
                    reads=(skey, "convw"), writes=(ackey,))
                for k in range(1, 4):
                    em.op("dve", lambda v, sl=sl, ac=ac, c=c, k=k: v.scalar_tensor_tensor(
                        ac[:, :], sl[:, k:k + T_CONV], convw[:, 4 * c + k:4 * c + k + 1],
                        ac[:, :], op0=OP.mult, op1=OP.add),
                        reads=(skey, "convw", ackey), writes=(ackey,))
                em.op("act", lambda s, ac=ac, xa=xa, c=c: s.activation(
                    xa[:, :], ac[:, :], AF.Silu, bias=convb[:, c:c + 1], scale=1.0),
                    reads=(ackey, "convb"), writes=(xakey,))
                if c < 32:
                    em.dma("qact", x_sp[128 * c:128 * (c + 1), :], xa[:, :],
                           reads=(xakey,), writes=("x_sp",))
                elif c == 32:
                    em.op("dve", lambda v, xa=xa: v.tensor_copy(b_res[:, :], xa[:, :]),
                          reads=(xakey,), writes=("b_res",))
                else:
                    em.op("dve", lambda v, xa=xa: v.tensor_copy(c_res[:, :], xa[:, :]),
                          reads=(xakey,), writes=("c_res",))

            while stat_s < len(stat_jobs):
                while stat_d < min(stat_s + 4, len(stat_jobs)):
                    stat_dma(stat_d)
                    stat_d += 1
                stat_sq(stat_s)
                stat_s += 1
            em.op("dve", lambda v: v.reduce_sum(small[:, 14:15], statacc[:, 0:128], axis=AX.X),
                  reads=("statacc",), writes=("small",))
            em.op("dve", lambda v: v.reduce_sum(small[:, 15:16], statacc[:, 128:256], axis=AX.X),
                  reads=("statacc",), writes=("small",))
            em.op("pe", lambda t: t.matmul(p_st[0:1, 0:1], small[:, 14:15],
                                           small[:, 4:5], start=True, stop=True),
                  reads=("small", "p_st"), writes=("p_st",))
            em.op("pe", lambda t: t.matmul(p_st[0:1, 1:2], small[:, 15:16],
                                           small[:, 4:5], start=False, stop=True),
                  reads=("small", "p_st"), writes=("p_st",))
            NW = float(2048 * 8192)
            em.op("dve", lambda v: v.tensor_scalar(rmsrow[0:1, 0:2], p_st[0:1, 0:2],
                                                   1.0 / NW, EPS, op0=OP.mult, op1=OP.max),
                  reads=("p_st",), writes=("rmsrow",))
            em.op("act", lambda s: s.activation(rmsrow[0:1, 0:2], rmsrow[0:1, 0:2], AF.Ln),
                  reads=("rmsrow",), writes=("rmsrow",))
            em.op("dve", lambda v: v.tensor_scalar_mul(rmsrow[0:1, 0:2], rmsrow[0:1, 0:2], -0.5),
                  reads=("rmsrow",), writes=("rmsrow",))
            em.op("act", lambda s: s.activation(rmsrow[0:1, 0:2], rmsrow[0:1, 0:2], AF.Exp),
                  reads=("rmsrow",), writes=("rmsrow",))
            em.op("dve", lambda v: v.tensor_scalar_mul(rmsrow[0:1, 2:4], rmsrow[0:1, 0:2], 254.0),
                  reads=("rmsrow",), writes=("rmsrow",))
            em.op("dve", lambda v: v.reciprocal(rmsrow[0:1, 2:4], rmsrow[0:1, 2:4]),
                  reads=("rmsrow",), writes=("rmsrow",))
            em.op("pe", lambda t: t.matmul(p_st[:, 4:8], ones32[0:1, :],
                                           rmsrow[0:1, 0:4], start=True, stop=True),
                  reads=("rmsrow", "ones32", "p_st"), writes=("p_st2",))
            em.op("dve", lambda v: v.tensor_copy(small[:, 10:12], p_st[:, 4:6]),
                  reads=("p_st2",), writes=("small",))
            em.op("dve", lambda v: v.tensor_copy(small[:, 12:14], p_st[:, 6:8]),
                  reads=("p_st2",), writes=("small",))

            wzb = [scA.enter_context(nc.sbuf_tensor(f"wzb{i}", [128, 16 * 512], BF16)) for i in range(2)]
            zst = [scA.enter_context(nc.sbuf_tensor(f"zst{i}", [128, 512], BF16)) for i in range(2)]
            for zc in range(8):
                wz, wzk = wzb[zc % 2], f"wzb{zc % 2}"
                for part in range(4):
                    em.dma("qpool",
                           wz.ap().rearrange("p (k n) -> p k n", k=16)[:, 4 * part:4 * (part + 1), :],
                           w_in_z.rearrange("(k p) n -> p k n", p=128)
                           [:, 4 * part:4 * (part + 1), 512 * zc:512 * (zc + 1)],
                           writes=(wzk,))
                for m in range(16):
                    pb, pkey = p2p[m % 2], f"p2p{m % 2}"

                    def mmz(t, wz=wz, m=m, pb=pb):
                        for k in range(16):
                            o = k * T_ALL + HALO + T_FOR + 128 * m
                            ins = t.matmul(pb[:, :], utb[:, o:o + 128],
                                           wz[:, 512 * k:512 * (k + 1)],
                                           start=(k == 0), stop=(k == 15))
                        return ins
                    em.op("pe", mmz, reads=(wzk, "utb", pkey), writes=(pkey,))
                    zb, zkey = zst[m % 2], f"zst{m % 2}"
                    em.op("act", lambda s, pb=pb, zb=zb: s.activation(zb[:, :], pb[:, :], AF.Silu),
                          reads=(pkey,), writes=(zkey,))
                    em.dma("qact", zsil_sp[zc, 128 * m:128 * (m + 1), :], zb[:, :],
                           reads=(zkey,), writes=("zsil_sp",))

        em.barrier()

        with ExitStack() as sc3:
            s_st = sc3.enter_context(nc.sbuf_tensor("s_st", [128, 4096], F32))
            em.op("dve", lambda v: v.memset(s_st[:, :], 0.0), writes=("s_st",))
            xtok = [sc3.enter_context(nc.sbuf_tensor(f"xtok{i}", [128, 4096], BF16)) for i in range(2)]
            xch = [sc3.enter_context(nc.sbuf_tensor("xch0", [128, 4096], BF16))]
            ybuf = [sc3.enter_context(nc.sbuf_tensor("ybuf0", [128, 4096], BF16))]
            sbf = sc3.enter_context(nc.sbuf_tensor("sbf", [128, 4096], BF16))
            xd = sc3.enter_context(nc.sbuf_tensor("xd", [128, 4096], BF16))
            xw = sc3.enter_context(nc.sbuf_tensor("xw", [128, 4096], BF16))
            mpp = sc3.enter_context(nc.sbuf_tensor("mpp", [128, 8192], BF16))
            dtap = sc3.enter_context(nc.sbuf_tensor("dtap", [128, 8192], BF16))
            decb = [sc3.enter_context(nc.sbuf_tensor(f"decb{i}", [128, 512], BF16)) for i in range(2)]
            cbsb = sc3.enter_context(nc.sbuf_tensor("cbsb", [128, 128], BF16))
            btok = sc3.enter_context(nc.sbuf_tensor("btok", [128, 128], BF16))
            dch = sc3.enter_context(nc.sbuf_tensor("dch", [64, 128 * 8], F32))
            dchb = sc3.enter_context(nc.sbuf_tensor("dchb", [64, 128], BF16))
            dtok = sc3.enter_context(nc.sbuf_tensor("dtok", [128, 64 * 4], F32))
            dtokb = sc3.enter_context(nc.sbuf_tensor("dtokb", [128, 128], BF16))
            arow = sc3.enter_context(nc.sbuf_tensor("arow", [1, 64], F32))
            evtmp = sc3.enter_context(nc.sbuf_tensor("evtmp", [128, 512], F32))
            wrep = sc3.enter_context(nc.sbuf_tensor("wrep", [128, 4096], BF16))
            arep = sc3.enter_context(nc.sbuf_tensor("arep", [128, 4096], BF16))
            erep = sc3.enter_context(nc.sbuf_tensor("erep", [128, 4096], BF16))
            sgbuf = [sc3.enter_context(nc.sbuf_tensor(f"sgbuf{i}", [128, 2048], F32)) for i in range(2)]
            sgout = sc3.enter_context(nc.sbuf_tensor("sgout", [128, 2048], BF16))
            sg2 = sc3.enter_context(nc.sbuf_tensor("sg2", [128, 2048], BF16))
            sgq = sc3.enter_context(nc.sbuf_tensor("sgq", [128, 2048], FP8))

            p_dec = [sc3.enter_context(nc.psum_tensor(f"p_dec{i}", [128, 512], F32)) for i in range(2)]
            p_y1 = [sc3.enter_context(nc.psum_tensor(f"p_y1{i}", [128, 512], F32)) for i in range(2)]
            p_y2 = [sc3.enter_context(nc.psum_tensor(f"p_y2{i}", [128, 512], F32)) for i in range(2)]
            p_s3 = sc3.enter_context(nc.psum_tensor("p_s3", [128, 512], F32))
            p_aux = sc3.enter_context(nc.psum_tensor("p_aux", [128, 128], F32))
            p_auxb = p_aux.ap().bitcast(BF16)

            em.op("dve", lambda v: v.memset(dch[:, 128 * 7:128 * 8], 0.0), writes=("dchz",))
            DTF, DTA, LND, DLS, WDT, CUM, SCR, DZ = [
                dch[0:64, 128 * i:128 * (i + 1)] for i in range(8)]

            sign_jobs = []
            for fc in range(16):
                for qf in range(4):
                    k0 = 4 * qf
                    src = w1t.rearrange("(k p) f -> p k f", p=128)[:, k0:k0 + 4,
                                                                  512 * fc:512 * (fc + 1)]
                    sign_jobs.append((src, w1q_sp[fc, :, 2048 * qf:2048 * (qf + 1)], 10))
            for d2c in range(4):
                for ks in range(4):
                    for qf in range(4):
                        k0 = 16 * ks + 4 * qf
                        src = w2t.rearrange("(k p) f -> p k f", p=128)[:, k0:k0 + 4,
                                                                      512 * d2c:512 * (d2c + 1)]
                        sign_jobs.append((src, w2q_sp[d2c, 16 * ks + 4 * qf:16 * ks + 4 * (qf + 1), :, :]
                                          .rearrange("k p f -> p k f"), 11))

            def sign_dma(idx):
                src, _, _ = sign_jobs[idx]
                em.dma("qpool", sgbuf[idx % 2].ap().rearrange("p (k f) -> p k f", k=4),
                       src, writes=(f"sgbuf{idx % 2}",))

            def sign_emit(idx):
                _, dst_ap, rms_col = sign_jobs[idx]
                sb, sk = sgbuf[idx % 2], f"sgbuf{idx % 2}"
                em.op("act", lambda s, sb=sb, rms_col=rms_col: s.activation(
                    sgout[:, :], sb[:, :], AF.Sign, bias=small[:, 9:10],
                    scale=small[:, rms_col:rms_col + 1]),
                    reads=(sk, "small"), writes=("sgout",))
                em.op("act", lambda s, sb=sb, rms_col=rms_col: s.activation(
                    sg2[:, :], sb[:, :], AF.Sign, bias=small[:, 5:6],
                    scale=small[:, rms_col:rms_col + 1]),
                    reads=(sk, "small"), writes=("sg2",))
                em.op("dve", lambda v: v.tensor_tensor(sgq[:, :], sgout[:, :], sg2[:, :],
                                                       op=OP.add),
                      reads=("sgout", "sg2"), writes=("sgq",))
                em.dma("qpool", dst_ap,
                       sgq.ap().rearrange("p (k f) -> p k f", k=4) if dst_ap.ndim == 3
                       else sgq[:, :],
                       reads=("sgq",), writes=("wq_sp",))

            sign_d = 0
            sign_i = 0

            for c in range(NCH):
                own = c >= KFOR
                xt, xk = xtok[c % 2], f"xtok{c % 2}"
                xc, xck = xch[0], "xch0"
                em.dma("qsp", xc.ap().rearrange("p (b t) -> p b t", b=32),
                       x_sp.rearrange("(b p) t -> p b t", p=128)[:, :, 128 * c:128 * (c + 1)],
                       reads=("x_sp",), writes=(xck,))
                for g in range(8):
                    pd, pdk = p_dec[g % 2], f"p_dec{g % 2}"

                    def xtg(t, g=g, xc=xc, pd=pd):
                        pb = pd.ap().bitcast(BF16)
                        for qq in range(4):
                            j = 4 * g + qq
                            ins = t.transpose(pb[:, 128 * qq:128 * (qq + 1)],
                                              xc[:, 128 * j:128 * (j + 1)], ident[:, :])
                        return ins
                    em.op("pe", xtg, reads=(xck, "ident", pdk), writes=(pdk,))
                    em.op("act", lambda s, g=g, pd=pd: s.copy(
                        xt[:, 512 * g:512 * (g + 1)], pd.ap().bitcast(BF16)[:, 0:512]),
                        reads=(pdk,), writes=(xk,))
                em.op("pe", lambda t, c=c: t.transpose(p_auxb[:, 0:128],
                                                       b_res[:, 128 * c:128 * (c + 1)], ident[:, :]),
                      reads=("b_res", "ident"), writes=("p_aux",))
                em.op("dve", lambda v: v.tensor_copy(btok[:, :], p_auxb[:, 0:128]),
                      reads=("p_aux",), writes=("btok",))
                em.op("dve", lambda v, c=c: v.tensor_scalar_add(
                    SCR, dt_res[0:64, 128 * c:128 * (c + 1)], small[0:64, 0:1]),
                    reads=("dt_res", "small"), writes=("dscr",))
                em.op("act", lambda s: s.activation(DTF, SCR, AF.Abs), reads=("dscr",), writes=("dtf",))
                em.op("act", lambda s: s.activation(DTF, DTF, AF.Exp, scale=-1.0),
                      reads=("dtf",), writes=("dtf",))
                em.op("act", lambda s: s.activation(DTF, DTF, AF.Ln, bias=small[0:64, 4:5], scale=1.0),
                      reads=("dtf", "small"), writes=("dtf",))
                em.op("dve", lambda v: v.tensor_scalar_max(SCR, SCR, 0.0), reads=("dscr",), writes=("dscr",))
                em.op("dve", lambda v: v.tensor_tensor(DTF, DTF, SCR, op=OP.add),
                      reads=("dtf", "dscr"), writes=("dtf",))
                em.op("dve", lambda v: v.tensor_scalar_mul(DTA, DTF, small[0:64, 3:4]),
                      reads=("dtf", "small"), writes=("dta",))
                em.op("dve", lambda v: v.tensor_tensor_scan(CUM, DTA, DZ, 0.0, op0=OP.add, op1=OP.add),
                      reads=("dta", "dchz"), writes=("cum",))
                em.op("dve", lambda v: v.tensor_scalar_max(SCR, DTF, 1e-38),
                      reads=("dtf",), writes=("dscr",))
                em.op("act", lambda s: s.activation(LND, SCR, AF.Ln), reads=("dscr",), writes=("lnd",))
                em.op("dve", lambda v: v.tensor_scalar_max(LND, LND, -50.0), reads=("lnd",), writes=("lnd",))
                em.op("dve", lambda v: v.tensor_scalar(DLS, CUM, dch[0:64, 128 * 5 + 127:128 * 5 + 128],
                                                       None, op0=OP.subtract),
                      reads=("cum",), writes=("dls",))
                em.op("act", lambda s: s.activation(DLS, DLS, AF.Exp, scale=-1.0),
                      reads=("dls",), writes=("dls",))
                em.op("dve", lambda v: v.tensor_tensor(WDT, DLS, DTF, op=OP.mult),
                      reads=("dls", "dtf"), writes=("wdt",))
                em.op("act", lambda s: s.activation(SCR[:, 0:1], dch[0:64, 128 * 5 + 127:128 * 5 + 128],
                                                    AF.Exp),
                      reads=("cum",), writes=("dscr",))
                em.op("pe", lambda t: t.transpose(p_aux[0:1, 0:64], SCR[:, 0:1], ident32[0:64, 0:64]),
                      reads=("dscr", "ident32"), writes=("p_aux",))
                em.op("act", lambda s: s.copy(arow[0:1, :], p_aux[0:1, 0:64]),
                      reads=("p_aux",), writes=("arow",))
                em.op("pe", lambda t: t.matmul(p_aux[:, 0:64], ones32[0:1, :], arow[0:1, :],
                                               start=True, stop=True),
                      reads=("ones32", "arow", "p_aux"), writes=("p_aux",))
                em.op("dve", lambda v: v.tensor_copy(dtok[:, 128:192], p_aux[:, 0:64]),
                      reads=("p_aux",), writes=("dtok2",))
                em.op("pe", lambda t: t.transpose(p_aux[:, 0:64], CUM, ident32[0:64, 0:64]),
                      reads=("cum", "ident32"), writes=("p_aux",))
                em.op("dve", lambda v: v.tensor_copy(dtok[:, 0:64], p_aux[:, 0:64]),
                      reads=("p_aux",), writes=("dtok0",))
                em.op("pe", lambda t: t.transpose(p_aux[:, 0:64], WDT, ident32[0:64, 0:64]),
                      reads=("wdt", "ident32"), writes=("p_aux",))
                em.op("dve", lambda v: v.tensor_copy(dtok[:, 192:256], p_aux[:, 0:64]),
                      reads=("p_aux",), writes=("dtok3",))
                em.op("act", lambda s: s.activation(dtok[:, 64:128], dtok[:, 0:64], AF.Exp),
                      reads=("dtok0",), writes=("dtok1",))
                em.op("dve", lambda v: v.tensor_copy(dchb[0:64, :], DTA), reads=("dta",), writes=("dchb",))
                em.op("pe", lambda t: t.transpose(p_auxb[:, 0:64], dchb[0:64, :], ident[0:64, 0:64]),
                      reads=("dchb", "ident"), writes=("p_aux",))
                em.op("dve", lambda v: v.tensor_copy(dtokb[:, 0:64], p_auxb[:, 0:64]),
                      reads=("p_aux",), writes=("dtokb0",))
                em.op("dve", lambda v: v.tensor_copy(dchb[0:64, :], LND), reads=("lnd",), writes=("dchb",))
                em.op("pe", lambda t: t.transpose(p_auxb[:, 0:64], dchb[0:64, :], ident[0:64, 0:64]),
                      reads=("dchb", "ident"), writes=("p_aux",))
                em.op("dve", lambda v: v.tensor_copy(dtokb[:, 64:128], p_auxb[:, 0:64]),
                      reads=("p_aux",), writes=("dtokb1",))
                em.op("dve", lambda v: v.tensor_copy(
                    wrep.ap().rearrange("p (h q) -> p h q", h=64),
                    dtok[:, 192:256].unsqueeze(2).broadcast_to([128, 64, 64])),
                    reads=("dtok3",), writes=("wrep",))
                em.op("dve", lambda v: v.tensor_copy(
                    arep.ap().rearrange("p (h q) -> p h q", h=64),
                    dtok[:, 128:192].unsqueeze(2).broadcast_to([128, 64, 64])),
                    reads=("dtok2",), writes=("arep",))
                em.op("dve", lambda v: v.tensor_tensor(xw[:, :], xt[:, :], wrep[:, :],
                                                       op=OP.mult),
                    reads=(xk, "wrep"), writes=("xw",))
                if debug and c == KFOR:
                    em.dma("qpool", dbg_d[:, 0:2048], wrep[:, 0:2048], reads=("wrep",), writes=("dbg_d",))
                    em.dma("qpool", dbg_d[:, 2048:4096], xw[:, 0:2048], reads=("xw",), writes=("dbg_d",))
                if own:
                    em.op("dve", lambda v: v.tensor_copy(
                        erep.ap().rearrange("p (h q) -> p h q", h=64),
                        dtok[:, 64:128].unsqueeze(2).broadcast_to([128, 64, 64])),
                        reads=("dtok1",), writes=("erep",))
                    em.op("dve", lambda v: v.tensor_copy(sbf[:, :], s_st[:, :]),
                          reads=("s_st",), writes=("sbf",))
                    em.op("pe", lambda t, c=c: t.matmul(p_aux[:, :], b_res[:, 128 * c:128 * (c + 1)],
                                                        c_res[:, 128 * c:128 * (c + 1)],
                                                        start=True, stop=True),
                          reads=("b_res", "c_res", "p_aux"), writes=("p_aux",))
                    em.op("act", lambda s: s.copy(cbsb[:, :], p_aux[:, :]),
                          reads=("p_aux",), writes=("cbsb",))
                    em.op("dve", lambda v: v.tensor_tensor(xd[:, :], xt[:, :], dbcr[:, :],
                                                           op=OP.mult),
                        reads=(xk, "dbcr"), writes=("xd",))
                    em.op("dve", lambda v: v.tensor_tensor(
                        dtap.ap().rearrange("p (h i) -> p h i", h=64),
                        dtokb[:, 0:64].unsqueeze(2).broadcast_to([128, 64, 128]),
                        trib.ap().unsqueeze(1).broadcast_to([128, 64, 128]), op=OP.mult),
                        reads=("dtokb0", "trib"), writes=("dtap",))
                    for g4 in range(16):
                        pd, pdk = p_dec[g4 % 2], f"p_dec{g4 % 2}"

                        def decmm(t, g4=g4, pd=pd):
                            t.matmul(pd[:, :], onesb[:, :], dtap[:, 512 * g4:512 * (g4 + 1)],
                                     start=True, stop=False)
                            t.matmul(pd[:, :], ntrilb[:, :],
                                     dtokb[:, 4 * g4:4 * g4 + 4].unsqueeze(2)
                                     .broadcast_to([128, 4, 128]), start=False, stop=False)
                            t.matmul(pd[:, :], ident[:, :],
                                     dtokb[:, 64 + 4 * g4:64 + 4 * g4 + 4].unsqueeze(2)
                                     .broadcast_to([128, 4, 128]), start=False, stop=False)
                            return t.matmul(pd[:, :], nbigb[:, :],
                                            ident.ap().unsqueeze(1).broadcast_to([128, 4, 128]),
                                            start=False, stop=True)
                        em.op("pe", decmm, reads=("onesb", "dtap", "ntrilb", "dtokb0",
                                                  "dtokb1", "ident", "nbigb", pdk), writes=(pdk,))
                        db, dbk = decb[g4 % 2], f"decb{g4 % 2}"
                        em.op("act", lambda s, pd=pd, db=db: s.activation(db[:, :], pd[:, :], AF.Exp),
                              reads=(pdk,), writes=(dbk,))
                        em.op("dve", lambda v, db=db, g4=g4: v.tensor_tensor(
                            mpp[:, 512 * g4:512 * (g4 + 1)].rearrange("p (h i) -> p h i", h=4),
                            db.ap().rearrange("p (h i) -> p h i", h=4),
                            cbsb.ap().unsqueeze(1).broadcast_to([128, 4, 128]), op=OP.mult),
                            reads=(dbk, "cbsb"), writes=("mpp",))
                for g8 in range(8):
                    if own:
                        py1, k1 = p_y1[g8 % 2], f"p_y1{g8 % 2}"
                        py2, k2 = p_y2[g8 % 2], f"p_y2{g8 % 2}"

                        def ymm(t, g8=g8, py1=py1):
                            t.matmul(py1[:, :], ident[:, :], xd[:, 512 * g8:512 * (g8 + 1)],
                                     start=True, stop=False)
                            for hl in range(8):
                                h = 8 * g8 + hl
                                ins = t.matmul(py1[:, 64 * hl:64 * (hl + 1)],
                                               mpp[:, 128 * h:128 * (h + 1)],
                                               xt[:, 64 * h:64 * (h + 1)],
                                               start=False, stop=(hl == 7))
                            return ins
                        em.op("pe", ymm, reads=("ident", "xd", "mpp", xk, k1), writes=(k1,))
                        em.op("pe", lambda t, py2=py2, g8=g8, c=c: t.matmul(
                            py2[:, :], c_res[:, 128 * c:128 * (c + 1)],
                            sbf[:, 512 * g8:512 * (g8 + 1)], start=True, stop=True),
                            reads=("c_res", "sbf", k2), writes=(k2,))
                        yb, ybk = ybuf[0], "ybuf0"
                        em.op("dve", lambda v, py2=py2, g8=g8: v.tensor_tensor(
                            evtmp[:, :], py2[:, :], erep[:, 512 * g8:512 * (g8 + 1)],
                            op=OP.mult),
                            reads=(k2, "erep"), writes=("evtmp",))
                        em.op("dve", lambda v, py1=py1, yb=yb, g8=g8: v.tensor_tensor(
                            yb[:, 512 * g8:512 * (g8 + 1)], evtmp[:, :], py1[:, :], op=OP.add),
                            reads=("evtmp", k1), writes=(ybk,))
                    em.op("pe", lambda t, g8=g8: t.matmul(
                        p_s3[:, :], btok[:, :], xw[:, 512 * g8:512 * (g8 + 1)],
                        start=True, stop=True),
                        reads=("btok", "xw", "p_s3"), writes=("p_s3",))
                    em.op("dve", lambda v, g8=g8: v.tensor_tensor(
                        s_st[:, 512 * g8:512 * (g8 + 1)], s_st[:, 512 * g8:512 * (g8 + 1)],
                        arep[:, 512 * g8:512 * (g8 + 1)], op=OP.mult),
                        reads=("arep", "s_st"), writes=("s_st",))
                    em.op("dve", lambda v, g8=g8: v.tensor_tensor(
                        s_st[:, 512 * g8:512 * (g8 + 1)], s_st[:, 512 * g8:512 * (g8 + 1)],
                        p_s3[:, :], op=OP.add),
                        reads=("p_s3", "s_st"), writes=("s_st",))
                if own:
                    em.dma("qact", y_sp[128 * (c - KFOR):128 * (c - KFOR + 1), :],
                           ybuf[0][:, :], reads=("ybuf0",), writes=("y_sp",))
                if debug and c == KFOR:
                    em.op("dve", lambda v: v.tensor_copy(evtmp[:, 0:64], dbc[:, :]),
                          reads=("dbc",), writes=("evtmp",))
                    em.op("dve", lambda v: v.tensor_copy(
                        evtmp.ap()[:, 64:256].rearrange("p (h q) -> p h q", q=3),
                        dtok[:, 192:256].unsqueeze(2).broadcast_to([128, 64, 3])),
                        reads=("dtok3",), writes=("evtmp",))
                    em.dma("qsp", dbg_a[:, 256:512], evtmp[:, 0:256], reads=("evtmp",), writes=("dbg_a",))
                    em.dma("qsp", dbg_a[:, 0:256], dtok[:, :],
                           reads=("dtok0", "dtok1", "dtok2", "dtok3"), writes=("dbg_a",))
                    em.dma("qsp", dbg_a[0:64, 1024:2048], dch[0:64, :],
                           reads=("dtf", "dta", "lnd", "dls", "wdt", "cum"), writes=("dbg_a",))
                    em.dma("qsp", dbg_b[:, 0:4096], xtok[c % 2][:, :], reads=(f"xtok{c % 2}",), writes=("dbg_b",))
                    em.dma("qsp", dbg_b[:, 4096:8192], xd[:, :], reads=("xd",), writes=("dbg_b",))
                    em.dma("qsp", dbg_c[:, 0:4096], xw[:, :], reads=("xw",), writes=("dbg_c",))
                    em.dma("qsp", dbg_c[:, 4096:4224], cbsb[:, :], reads=("cbsb",), writes=("dbg_c",))
                    em.dma("qsp", dbg_c[:, 4224:4352], btok[:, :], reads=("btok",), writes=("dbg_c",))
                    em.dma("qsp", dbg_c[:, 4352:4480], dtokb[:, :], reads=("dtokb0", "dtokb1"), writes=("dbg_c",))

                if c == KFOR - 1:
                    em.op("dve", lambda v: v.tensor_scalar_mul(s_st[:, :], s_st[:, :],
                                                               small[:, 8:9]),
                          reads=("small", "s_st"), writes=("s_st",))
                want_s = (c + 1) * len(sign_jobs) // NCH
                while sign_i < want_s:
                    while sign_d < min(sign_i + 2, len(sign_jobs)):
                        sign_dma(sign_d)
                        sign_d += 1
                    sign_emit(sign_i)
                    sign_i += 1

            for part in range(8):
                em.dma("qpool", woutb_sp[512 * part:512 * (part + 1), :],
                       w_out[512 * part:512 * (part + 1), :], writes=("woutb_sp",))
            while sign_i < len(sign_jobs):
                while sign_d < min(sign_i + 2, len(sign_jobs)):
                    sign_dma(sign_d)
                    sign_d += 1
                sign_emit(sign_i)
                sign_i += 1

        scAP3.close()
        em.barrier()

        with ExitStack() as sc4a:
            nwb = sc4a.enter_context(nc.sbuf_tensor("nwb", [128, 4096], BF16))
            nwrow = sc4a.enter_context(nc.sbuf_tensor("nwrow", [1, 512], F32))
            p_nw = sc4a.enter_context(nc.psum_tensor("p_nw", [128, 512], F32))
            for j in range(8):
                em.dma("qsp", nwrow[0:1, :], norm_w[512 * j:512 * (j + 1)].unsqueeze(0),
                       writes=("nwrow",))
                em.op("pe", lambda t, j=j: t.matmul(p_nw[:, :], ones32[0:1, :],
                                                    nwrow[0:1, :], start=True, stop=True),
                      reads=("ones32", "nwrow", "p_nw"), writes=("p_nw",))
                em.op("dve", lambda v, j=j: v.tensor_copy(nwb[:, 512 * j:512 * (j + 1)],
                                                          p_nw[:, :]),
                      reads=("p_nw",), writes=("nwb",))
            yb4 = [sc4a.enter_context(nc.sbuf_tensor(f"y4_{i}", [128, 4096], BF16)) for i in range(2)]
            zb4 = [sc4a.enter_context(nc.sbuf_tensor(f"z4_{i}", [128, 4096], BF16)) for i in range(2)]
            tb4 = [sc4a.enter_context(nc.sbuf_tensor(f"tb_{i}", [128, 4096], BF16)) for i in range(2)]
            tsq4 = [sc4a.enter_context(nc.sbuf_tensor(f"tsq_{i}", [128, 4096], BF16)) for i in range(2)]
            tn4 = [sc4a.enter_context(nc.sbuf_tensor(f"tn_{i}", [128, 4096], BF16)) for i in range(2)]
            cla = sc4a.enter_context(nc.sbuf_tensor("cla", [128, 8], F32))
            for m in range(16):
                i2 = m % 2
                yb, zb, tb, tsq, tn = yb4[i2], zb4[i2], tb4[i2], tsq4[i2], tn4[i2]
                yk, zk, tbk, tqk, tnk = (f"y4_{i2}", f"z4_{i2}", f"tb_{i2}", f"tsq_{i2}", f"tn_{i2}")
                c0 = 4 * i2
                ck = f"cla{i2}"
                em.dma("qsp", yb[:, :], y_sp[128 * m:128 * (m + 1), :],
                       reads=("y_sp",), writes=(yk,))
                for zc in range(8):
                    em.dma("qsp", zb[:, 512 * zc:512 * (zc + 1)],
                           zsil_sp[zc, 128 * m:128 * (m + 1), :],
                           reads=("zsil_sp",), writes=(zk,))
                em.op("dve", lambda v, yb=yb, zb=zb, tb=tb: v.tensor_tensor(
                    tb[:, :], yb[:, :], zb[:, :], op=OP.mult),
                    reads=(yk, zk), writes=(tbk,))
                em.op("act", lambda s, tb=tb, tsq=tsq, c0=c0: s.activation(
                    tsq[:, :], tb[:, :], AF.Square, accum_out=cla[:, c0:c0 + 1]),
                    reads=(tbk,), writes=(tqk, ck))
                em.op("act", lambda s, c0=c0: s.activation(cla[:, c0 + 1:c0 + 2],
                                                           cla[:, c0:c0 + 1], AF.Ln,
                                                           bias=small[:, 7:8],
                                                           scale=1.0 / D_INNER),
                      reads=(ck, "small"), writes=(ck,))
                em.op("dve", lambda v, c0=c0: v.tensor_scalar_mul(cla[:, c0 + 1:c0 + 2],
                                                                  cla[:, c0 + 1:c0 + 2], -0.5),
                      reads=(ck,), writes=(ck,))
                em.op("act", lambda s, c0=c0: s.activation(cla[:, c0 + 2:c0 + 3],
                                                           cla[:, c0 + 1:c0 + 2], AF.Exp),
                      reads=(ck,), writes=(ck,))
                em.op("dve", lambda v, tb=tb, tn=tn, c0=c0: v.scalar_tensor_tensor(
                    tn[:, :], tb[:, :], cla[:, c0 + 2:c0 + 3], nwb[:, :],
                    op0=OP.mult, op1=OP.mult),
                    reads=(tbk, ck, "nwb"), writes=(tnk,))
                em.dma("qact", tn_sp[128 * m:128 * (m + 1), :], tn[:, :],
                       reads=(tnk,), writes=("tn_sp",))

        em.barrier()

        with ExitStack() as sc4:
            woutb = sc4.enter_context(nc.sbuf_tensor("woutb", [128, 32 * 2048], BF16))
            for part in range(8):
                em.dma("qsp",
                       woutb.ap().rearrange("p (k n) -> p k n", k=32)[:, 4 * part:4 * (part + 1), :],
                       woutb_sp.rearrange("(k p) n -> p k n", p=128)[:, 4 * part:4 * (part + 1), :],
                       reads=("woutb_sp",), writes=("woutb",))
            tnl = [sc4.enter_context(nc.sbuf_tensor(f"tnl{i}", [128, 4096], BF16)) for i in range(2)]
            ub4 = [sc4.enter_context(nc.sbuf_tensor(f"u4_{i}", [128, 2048], BF16)) for i in range(2)]
            tT4 = [sc4.enter_context(nc.sbuf_tensor(f"tT{i}", [128, 4096], BF16)) for i in range(2)]
            x1p4 = [sc4.enter_context(nc.sbuf_tensor(f"x1p{i}", [128, 2048], F32)) for i in range(2)]
            sqb = [sc4.enter_context(nc.sbuf_tensor(f"sqb{i}", [128, 2048], BF16)) for i in range(2)]
            x1b4 = [sc4.enter_context(nc.sbuf_tensor(f"x1b{i}", [128, 2048], BF16)) for i in range(2)]
            clb = sc4.enter_context(nc.sbuf_tensor("clb", [128, 12], F32))
            p_w = [sc4.enter_context(nc.psum_tensor(f"p_w{i}", [128, 512], F32)) for i in range(2)]
            p_tr32 = [sc4.enter_context(nc.psum_tensor(f"p_tr{i}", [128, 512], F32)) for i in range(2)]
            p_tr = [p.ap().bitcast(BF16) for p in p_tr32]

            for m in range(16):
                i2 = m % 2
                tn, ub, tT, x1pre, x1b, sq = tnl[i2], ub4[i2], tT4[i2], x1p4[i2], x1b4[i2], sqb[i2]
                tnk, uk, tTk, xpk, xbk, sqk = (f"tnl{i2}", f"u4_{i2}", f"tT{i2}",
                                               f"x1p{i2}", f"x1b{i2}", f"sqb{i2}")
                c0 = 6 * i2
                ck = f"clb{i2}"
                em.dma("qsp", tn[:, :], tn_sp[128 * m:128 * (m + 1), :],
                       reads=("tn_sp",), writes=(tnk,))
                em.dma("qpool", ub[:, :], u_own[128 * m:128 * (m + 1), :], writes=(uk,))
                for tt in range(8):
                    ptr, pk = p_tr[tt % 2], f"p_tr{tt % 2}"

                    def trg(t, tt=tt, ptr=ptr, tn=tn):
                        for qq in range(4):
                            j = 4 * tt + qq
                            ins = t.transpose(ptr[:, 128 * qq:128 * (qq + 1)],
                                              tn[:, 128 * j:128 * (j + 1)], ident[:, :])
                        return ins
                    em.op("pe", trg, reads=(tnk, "ident", pk), writes=(pk,))
                    em.op("act", lambda s, tt=tt, ptr=ptr, tT=tT: s.copy(
                        tT[:, 512 * tt:512 * (tt + 1)], ptr[:, 0:512]),
                        reads=(pk,), writes=(tTk,))
                for nchk in range(4):
                    pw, pk = p_w[nchk % 2], f"p_w{nchk % 2}"

                    def wmm(t, nchk=nchk, pw=pw, tT=tT):
                        for k in range(32):
                            ins = t.matmul(pw[:, :], tT[:, 128 * k:128 * (k + 1)],
                                           woutb[:, 2048 * k + 512 * nchk:2048 * k + 512 * (nchk + 1)],
                                           start=(k == 0), stop=(k == 31))
                        return ins
                    em.op("pe", wmm, reads=(tTk, "woutb", pk), writes=(pk,))
                    em.op("dve", lambda v, nchk=nchk, pw=pw, ub=ub, x1pre=x1pre: v.tensor_tensor(
                        x1pre[:, 512 * nchk:512 * (nchk + 1)], pw[:, :],
                        ub[:, 512 * nchk:512 * (nchk + 1)], op=OP.add),
                        reads=(pk, uk), writes=(xpk,))
                em.op("dve", lambda v, x1pre=x1pre, c0=c0: v.tensor_reduce(
                    clb[:, c0:c0 + 1], x1pre[:, :], axis=AX.X, op=OP.add),
                    reads=(xpk,), writes=(ck,))
                em.op("dve", lambda v, c0=c0: v.tensor_scalar_mul(clb[:, c0:c0 + 1],
                                                                  clb[:, c0:c0 + 1], 1.0 / D_MODEL),
                      reads=(ck,), writes=(ck,))
                em.op("act", lambda s, x1pre=x1pre, sq=sq, c0=c0: s.activation(
                    sq[:, :], x1pre[:, :], AF.Square, accum_out=clb[:, c0 + 1:c0 + 2]),
                    reads=(xpk,), writes=(sqk, ck))
                em.op("dve", lambda v, c0=c0: v.tensor_scalar_mul(clb[:, c0 + 1:c0 + 2],
                                                                  clb[:, c0 + 1:c0 + 2], 1.0 / D_MODEL),
                      reads=(ck,), writes=(ck,))
                em.op("dve", lambda v, c0=c0: v.scalar_tensor_tensor(
                    clb[:, c0 + 2:c0 + 3], clb[:, c0:c0 + 1], clb[:, c0:c0 + 1],
                    clb[:, c0 + 1:c0 + 2], op0=OP.mult, op1=OP.subtract),
                    reads=(ck,), writes=(ck,))
                em.op("dve", lambda v, c0=c0: v.tensor_scalar_mul(clb[:, c0 + 2:c0 + 3],
                                                                  clb[:, c0 + 2:c0 + 3], -1.0),
                      reads=(ck,), writes=(ck,))
                em.op("act", lambda s, c0=c0: s.activation(clb[:, c0 + 3:c0 + 4],
                                                           clb[:, c0 + 2:c0 + 3], AF.Ln,
                                                           bias=small[:, 7:8], scale=1.0),
                      reads=(ck, "small"), writes=(ck,))
                em.op("dve", lambda v, c0=c0: v.tensor_scalar_mul(clb[:, c0 + 3:c0 + 4],
                                                                  clb[:, c0 + 3:c0 + 4], -0.5),
                      reads=(ck,), writes=(ck,))
                em.op("act", lambda s, c0=c0: s.activation(clb[:, c0 + 3:c0 + 4],
                                                           clb[:, c0 + 3:c0 + 4], AF.Exp),
                      reads=(ck,), writes=(ck,))
                em.op("dve", lambda v, c0=c0: v.scalar_tensor_tensor(
                    clb[:, c0 + 4:c0 + 5], clb[:, c0:c0 + 1], -1.0, clb[:, c0 + 3:c0 + 4],
                    op0=OP.mult, op1=OP.mult),
                    reads=(ck,), writes=(ck,))
                em.op("dve", lambda v, x1pre=x1pre, x1b=x1b, c0=c0: v.tensor_scalar(
                    x1b[:, :], x1pre[:, :], clb[:, c0 + 3:c0 + 4], clb[:, c0 + 4:c0 + 5],
                    op0=OP.mult, op1=OP.add),
                    reads=(xpk, ck), writes=(xbk,))
                em.dma("qact", x1_sp[128 * m:128 * (m + 1), :], x1b[:, :], reads=(xbk,),
                       writes=("x1_sp",))

        em.barrier()

        qcols = stack.enter_context(nc.sbuf_tensor("qcols", [128, 64], F32))
        p_f1 = [stack.enter_context(nc.psum_tensor(f"p_f1{i}", [128, 512], F32)) for i in range(4)]
        p_tq32 = [stack.enter_context(nc.psum_tensor(f"p_tq{i}", [128, 512], F32)) for i in range(2)]
        p_tq = [p.ap().bitcast(BF16) for p in p_tq32]

        with ExitStack() as scE1:
            xqT = scE1.enter_context(nc.sbuf_tensor("xqT", [128, 16 * 2048], BF16))
            x1l = [scE1.enter_context(nc.sbuf_tensor(f"x1l{i}", [128, 2048], BF16)) for i in range(2)]
            hgrp = [scE1.enter_context(nc.sbuf_tensor(f"hgrp{i}", [128, 8192], BF16)) for i in range(4)]
            w1qb = [scE1.enter_context(nc.sbuf_tensor(f"w1qb{i}", [128, 8192], BF16)) for i in range(2)]
            y2b = scE1.enter_context(nc.sbuf_tensor("y2b", [128, 2048], F32))
            mfr = scE1.enter_context(nc.sbuf_tensor("mfr", [128, 2048], F32))
            ibuf = scE1.enter_context(nc.sbuf_tensor("ibuf", [128, 2048], mybir.dt.int32))
            qfb = scE1.enter_context(nc.sbuf_tensor("qfb", [128, 2048], F32))
            hqs = scE1.enter_context(nc.sbuf_tensor("hqs", [128, 2048], BF16))


            y2c = y2b.ap().bitcast(BF16)
            mfc = mfr.ap().bitcast(BF16)

            def emit_round(in_ap, in_key, sc_key, sc_ap, dst_ap, dst_key):
                em.op("dve", lambda v: v.tensor_scalar_mul(y2b[:, :], in_ap, sc_ap),
                      reads=(in_key, sc_key), writes=("y2b",))
                em.op("dve", lambda v: v.tensor_copy(ibuf[:, :], y2b[:, :]),
                      reads=("y2b",), writes=("ibuf",))
                em.op("dve", lambda v: v.tensor_copy(qfb[:, :], ibuf[:, :]),
                      reads=("ibuf",), writes=("qfb",))
                em.op("dve", lambda v: v.scalar_tensor_tensor(mfr[:, :], qfb[:, :], -1.0,
                                                              y2b[:, :], op0=OP.mult, op1=OP.add),
                      reads=("qfb", "y2b"), writes=("mfr",))
                em.op("act", lambda s: s.activation(y2c[:, 0:2048], mfr[:, :], AF.Sign,
                                                    bias=small[:, 9:10], scale=1.0),
                      reads=("mfr", "small"), writes=("y2b",))
                em.op("act", lambda s: s.activation(y2c[:, 2048:4096], mfr[:, :], AF.Sign,
                                                    bias=small[:, 5:6], scale=1.0),
                      reads=("mfr", "small"), writes=("y2b",))
                em.op("dve", lambda v: v.tensor_tensor(mfc[:, 0:2048], y2c[:, 0:2048],
                                                       y2c[:, 2048:4096], op=OP.add),
                      reads=("y2b",), writes=("mfr",))
                em.op("dve", lambda v: v.scalar_tensor_tensor(dst_ap, mfc[:, 0:2048], 0.5,
                                                              qfb[:, :], op0=OP.mult, op1=OP.add),
                      reads=("mfr", "qfb"), writes=(dst_key,))

            for q in range(4):
                for mi in range(4):
                    m = 4 * q + mi
                    xb, xk = x1l[mi % 2], f"x1l{mi % 2}"
                    em.dma("qsp", xb[:, :], x1_sp[128 * m:128 * (m + 1), :],
                           reads=("x1_sp",), writes=(xk,))
                    em.op("dve", lambda v, xb=xb, m=m: v.reduce_max(
                        qcols[:, m:m + 1], xb[:, :], axis=AX.X,
                        apply_absolute_value=True),
                        reads=(xk,), writes=("qamax",))
                    em.op("dve", lambda v, m=m: v.tensor_scalar_max(
                        qcols[:, 48:49], qcols[:, m:m + 1], EPS),
                        reads=("qamax",), writes=("q48",))
                    em.op("dve", lambda v: v.reciprocal(qcols[:, 49:50], qcols[:, 48:49]),
                          reads=("q48",), writes=("q49",))
                    em.op("dve", lambda v: v.tensor_scalar_mul(qcols[:, 49:50],
                                                               qcols[:, 49:50], 127.0),
                          reads=("q49",), writes=("q49",))
                    em.op("dve", lambda v, m=m: v.tensor_tensor(
                        qcols[:, 16 + m:17 + m], qcols[:, 48:49], small[:, 12:13], op=OP.mult),
                        reads=("q48", "small"), writes=("qc1",))
                    emit_round(xb[:, :], xk, "q49", qcols[:, 49:50], hqs[:, :],
                               "hqsAB")
                    for tt in range(4):
                        ptq, pk = p_tq[tt % 2], f"p_tq{tt % 2}"

                        def xtr(t, tt=tt, ptq=ptq):
                            for qq in range(4):
                                j = 4 * tt + qq
                                ins = t.transpose(ptq[:, 128 * qq:128 * (qq + 1)],
                                                  hqs[:, 128 * j:128 * (j + 1)], ident[:, :])
                            return ins
                        em.op("pe", xtr, reads=("hqsAB", "ident", pk), writes=(pk,))

                        def xev(v, tt=tt, ptq=ptq, m=m):
                            for qq in range(4):
                                k = 4 * tt + qq
                                ins = v.tensor_copy(
                                    xqT[:, 2048 * k + 128 * m:2048 * k + 128 * (m + 1)],
                                    ptq[:, 128 * qq:128 * (qq + 1)])
                            return ins
                        em.op("dve", xev, reads=(pk,), writes=("xqT",))
                if q == 0:
                    em.dma("qpool", w1qb[0][:, :], w1q_sp[0, :, :], reads=("wq_sp",),
                           writes=("w1qb0",))
                for fc in range(16):
                    wq, wk = w1qb[fc % 2], f"w1qb{fc % 2}"
                    nfc = fc + 1 if fc < 15 else 0
                    em.dma("qpool", w1qb[nfc % 2][:, :], w1q_sp[nfc, :, :], reads=("wq_sp",),
                           writes=(f"w1qb{nfc % 2}",))
                    for mi in range(4):
                        m = 4 * q + mi
                        pf, pk = p_f1[mi], f"p_f1{mi}"

                        def f1mm(t, wq=wq, m=m, pf=pf):
                            for k in range(16):
                                ins = t.matmul(pf[:, :],
                                               xqT[:, 2048 * k + 128 * m:2048 * k + 128 * (m + 1)],
                                               wq[:, 512 * k:512 * (k + 1)],
                                               start=(k == 0), stop=(k == 15))
                            return ins
                        em.op("pe", f1mm, reads=("xqT", wk, pk), writes=(pk,))
                        em.op("act", lambda s, pf=pf, mi=mi, m=m, fc=fc: s.activation(
                            hgrp[mi][:, 512 * fc:512 * (fc + 1)], pf[:, :], AF.Gelu,
                            bias=small[:, 18:19], scale=qcols[:, 16 + m:17 + m]),
                            reads=(pk, "qc1", "small"), writes=(f"hgrp{mi}",))
                if debug and q == 3:
                    em.dma("qsp", dbg_b[:, :], hgrp[0][:, :], reads=("hgrp0",), writes=("dbg_b",))
                    em.dma("qsp", dbg_a[:, 0:64], qcols[:, :], reads=("qamax", "qc1", "qc2", "q48", "q49", "q50", "q51"), writes=("dbg_a",))
                for mi in range(4):
                    m = 4 * q + mi
                    hg, hk = hgrp[mi], f"hgrp{mi}"
                    em.op("dve", lambda v, hg=hg: v.reduce_max(
                        qcols[:, 50:51], hg[:, :], axis=AX.X,
                        apply_absolute_value=True),
                        reads=(hk,), writes=("q50",))
                    em.op("dve", lambda v: v.tensor_scalar_max(qcols[:, 50:51], qcols[:, 50:51],
                                                               EPS),
                          reads=("q50",), writes=("q50",))
                    em.op("dve", lambda v: v.reciprocal(qcols[:, 51:52], qcols[:, 50:51]),
                          reads=("q50",), writes=("q51",))
                    em.op("dve", lambda v: v.tensor_scalar_mul(qcols[:, 51:52],
                                                               qcols[:, 51:52], 127.0),
                          reads=("q51",), writes=("q51",))
                    em.op("dve", lambda v, m=m: v.tensor_tensor(
                        qcols[:, 32 + m:33 + m], qcols[:, 50:51], small[:, 13:14], op=OP.mult),
                        reads=("q50", "small"), writes=("qc2",))
                    for piece in range(4):
                        sl = slice(2048 * piece, 2048 * (piece + 1))
                        emit_round(hg[:, sl], hk, "q51", qcols[:, 51:52], hg[:, sl], hk)
                    for tt in range(16):
                        ptq, pk = p_tq[tt % 2], f"p_tq{tt % 2}"

                        def htr(t, tt=tt, ptq=ptq, hg=hg):
                            for qq in range(4):
                                j = 4 * tt + qq
                                ins = t.transpose(ptq[:, 128 * qq:128 * (qq + 1)],
                                                  hg[:, 128 * j:128 * (j + 1)], ident[:, :])
                            return ins
                        em.op("pe", htr, reads=(hk, "ident", pk), writes=(pk,))
                        st_off = 512 * (tt % 2)
                        em.op("dve", lambda v, ptq=ptq, st_off=st_off: v.tensor_copy(
                            hqs[:, st_off:st_off + 512], ptq[:, 0:512]),
                            reads=(pk,), writes=("hqsAB",))
                        em.dma("qact",
                               hqT_sp.rearrange("(b p) t -> b p t", p=128)
                               [4 * tt:4 * (tt + 1), :, 128 * m:128 * (m + 1)]
                               .transpose([1, 0, 2]),
                               hqs.ap().rearrange("p (b t) -> p b t", b=16)
                               [:, 4 * (tt % 2):4 * (tt % 2 + 1), :],
                               reads=("hqsAB",), writes=("hqT_sp",))

        em.barrier()

        with ExitStack() as scE2:
            hqTl = scE2.enter_context(nc.sbuf_tensor("hqTl", [128, 64 * 512], BF16))
            w2qb = [scE2.enter_context(nc.sbuf_tensor(f"w2qb{i}", [128, 16 * 512], BF16)) for i in range(2)]
            x2pre = [scE2.enter_context(nc.sbuf_tensor(f"x2pre{i}", [128, 2048], F32)) for i in range(4)]
            x1q = [scE2.enter_context(nc.sbuf_tensor(f"x1q{i}", [128, 2048], BF16)) for i in range(4)]
            outb = scE2.enter_context(nc.sbuf_tensor("outb", [128, 2048], F32))
            sq2 = scE2.enter_context(nc.sbuf_tensor("sq2", [128, 2048], BF16))
            for q in range(4):
                for k in range(64):
                    em.dma("qsp", hqTl[:, 512 * k:512 * (k + 1)],
                           hqT_sp[128 * k:128 * (k + 1), 512 * q:512 * (q + 1)],
                           reads=("hqT_sp",), writes=("hqTl",))
                for mi in range(4):
                    m = 4 * q + mi
                    em.dma("qsp", x1q[mi][:, :], x1_sp[128 * m:128 * (m + 1), :],
                           reads=("x1_sp",), writes=(f"x1q{mi}",))
                if q == 0:
                    em.dma("qpool", w2qb[0].ap().rearrange("p (k f) -> p k f", k=16),
                           w2q_sp[0, 0:16, :, :].rearrange("k p f -> p k f"),
                           reads=("wq_sp",), writes=("w2qb0",))
                for d2c in range(4):
                    for ks in range(4):
                        st16 = 4 * d2c + ks
                        wq, wk = w2qb[st16 % 2], f"w2qb{st16 % 2}"
                        nst = (st16 + 1) % 16
                        nd2c, nks = nst // 4, nst % 4
                        em.dma("qpool", w2qb[(st16 + 1) % 2].ap().rearrange("p (k f) -> p k f", k=16),
                               w2q_sp[nd2c, 16 * nks:16 * (nks + 1), :, :].rearrange("k p f -> p k f"),
                               reads=("wq_sp",), writes=(f"w2qb{(st16 + 1) % 2}",))
                        for mi in range(4):
                            pf, pk = p_f1[mi], f"p_f1{mi}"

                            def f2mm(t, wq=wq, ks=ks, mi=mi, pf=pf):
                                for kk in range(16):
                                    k = 16 * ks + kk
                                    ins = t.matmul(
                                        pf[:, :],
                                        hqTl[:, 512 * k + 128 * mi:512 * k + 128 * (mi + 1)],
                                        wq[:, 512 * kk:512 * (kk + 1)],
                                        start=(k == 0), stop=(k == 63))
                                return ins
                            em.op("pe", f2mm, reads=("hqTl", wk, pk), writes=(pk,))
                    for mi in range(4):
                        m = 4 * q + mi
                        pf, pk = p_f1[mi], f"p_f1{mi}"
                        em.op("dve", lambda v, pf=pf, mi=mi, d2c=d2c, m=m: v.scalar_tensor_tensor(
                            x2pre[mi][:, 512 * d2c:512 * (d2c + 1)], pf[:, :],
                            qcols[:, 32 + m:33 + m], x1q[mi][:, 512 * d2c:512 * (d2c + 1)],
                            op0=OP.mult, op1=OP.add),
                            reads=(pk, "qc2", f"x1q{mi}"), writes=(f"x2pre{mi}",))
                for mi in range(4):
                    m = 4 * q + mi
                    xp, xk = x2pre[mi], f"x2pre{mi}"
                    em.op("dve", lambda v, xp=xp: v.tensor_reduce(qcols[:, 52:53], xp[:, :],
                                                                  axis=AX.X, op=OP.add),
                          reads=(xk,), writes=("q52",))
                    em.op("dve", lambda v: v.tensor_scalar_mul(qcols[:, 52:53], qcols[:, 52:53],
                                                               1.0 / D_MODEL),
                          reads=("q52",), writes=("q52",))
                    em.op("act", lambda s, xp=xp: s.activation(sq2[:, :], xp[:, :], AF.Square,
                                                               accum_out=qcols[:, 53:54]),
                          reads=(xk,), writes=("sq2", "q53"))
                    em.op("dve", lambda v: v.tensor_scalar_mul(qcols[:, 53:54], qcols[:, 53:54],
                                                               1.0 / D_MODEL),
                          reads=("q53",), writes=("q53",))
                    em.op("dve", lambda v: v.scalar_tensor_tensor(qcols[:, 54:55], qcols[:, 52:53],
                                                                  qcols[:, 52:53], qcols[:, 53:54],
                                                                  op0=OP.mult, op1=OP.subtract),
                          reads=("q52", "q53"), writes=("q54",))
                    em.op("dve", lambda v: v.tensor_scalar_mul(qcols[:, 54:55], qcols[:, 54:55],
                                                               -1.0),
                          reads=("q54",), writes=("q54",))
                    em.op("act", lambda s: s.activation(qcols[:, 54:55], qcols[:, 54:55], AF.Ln,
                                                        bias=small[:, 7:8], scale=1.0),
                          reads=("q54", "small"), writes=("q54",))
                    em.op("dve", lambda v: v.tensor_scalar_mul(qcols[:, 54:55], qcols[:, 54:55],
                                                               -0.5),
                          reads=("q54",), writes=("q54",))
                    em.op("act", lambda s: s.activation(qcols[:, 54:55], qcols[:, 54:55], AF.Exp),
                          reads=("q54",), writes=("q54",))
                    em.op("dve", lambda v: v.scalar_tensor_tensor(qcols[:, 55:56], qcols[:, 52:53],
                                                                  -1.0, qcols[:, 54:55],
                                                                  op0=OP.mult, op1=OP.mult),
                          reads=("q52", "q54"), writes=("q55",))
                    em.op("dve", lambda v, xp=xp: v.tensor_scalar(outb[:, :], xp[:, :],
                                                                  qcols[:, 54:55], qcols[:, 55:56],
                                                                  op0=OP.mult, op1=OP.add),
                          reads=(xk, "q54", "q55"), writes=("outb",))
                    em.dma("qact", out_t[128 * m:128 * (m + 1), :], outb[:, :], reads=("outb",),
                           writes=("out_t",))

        em.wait_all("sp")
        em.wait_all("pool")
        em.flush(block)
    stack.close()
    return nc


_NC_CACHE = {}


def kernel(**inputs):
    u = np.ascontiguousarray(np.asarray(inputs["L0_attn_u"], np.float32))
    W_in = np.ascontiguousarray(np.asarray(inputs["L0_attn_W_in"], np.float32))
    conv_w = np.asarray(inputs["L0_attn_conv_w"], np.float32)
    conv_b = np.asarray(inputs["L0_attn_conv_b"], np.float32)
    dt_bias = np.asarray(inputs["L0_attn_dt_bias"], np.float32)
    A_log = np.asarray(inputs["L0_attn_A_log"], np.float32)
    Dv = np.asarray(inputs["L0_attn_D"], np.float32)
    norm_w = np.asarray(inputs["L0_attn_norm_w"], np.float32)
    W_out = np.ascontiguousarray(np.asarray(inputs["L0_attn_W_out"], np.float32))
    w1 = np.asarray(inputs["L0_ffn_w1"], np.float32)
    w2 = np.asarray(inputs["L0_ffn_w2"], np.float32)

    wx_pad = np.zeros((D_MODEL, XB * 128), np.float32)
    wx_pad[:, :4416] = W_in[:, D_INNER:]
    w_in_x = np.ascontiguousarray(
        wx_pad.reshape(16, 128, XB, 128).transpose(2, 0, 1, 3))
    w_in_z = np.ascontiguousarray(W_in[:, :D_INNER])
    w1t = np.ascontiguousarray(w1.T)
    w2t = np.ascontiguousarray(w2.T)
    convw_pad = np.zeros((XB * 128, 4), np.float32)
    convw_pad[:4352] = conv_w
    convb_pad = np.zeros((XB * 128,), np.float32)
    convb_pad[:4352] = conv_b
    small_c = np.zeros((64, 4), np.float32)
    small_c[:, 0] = dt_bias
    small_c[:, 1] = A_log
    small_r = np.zeros((4, 64), np.float32)
    small_r[2] = Dv

    in_maps = []
    for c in range(8):
        b, h = c // 2, c % 2
        g0 = h * T_OWN
        ut = np.zeros((D_MODEL, T_ALL), np.float32)
        lo = g0 - (T_FOR + HALO)
        src_lo = max(lo, 0)
        ut[:, src_lo - lo:T_FOR + HALO] = u[b, src_lo:g0].T
        ut[:, T_FOR + HALO:] = u[b, g0:g0 + T_OWN].T
        mask = np.full((128, 1), 0.0 if h == 0 else 1.0, np.float32)
        in_maps.append(dict(
            u_t=np.ascontiguousarray(ut),
            u_own=np.ascontiguousarray(u[b, g0:g0 + T_OWN]),
            w_in_x=w_in_x, w_in_z=w_in_z, w_out=W_out, w1t=w1t, w2t=w2t,
            conv_wt=convw_pad, conv_bt=convb_pad, small_c=small_c,
            small_r=small_r, norm_w=norm_w, seq_mask=mask))

    if "nc" not in _NC_CACHE:
        _NC_CACHE["nc"] = build_program()
    nc = _NC_CACHE["nc"]
    res = run_bass_kernel_spmd(nc, in_maps, core_ids=list(range(8)))
    out = np.zeros((4, 4096, D_MODEL), np.float32)
    for c in range(8):
        b, h = c // 2, c % 2
        out[b, h * T_OWN:(h + 1) * T_OWN] = res.results[c]["out"]
    return out


# version 17 (final)
# speedup vs baseline: 1.1190x; 1.0145x over previous
import zlib
import numpy as np
import ml_dtypes
import concourse.bass as bass
import concourse.mybir as mybir
from concourse.bass_utils import run_bass_kernel_spmd
from contextlib import ExitStack

F32 = mybir.dt.float32
BF16 = mybir.dt.bfloat16
FP8 = mybir.dt.float8e4
AF = mybir.ActivationFunctionType
OP = mybir.AluOpType
AX = mybir.AxisListType
bfd = ml_dtypes.bfloat16

D_MODEL = 2048
D_INNER = 4096
CHUNK = 128
KFOR = 4
T_OWN = 2048
T_FOR = KFOR * CHUNK
HALO = 3
T_ALL = T_OWN + T_FOR + HALO
T_CONV = T_OWN + T_FOR
NCH = T_CONV // CHUNK
XB = 35
EPS = 1e-5
BIG = 8192.0

ENGINES = ("pe", "dve", "act", "pool", "sp")
NSUBQ = 6
QUEUES = tuple(f"qsp_{i}" for i in range(NSUBQ)) + ("qact",) + tuple(f"qpool_{i}" for i in range(NSUBQ))
Q_OWNER = {**{f"qsp_{i}": "sp" for i in range(NSUBQ)}, "qact": "act",
           **{f"qpool_{i}": "pool" for i in range(NSUBQ)}}


class Em:

    def __init__(self, nc):
        self.nc = nc
        self.sems = {}
        self.cnt = {s: 0 for s in ENGINES + QUEUES}
        self.pend = {e: [] for e in ENGINES}
        self.last_w = {}
        self.readers = {}
        self.wm = {}

    def init_sems(self, stack):
        for s in ENGINES + QUEUES:
            self.sems[s] = stack.enter_context(self.nc.semaphore("sem_" + s))

    SAME_ENGINE_EDGES = True

    def _need(self, eng, reads, writes):
        need = {}
        se = self.SAME_ENGINE_EDGES
        for k in reads:
            lw = self.last_w.get(k)
            if lw is not None and (se or lw[0] != eng):
                need[lw[0]] = max(need.get(lw[0], 0), lw[1])
        for k in writes:
            lw = self.last_w.get(k)
            if lw is not None and (se or lw[0] != eng):
                need[lw[0]] = max(need.get(lw[0], 0), lw[1])
            for st, c in self.readers.get(k, {}).items():
                if se or st != eng:
                    need[st] = max(need.get(st, 0), c)
        out = {}
        for st, c in need.items():
            if st in QUEUES:
                c = self.cnt[st]
            if st == eng:
                if c >= self.cnt[eng] + 1:
                    continue
            if self.wm.get((eng, st), 0) < c:
                out[st] = c
                self.wm[(eng, st)] = c
        return out

    def _track(self, stream, cnt, reads, writes):
        for k in reads:
            self.readers.setdefault(k, {})[stream] = cnt
        for k in writes:
            self.last_w[k] = (stream, cnt)
            self.readers[k] = {}

    def op(self, eng, fn, reads=(), writes=()):
        need = self._need(eng, reads, writes)
        self.cnt[eng] += 1
        cnt = self.cnt[eng]
        sems = self.sems

        def closure(e, need=need, fn=fn):
            for st, c in need.items():
                e.wait_ge(sems[st], c)
            fn(e).then_inc(sems[eng], 1)
        self.pend[eng].append(closure)
        self._track(eng, cnt, reads, writes)

    def _subq(self, q, writes):
        if q in ("qpool", "qsp"):
            tag = writes[0] if writes else "x"
            return f"{q}_{zlib.crc32(str(tag).encode()) % NSUBQ}"
        return q

    def dma(self, q, out, in_, reads=(), writes=(), **kw):
        q = self._subq(q, writes)
        eng = Q_OWNER[q]
        need = self._need(eng, reads, writes)
        self.cnt[q] += 16
        qc = self.cnt[q]
        sems = self.sems

        def closure(e, need=need, out=out, in_=in_, kw=kw):
            for st, c in need.items():
                e.wait_ge(sems[st], c)
            e.dma_start(out=out, in_=in_, **kw).then_inc(sems[q], 16)
        self.pend[eng].append(closure)
        self._track(q, qc, reads, writes)

    def dma_t(self, q, out, in_, reads=(), writes=()):
        q = self._subq(q, writes)
        eng = Q_OWNER[q]
        need = self._need(eng, reads, writes)
        self.cnt[q] += 16
        qc = self.cnt[q]
        sems = self.sems

        def closure(e, need=need, out=out, in_=in_):
            for st, c in need.items():
                e.wait_ge(sems[st], c)
            e.dma_start_transpose(out, in_).then_inc(sems[q], 16)
        self.pend[eng].append(closure)
        self._track(q, qc, reads, writes)

    def wait_all(self, eng):
        sems = self.sems
        vals = [(st, self.cnt[st]) for st in ENGINES + QUEUES
                if st != eng and self.cnt[st] > 0]

        def closure(e, vals=vals):
            for st, c in vals:
                e.wait_ge(sems[st], c)
        self.pend[eng].append(closure)
        for st, c in vals:
            self.wm[(eng, st)] = max(self.wm.get((eng, st), 0), c)

    def barrier(self):
        for e in ENGINES:
            self.wait_all(e)
        self.last_w = {}
        self.readers = {}

    def flush(self, block):
        for e, lst in self.pend.items():
            if not lst:
                continue
            body = {"pe": block.tensor, "dve": block.vector, "act": block.scalar,
                    "pool": block.gpsimd, "sp": block.sync}[e]

            def run(engine, lst=tuple(lst)):
                for c in lst:
                    c(engine)
            body(run)
            self.pend[e] = []


def build_program(debug=False):
    nc = bass.Bass("TRN2", target_bir_lowering=False)
    SCR = "ExternalOutput" if debug else "Internal"
    u_t = nc.dram_tensor("u_t", [D_MODEL, T_ALL], F32, kind="ExternalInput").ap()
    u_own = nc.dram_tensor("u_own", [T_OWN, D_MODEL], F32, kind="ExternalInput").ap()
    w_in_x = nc.dram_tensor("w_in_x", [XB, 16, 128, 128], F32, kind="ExternalInput").ap()
    w_in_z = nc.dram_tensor("w_in_z", [D_MODEL, D_INNER], F32, kind="ExternalInput").ap()
    w_out = nc.dram_tensor("w_out", [D_INNER, D_MODEL], F32, kind="ExternalInput").ap()
    w1t = nc.dram_tensor("w1t", [D_MODEL, 8192], F32, kind="ExternalInput").ap()
    w2t = nc.dram_tensor("w2t", [8192, D_MODEL], F32, kind="ExternalInput").ap()
    conv_wt = nc.dram_tensor("conv_wt", [XB * 128, 4], F32, kind="ExternalInput").ap()
    conv_bt = nc.dram_tensor("conv_bt", [XB * 128], F32, kind="ExternalInput").ap()
    small_c = nc.dram_tensor("small_c", [64, 4], F32, kind="ExternalInput").ap()
    small_r = nc.dram_tensor("small_r", [4, 64], F32, kind="ExternalInput").ap()
    norm_w = nc.dram_tensor("norm_w", [D_INNER], F32, kind="ExternalInput").ap()
    seq_mask = nc.dram_tensor("seq_mask", [128, 1], F32, kind="ExternalInput").ap()
    out_t = nc.dram_tensor("out", [T_OWN, D_MODEL], F32, kind="ExternalOutput").ap()

    x_sp = nc.dram_tensor("x_sp", [D_INNER, T_CONV], BF16, kind=SCR).ap()
    zsil_sp = nc.dram_tensor("zsil_sp", [8, T_OWN, 512], BF16, kind=SCR).ap()
    y_sp = nc.dram_tensor("y_sp", [T_OWN, D_INNER], BF16, kind=SCR).ap()
    x1_sp = nc.dram_tensor("x1_sp", [T_OWN, D_MODEL], BF16, kind=SCR).ap()
    woutb_sp = nc.dram_tensor("woutb_sp", [D_INNER, D_MODEL], BF16, kind=SCR).ap()
    w1q_sp = nc.dram_tensor("w1q_sp", [16, 128, 8192], FP8, kind=SCR).ap()
    w2q_sp = nc.dram_tensor("w2q_sp", [4, 64, 128, 512], FP8, kind=SCR).ap()
    hqT_sp = nc.dram_tensor("hqT_sp", [8192, T_OWN], BF16, kind=SCR).ap()
    tn_sp = nc.dram_tensor("tn_sp", [T_OWN, D_INNER], BF16, kind="Internal").ap()
    dbg_a = nc.dram_tensor("dbg_a", [128, 4096], F32, kind=SCR).ap()
    dbg_b = nc.dram_tensor("dbg_b", [128, 8192], BF16, kind=SCR).ap()
    dbg_c = nc.dram_tensor("dbg_c", [128, 8192], BF16, kind=SCR).ap()
    dbg_d = nc.dram_tensor("dbg_d", [128, 4096], F32, kind=SCR).ap()

    ii = np.arange(128)
    c_ident = nc.inline_tensor(np.eye(128, dtype=bfd), "c_ident").ap()
    c_ident32 = nc.inline_tensor(np.eye(128, dtype=np.float32), "c_ident32").ap()
    c_ones = nc.inline_tensor(np.ones((128, 128), dtype=bfd), "c_ones").ap()
    c_ones32 = nc.inline_tensor(np.ones((1, 128), dtype=np.float32), "c_ones32").ap()
    c_tri = nc.inline_tensor((ii[:, None] <= ii[None, :]).astype(bfd), "c_tri").ap()
    c_ntril = nc.inline_tensor((-(ii[:, None] <= ii[None, :]).astype(np.float32)).astype(bfd), "c_ntril").ap()
    c_nbig = nc.inline_tensor((-BIG * (ii[:, None] < ii[None, :]).astype(np.float32)).astype(bfd), "c_nbig").ap()

    stack = ExitStack()
    em = Em(nc)

    with nc.Block() as block:
        em.init_sems(stack)
        ident = stack.enter_context(nc.sbuf_tensor("ident", [128, 128], BF16))
        ident32 = stack.enter_context(nc.sbuf_tensor("ident32", [128, 128], F32))
        onesb = stack.enter_context(nc.sbuf_tensor("onesb", [128, 128], BF16))
        ones32 = stack.enter_context(nc.sbuf_tensor("ones32", [1, 128], F32))
        trib = stack.enter_context(nc.sbuf_tensor("trib", [128, 128], BF16))
        ntrilb = stack.enter_context(nc.sbuf_tensor("ntrilb", [128, 128], BF16))
        nbigb = stack.enter_context(nc.sbuf_tensor("nbigb", [128, 128], BF16))
        small = stack.enter_context(nc.sbuf_tensor("small", [128, 24], F32))
        dbc = stack.enter_context(nc.sbuf_tensor("dbc", [128, 64], BF16))
        drow = stack.enter_context(nc.sbuf_tensor("drow", [1, 64], BF16))
        convw = stack.enter_context(nc.sbuf_tensor("convw", [128, XB * 4], F32))
        convb = stack.enter_context(nc.sbuf_tensor("convb", [128, XB], F32))

        em.dma("qsp", ident[:, :], c_ident[:, :], writes=("ident",))
        em.dma("qsp", ident32[:, :], c_ident32[:, :], writes=("ident32",))
        em.dma("qsp", onesb[:, :], c_ones[:, :], writes=("onesb",))
        em.dma("qsp", ones32[:, :], c_ones32[:, :], writes=("ones32",))
        em.dma("qsp", trib[:, :], c_tri[:, :], writes=("trib",))
        em.dma("qsp", ntrilb[:, :], c_ntril[:, :], writes=("ntrilb",))
        em.dma("qsp", nbigb[:, :], c_nbig[:, :], writes=("nbigb",))
        em.dma("qsp", small[0:64, 0:2], small_c[:, 0:2], writes=("small",))
        em.dma("qsp", small[:, 8:9], seq_mask[:, :], writes=("small",))
        em.dma("qpool", drow[0:1, :], small_r[2:3, :], writes=("drow",))
        em.dma("qpool", convw.ap().rearrange("p (c k) -> p c k", k=4),
               conv_wt.rearrange("(c p) k -> p c k", p=128), writes=("convw",))
        em.dma("qsp", convb[:, :],
               conv_bt.rearrange("(c p) -> p c", p=128), writes=("convb",),
               allow_slow_non_contiguous=True)
        for col, val in ((4, 1.0), (5, 0.5), (6, 127.0), (7, EPS), (9, -0.5),
                         (14, 0.0), (15, 0.0), (18, 0.0)):
            em.op("dve", lambda v, col=col, val=val: v.memset(small[:, col:col + 1], val),
                  writes=("small",))
        em.op("act", lambda s: s.activation(small[0:64, 3:4], small[0:64, 1:2], AF.Exp),
              reads=("small",), writes=("small",))
        em.op("dve", lambda v: v.tensor_scalar_mul(small[0:64, 3:4], small[0:64, 3:4], -1.0),
              reads=("small",), writes=("small",))
        with ExitStack() as sc0:
            p0 = sc0.enter_context(nc.psum_tensor("p0t", [128, 64], F32))
            em.op("pe", lambda t: t.matmul(p0[:, 0:64], onesb[0:1, :], drow[0:1, :],
                                           start=True, stop=True),
                  reads=("onesb", "drow"), writes=("p0t",))
            em.op("dve", lambda v: v.tensor_copy(dbc[:, :], p0[:, 0:64]),
                  reads=("p0t",), writes=("dbc",))

        em.barrier()

        scAP3 = ExitStack()
        dbcr = scAP3.enter_context(nc.sbuf_tensor("dbcr", [128, 4096], BF16))
        em.op("dve", lambda v: v.tensor_copy(
            dbcr.ap().rearrange("p (h q) -> p h q", h=64),
            dbc.ap().unsqueeze(2).broadcast_to([128, 64, 64])),
            reads=("dbc",), writes=("dbcr",))
        b_res = scAP3.enter_context(nc.sbuf_tensor("b_res", [128, T_CONV], BF16))
        c_res = scAP3.enter_context(nc.sbuf_tensor("c_res", [128, T_CONV], BF16))
        dt_res = scAP3.enter_context(nc.sbuf_tensor("dt_res", [64, T_CONV], F32))
        with ExitStack() as scA:
            utb = scA.enter_context(nc.sbuf_tensor("utb", [128, 16 * T_ALL], BF16))
            for part in range(4):
                em.dma("qpool",
                       utb.ap().rearrange("p (k t) -> p k t", k=16)[:, 4 * part:4 * (part + 1), :],
                       u_t.rearrange("(k p) t -> p k t", p=128)[:, 4 * part:4 * (part + 1), :],
                       writes=("utb",))
            wxb = [scA.enter_context(nc.sbuf_tensor(f"wxb{i}", [128, 2048], BF16)) for i in range(2)]
            slab = [scA.enter_context(nc.sbuf_tensor(f"slab{i}", [128, T_ALL], BF16)) for i in range(2)]
            acc = [scA.enter_context(nc.sbuf_tensor(f"acc{i}", [128, T_CONV], BF16)) for i in range(2)]
            xact = [scA.enter_context(nc.sbuf_tensor(f"xact{i}", [128, T_CONV], BF16)) for i in range(2)]
            stat = [scA.enter_context(nc.sbuf_tensor(f"stat{i}", [128, 1024], F32)) for i in range(4)]
            statq = scA.enter_context(nc.sbuf_tensor("statq", [128, 1024], BF16))
            statacc = scA.enter_context(nc.sbuf_tensor("statacc", [128, 256], F32))
            em.op("dve", lambda v: v.memset(statacc[:, :], 0.0), writes=("statacc",))
            rmsrow = scA.enter_context(nc.sbuf_tensor("rmsrow", [2, 4], F32))
            p2p = [scA.enter_context(nc.psum_tensor(f"p2p{i}", [128, 512], F32)) for i in range(2)]
            p_st = scA.enter_context(nc.psum_tensor("p_st", [128, 8], F32))

            stat_jobs = []
            for r in range(16):
                for qr in range(8):
                    stat_jobs.append((w1t[128 * r:128 * (r + 1), 1024 * qr:1024 * (qr + 1)],
                                      r * 8 + qr))
            for r in range(64):
                for hh in range(2):
                    stat_jobs.append((w2t[128 * r:128 * (r + 1), 1024 * hh:1024 * (hh + 1)],
                                      128 + r * 2 + hh))

            def stat_dma(idx):
                src_ap, col = stat_jobs[idx]
                em.dma("qsp", stat[idx % 4][:, :], src_ap, writes=(f"stat{idx % 4}",))

            def stat_sq(idx):
                _, col = stat_jobs[idx]
                sb, sk = stat[idx % 4], f"stat{idx % 4}"
                em.op("act", lambda s, sb=sb, col=col: s.activation(
                    statq[:, :], sb[:, :], AF.Square, accum_out=statacc[:, col:col + 1]),
                    reads=(sk,), writes=("statq", "statacc"))

            stat_d = 0
            stat_s = 0

            subs = [(i * 512, 512) for i in range(5)] + [(2560, 3)]
            for c in range(XB):
                want_s = c * 256 // XB
                while stat_s < min(want_s, len(stat_jobs)):
                    while stat_d < min(stat_s + 4, len(stat_jobs)):
                        stat_dma(stat_d)
                        stat_d += 1
                    stat_sq(stat_s)
                    stat_s += 1
                wb, wkey = wxb[c % 2], f"wxb{c % 2}"
                em.dma("qpool", wb.ap().rearrange("p (k n) -> p k n", k=16),
                       w_in_x[c, :, :, :].rearrange("k p n -> p k n"), writes=(wkey,))
                sl, skey = slab[c % 2], f"slab{c % 2}"
                for si, (o, w) in enumerate(subs):
                    pb, pkey = p2p[si % 2], f"p2p{si % 2}"

                    def mmg(t, wb=wb, o=o, w=w, pb=pb):
                        for k in range(16):
                            ins = t.matmul(pb[:, 0:w], wb[:, 128 * k:128 * (k + 1)],
                                           utb[:, k * T_ALL + o:k * T_ALL + o + w],
                                           start=(k == 0), stop=(k == 15))
                        return ins
                    em.op("pe", mmg, reads=(wkey, "utb", pkey), writes=(pkey,))
                    if c == 34:
                        lo = max(o, HALO)
                        if lo < o + w:
                            em.op("dve", lambda v, pb=pb, o=o, w=w, lo=lo: v.tensor_copy(
                                dt_res[0:64, lo - HALO:o + w - HALO], pb[0:64, lo - o:w]),
                                reads=(pkey,), writes=("dt_res",))
                    else:
                        em.op("dve", lambda v, pb=pb, sl=sl, o=o, w=w: v.tensor_copy(
                            sl[:, o:o + w], pb[:, 0:w]),
                            reads=(pkey,), writes=(skey,))
                if c == 34:
                    continue
                ac, ackey = acc[c % 2], f"acc{c % 2}"
                xa, xakey = xact[c % 2], f"xact{c % 2}"
                em.op("dve", lambda v, sl=sl, ac=ac, c=c: v.tensor_scalar_mul(
                    ac[:, :], sl[:, 0:T_CONV], convw[:, 4 * c:4 * c + 1]),
                    reads=(skey, "convw"), writes=(ackey,))
                for k in range(1, 4):
                    em.op("dve", lambda v, sl=sl, ac=ac, c=c, k=k: v.scalar_tensor_tensor(
                        ac[:, :], sl[:, k:k + T_CONV], convw[:, 4 * c + k:4 * c + k + 1],
                        ac[:, :], op0=OP.mult, op1=OP.add),
                        reads=(skey, "convw", ackey), writes=(ackey,))
                em.op("act", lambda s, ac=ac, xa=xa, c=c: s.activation(
                    xa[:, :], ac[:, :], AF.Silu, bias=convb[:, c:c + 1], scale=1.0),
                    reads=(ackey, "convb"), writes=(xakey,))
                if c < 32:
                    em.dma("qact", x_sp[128 * c:128 * (c + 1), :], xa[:, :],
                           reads=(xakey,), writes=("x_sp",))
                elif c == 32:
                    em.op("dve", lambda v, xa=xa: v.tensor_copy(b_res[:, :], xa[:, :]),
                          reads=(xakey,), writes=("b_res",))
                else:
                    em.op("dve", lambda v, xa=xa: v.tensor_copy(c_res[:, :], xa[:, :]),
                          reads=(xakey,), writes=("c_res",))

            while stat_s < len(stat_jobs):
                while stat_d < min(stat_s + 4, len(stat_jobs)):
                    stat_dma(stat_d)
                    stat_d += 1
                stat_sq(stat_s)
                stat_s += 1
            em.op("dve", lambda v: v.reduce_sum(small[:, 14:15], statacc[:, 0:128], axis=AX.X),
                  reads=("statacc",), writes=("small",))
            em.op("dve", lambda v: v.reduce_sum(small[:, 15:16], statacc[:, 128:256], axis=AX.X),
                  reads=("statacc",), writes=("small",))
            em.op("pe", lambda t: t.matmul(p_st[0:1, 0:1], small[:, 14:15],
                                           small[:, 4:5], start=True, stop=True),
                  reads=("small", "p_st"), writes=("p_st",))
            em.op("pe", lambda t: t.matmul(p_st[0:1, 1:2], small[:, 15:16],
                                           small[:, 4:5], start=False, stop=True),
                  reads=("small", "p_st"), writes=("p_st",))
            NW = float(2048 * 8192)
            em.op("dve", lambda v: v.tensor_scalar(rmsrow[0:1, 0:2], p_st[0:1, 0:2],
                                                   1.0 / NW, EPS, op0=OP.mult, op1=OP.max),
                  reads=("p_st",), writes=("rmsrow",))
            em.op("act", lambda s: s.activation(rmsrow[0:1, 0:2], rmsrow[0:1, 0:2], AF.Ln),
                  reads=("rmsrow",), writes=("rmsrow",))
            em.op("dve", lambda v: v.tensor_scalar_mul(rmsrow[0:1, 0:2], rmsrow[0:1, 0:2], -0.5),
                  reads=("rmsrow",), writes=("rmsrow",))
            em.op("act", lambda s: s.activation(rmsrow[0:1, 0:2], rmsrow[0:1, 0:2], AF.Exp),
                  reads=("rmsrow",), writes=("rmsrow",))
            em.op("dve", lambda v: v.tensor_scalar_mul(rmsrow[0:1, 2:4], rmsrow[0:1, 0:2], 254.0),
                  reads=("rmsrow",), writes=("rmsrow",))
            em.op("dve", lambda v: v.reciprocal(rmsrow[0:1, 2:4], rmsrow[0:1, 2:4]),
                  reads=("rmsrow",), writes=("rmsrow",))
            em.op("pe", lambda t: t.matmul(p_st[:, 4:8], ones32[0:1, :],
                                           rmsrow[0:1, 0:4], start=True, stop=True),
                  reads=("rmsrow", "ones32", "p_st"), writes=("p_st2",))
            em.op("dve", lambda v: v.tensor_copy(small[:, 10:12], p_st[:, 4:6]),
                  reads=("p_st2",), writes=("small",))
            em.op("dve", lambda v: v.tensor_copy(small[:, 12:14], p_st[:, 6:8]),
                  reads=("p_st2",), writes=("small",))

            wzb = [scA.enter_context(nc.sbuf_tensor(f"wzb{i}", [128, 16 * 512], BF16)) for i in range(2)]
            zst = [scA.enter_context(nc.sbuf_tensor(f"zst{i}", [128, 512], BF16)) for i in range(2)]
            for zc in range(8):
                wz, wzk = wzb[zc % 2], f"wzb{zc % 2}"
                for part in range(4):
                    em.dma("qpool",
                           wz.ap().rearrange("p (k n) -> p k n", k=16)[:, 4 * part:4 * (part + 1), :],
                           w_in_z.rearrange("(k p) n -> p k n", p=128)
                           [:, 4 * part:4 * (part + 1), 512 * zc:512 * (zc + 1)],
                           writes=(wzk,))
                for m in range(16):
                    pb, pkey = p2p[m % 2], f"p2p{m % 2}"

                    def mmz(t, wz=wz, m=m, pb=pb):
                        for k in range(16):
                            o = k * T_ALL + HALO + T_FOR + 128 * m
                            ins = t.matmul(pb[:, :], utb[:, o:o + 128],
                                           wz[:, 512 * k:512 * (k + 1)],
                                           start=(k == 0), stop=(k == 15))
                        return ins
                    em.op("pe", mmz, reads=(wzk, "utb", pkey), writes=(pkey,))
                    zb, zkey = zst[m % 2], f"zst{m % 2}"
                    em.op("act", lambda s, pb=pb, zb=zb: s.activation(zb[:, :], pb[:, :], AF.Silu),
                          reads=(pkey,), writes=(zkey,))
                    em.dma("qact", zsil_sp[zc, 128 * m:128 * (m + 1), :], zb[:, :],
                           reads=(zkey,), writes=("zsil_sp",))

        em.barrier()

        with ExitStack() as sc3:
            s_st = sc3.enter_context(nc.sbuf_tensor("s_st", [128, 4096], F32))
            em.op("dve", lambda v: v.memset(s_st[:, :], 0.0), writes=("s_st",))
            xtok = [sc3.enter_context(nc.sbuf_tensor(f"xtok{i}", [128, 4096], BF16)) for i in range(2)]
            xch = [sc3.enter_context(nc.sbuf_tensor("xch0", [128, 4096], BF16))]
            ybuf = [sc3.enter_context(nc.sbuf_tensor("ybuf0", [128, 4096], BF16))]
            sbf = sc3.enter_context(nc.sbuf_tensor("sbf", [128, 4096], BF16))
            xd = sc3.enter_context(nc.sbuf_tensor("xd", [128, 4096], BF16))
            xwb = [sc3.enter_context(nc.sbuf_tensor(f"xw{i}", [128, 4096], BF16)) for i in range(2)]
            mpp = sc3.enter_context(nc.sbuf_tensor("mpp", [128, 8192], BF16))
            dtap = sc3.enter_context(nc.sbuf_tensor("dtap", [128, 8192], BF16))
            decb = [sc3.enter_context(nc.sbuf_tensor(f"decb{i}", [128, 512], BF16)) for i in range(2)]
            cbsb = sc3.enter_context(nc.sbuf_tensor("cbsb", [128, 128], BF16))
            btok = sc3.enter_context(nc.sbuf_tensor("btok", [128, 128], BF16))
            dch = sc3.enter_context(nc.sbuf_tensor("dch", [64, 128 * 16], F32))
            dchb = sc3.enter_context(nc.sbuf_tensor("dchb", [64, 256], BF16))
            dtok = sc3.enter_context(nc.sbuf_tensor("dtok", [128, 64 * 3 * NCH], BF16))
            dtokb = sc3.enter_context(nc.sbuf_tensor("dtokb", [128, 128 * NCH], BF16))
            arow = sc3.enter_context(nc.sbuf_tensor("arow", [1, 64], F32))
            cumt = sc3.enter_context(nc.sbuf_tensor("cumt", [128, 64], F32))
            evtmp = sc3.enter_context(nc.sbuf_tensor("evtmp", [128, 512], F32))
            sgbuf = [sc3.enter_context(nc.sbuf_tensor(f"sgbuf{i}", [128, 2048], F32)) for i in range(2)]
            sgout = sc3.enter_context(nc.sbuf_tensor("sgout", [128, 2048], BF16))
            sg2 = sc3.enter_context(nc.sbuf_tensor("sg2", [128, 2048], BF16))
            sgq = sc3.enter_context(nc.sbuf_tensor("sgq", [128, 2048], FP8))

            p_dec = [sc3.enter_context(nc.psum_tensor(f"p_dec{i}", [128, 512], F32)) for i in range(2)]
            p_y1 = [sc3.enter_context(nc.psum_tensor(f"p_y1{i}", [128, 512], F32)) for i in range(2)]
            p_y2 = [sc3.enter_context(nc.psum_tensor(f"p_y2{i}", [128, 512], F32)) for i in range(2)]
            p_s3 = sc3.enter_context(nc.psum_tensor("p_s3", [128, 512], F32))
            p_aux = sc3.enter_context(nc.psum_tensor("p_aux", [128, 128], F32))
            p_auxb = p_aux.ap().bitcast(BF16)

            em.op("dve", lambda v: v.memset(dch[:, 128 * 7:128 * 8], 0.0), writes=("dchz0",))
            em.op("dve", lambda v: v.memset(dch[:, 1024 + 128 * 7:1024 + 128 * 8], 0.0), writes=("dchz1",))
            DTF, DTA, LND, DLS, WDT, CUM, SCR, DZ = [
                dch[0:64, 128 * i:128 * (i + 1)] for i in range(8)]

            sign_jobs = []
            for fc in range(16):
                for qf in range(4):
                    k0 = 4 * qf
                    src = w1t.rearrange("(k p) f -> p k f", p=128)[:, k0:k0 + 4,
                                                                  512 * fc:512 * (fc + 1)]
                    sign_jobs.append((src, w1q_sp[fc, :, 2048 * qf:2048 * (qf + 1)], 10))
            for d2c in range(4):
                for ks in range(4):
                    for qf in range(4):
                        k0 = 16 * ks + 4 * qf
                        src = w2t.rearrange("(k p) f -> p k f", p=128)[:, k0:k0 + 4,
                                                                      512 * d2c:512 * (d2c + 1)]
                        sign_jobs.append((src, w2q_sp[d2c, 16 * ks + 4 * qf:16 * ks + 4 * (qf + 1), :, :]
                                          .rearrange("k p f -> p k f"), 11))

            def sign_dma(idx):
                src, _, _ = sign_jobs[idx]
                em.dma("qpool", sgbuf[idx % 2].ap().rearrange("p (k f) -> p k f", k=4),
                       src, writes=(f"sgbuf{idx % 2}",))

            def sign_emit(idx):
                _, dst_ap, rms_col = sign_jobs[idx]
                sb, sk = sgbuf[idx % 2], f"sgbuf{idx % 2}"
                em.op("act", lambda s, sb=sb, rms_col=rms_col: s.activation(
                    sgout[:, :], sb[:, :], AF.Sign, bias=small[:, 9:10],
                    scale=small[:, rms_col:rms_col + 1]),
                    reads=(sk, "small"), writes=("sgout",))
                em.op("act", lambda s, sb=sb, rms_col=rms_col: s.activation(
                    sg2[:, :], sb[:, :], AF.Sign, bias=small[:, 5:6],
                    scale=small[:, rms_col:rms_col + 1]),
                    reads=(sk, "small"), writes=("sg2",))
                em.op("dve", lambda v: v.tensor_tensor(sgq[:, :], sgout[:, :], sg2[:, :],
                                                       op=OP.add),
                      reads=("sgout", "sg2"), writes=("sgq",))
                em.dma("qpool", dst_ap,
                       sgq.ap().rearrange("p (k f) -> p k f", k=4) if dst_ap.ndim == 3
                       else sgq[:, :],
                       reads=("sgq",), writes=("wq_sp",))

            sign_d = 0
            sign_i = 0

            for c in range(NCH):
                o2 = 1024 * (c % 2)
                DTF, DTA, LND, DLS, WDT, CUM, SCR, DZ = [
                    dch[0:64, o2 + 128 * i:o2 + 128 * (i + 1)] for i in range(8)]
                CUML = dch[0:64, o2 + 128 * 5 + 127:o2 + 128 * 5 + 128]
                DB = dchb[0:64, 128 * (c % 2):128 * (c % 2) + 128]
                p = c % 2
                em.op("dve", lambda v, SCR=SCR, c=c: v.tensor_scalar_add(
                    SCR, dt_res[0:64, 128 * c:128 * (c + 1)], small[0:64, 0:1]),
                    reads=("dt_res", "small"), writes=(f"dscr{p}",))
                em.op("act", lambda s, DTF=DTF, SCR=SCR: s.activation(DTF, SCR, AF.Abs),
                      reads=(f"dscr{p}",), writes=(f"dtf{p}",))
                em.op("act", lambda s, DTF=DTF: s.activation(DTF, DTF, AF.Exp, scale=-1.0),
                      reads=(f"dtf{p}",), writes=(f"dtf{p}",))
                em.op("act", lambda s, DTF=DTF: s.activation(DTF, DTF, AF.Ln,
                                                             bias=small[0:64, 4:5], scale=1.0),
                      reads=(f"dtf{p}", "small"), writes=(f"dtf{p}",))
                em.op("dve", lambda v, SCR=SCR: v.tensor_scalar_max(SCR, SCR, 0.0),
                      reads=(f"dscr{p}",), writes=(f"dscr{p}",))
                em.op("dve", lambda v, DTF=DTF, SCR=SCR: v.tensor_tensor(DTF, DTF, SCR, op=OP.add),
                      reads=(f"dtf{p}", f"dscr{p}"), writes=(f"dtf{p}",))
                em.op("dve", lambda v, DTA=DTA, DTF=DTF: v.tensor_scalar_mul(
                    DTA, DTF, small[0:64, 3:4]),
                    reads=(f"dtf{p}", "small"), writes=(f"dta{p}",))
                em.op("dve", lambda v, CUM=CUM, DTA=DTA, DZ=DZ: v.tensor_tensor_scan(
                    CUM, DTA, DZ, 0.0, op0=OP.add, op1=OP.add),
                    reads=(f"dta{p}", f"dchz{p}"), writes=(f"cum{p}",))
                em.op("dve", lambda v, SCR=SCR, DTF=DTF: v.tensor_scalar_max(SCR, DTF, 1e-38),
                      reads=(f"dtf{p}",), writes=(f"dscr{p}",))
                em.op("act", lambda s, LND=LND, SCR=SCR: s.activation(LND, SCR, AF.Ln),
                      reads=(f"dscr{p}",), writes=(f"lnd{p}",))
                em.op("dve", lambda v, LND=LND: v.tensor_scalar_max(LND, LND, -50.0),
                      reads=(f"lnd{p}",), writes=(f"lnd{p}",))
                em.op("dve", lambda v, DLS=DLS, CUM=CUM, CUML=CUML: v.tensor_scalar(
                    DLS, CUM, CUML, None, op0=OP.subtract),
                    reads=(f"cum{p}",), writes=(f"dls{p}",))
                em.op("act", lambda s, DLS=DLS: s.activation(DLS, DLS, AF.Exp, scale=-1.0),
                      reads=(f"dls{p}",), writes=(f"dls{p}",))
                em.op("dve", lambda v, WDT=WDT, DLS=DLS, DTF=DTF: v.tensor_tensor(
                    WDT, DLS, DTF, op=OP.mult),
                    reads=(f"dls{p}", f"dtf{p}"), writes=(f"wdt{p}",))
                em.op("act", lambda s, SCR=SCR, CUML=CUML: s.activation(SCR[:, 0:1], CUML, AF.Exp),
                      reads=(f"cum{p}",), writes=(f"dscr{p}",))
                em.op("pe", lambda t, SCR=SCR: t.transpose(p_aux[0:1, 0:64], SCR[:, 0:1],
                                                           ident32[0:64, 0:64]),
                      reads=(f"dscr{p}", "ident32"), writes=("p_aux",))
                em.op("act", lambda s: s.copy(arow[0:1, :], p_aux[0:1, 0:64]),
                      reads=("p_aux",), writes=("arow",))
                em.op("pe", lambda t: t.matmul(p_aux[:, 0:64], ones32[0:1, :], arow[0:1, :],
                                               start=True, stop=True),
                      reads=("ones32", "arow", "p_aux"), writes=("p_aux",))
                em.op("dve", lambda v, c=c: v.tensor_copy(
                    dtok[:, 192 * c + 64:192 * c + 128], p_aux[:, 0:64]),
                    reads=("p_aux",), writes=(f"dtok2_{c}",))
                em.op("pe", lambda t, CUM=CUM: t.transpose(p_aux[:, 0:64], CUM,
                                                           ident32[0:64, 0:64]),
                      reads=(f"cum{p}", "ident32"), writes=("p_aux",))
                em.op("dve", lambda v: v.tensor_copy(cumt[:, :], p_aux[:, 0:64]),
                      reads=("p_aux",), writes=("cumt",))
                em.op("act", lambda s, c=c: s.activation(dtok[:, 192 * c + 0:192 * c + 64],
                                                         cumt[:, :], AF.Exp),
                      reads=("cumt",), writes=(f"dtok1_{c}",))
                em.op("pe", lambda t, WDT=WDT: t.transpose(p_aux[:, 0:64], WDT,
                                                           ident32[0:64, 0:64]),
                      reads=(f"wdt{p}", "ident32"), writes=("p_aux",))
                em.op("dve", lambda v, c=c: v.tensor_copy(
                    dtok[:, 192 * c + 128:192 * c + 192], p_aux[:, 0:64]),
                    reads=("p_aux",), writes=(f"dtok3_{c}",))
                em.op("dve", lambda v, DB=DB, DTA=DTA: v.tensor_copy(DB, DTA),
                      reads=(f"dta{p}",), writes=(f"dchb{p}",))
                em.op("pe", lambda t, DB=DB: t.transpose(p_auxb[:, 0:64], DB, ident[0:64, 0:64]),
                      reads=(f"dchb{p}", "ident"), writes=("p_aux",))
                em.op("dve", lambda v, c=c: v.tensor_copy(
                    dtokb[:, 128 * c + 0:128 * c + 64], p_auxb[:, 0:64]),
                    reads=("p_aux",), writes=(f"dtokb0_{c}",))
                em.op("dve", lambda v, DB=DB, LND=LND: v.tensor_copy(DB, LND),
                      reads=(f"lnd{p}",), writes=(f"dchb{p}",))
                em.op("pe", lambda t, DB=DB: t.transpose(p_auxb[:, 0:64], DB, ident[0:64, 0:64]),
                      reads=(f"dchb{p}", "ident"), writes=("p_aux",))
                em.op("dve", lambda v, c=c: v.tensor_copy(
                    dtokb[:, 128 * c + 64:128 * c + 128], p_auxb[:, 0:64]),
                    reads=("p_aux",), writes=(f"dtokb1_{c}",))

            for c in range(NCH):
                own = c >= KFOR
                xt, xk = xtok[c % 2], f"xtok{c % 2}"
                xc, xck = xch[0], "xch0"
                em.dma("qsp", xc.ap().rearrange("p (b t) -> p b t", b=32),
                       x_sp.rearrange("(b p) t -> p b t", p=128)[:, :, 128 * c:128 * (c + 1)],
                       reads=("x_sp",), writes=(xck,))
                for g in range(8):
                    pd, pdk = p_dec[g % 2], f"p_dec{g % 2}"

                    def xtg(t, g=g, xc=xc, pd=pd):
                        pb = pd.ap().bitcast(BF16)
                        for qq in range(4):
                            j = 4 * g + qq
                            ins = t.transpose(pb[:, 128 * qq:128 * (qq + 1)],
                                              xc[:, 128 * j:128 * (j + 1)], ident[:, :])
                        return ins
                    em.op("pe", xtg, reads=(xck, "ident", pdk), writes=(pdk,))
                    em.op("act", lambda s, g=g, pd=pd, xt=xt: s.copy(
                        xt[:, 512 * g:512 * (g + 1)], pd.ap().bitcast(BF16)[:, 0:512]),
                        reads=(pdk,), writes=(xk,))
                em.op("pe", lambda t, c=c: t.transpose(p_auxb[:, 0:128],
                                                       b_res[:, 128 * c:128 * (c + 1)], ident[:, :]),
                      reads=("b_res", "ident"), writes=("p_aux",))
                em.op("dve", lambda v: v.tensor_copy(btok[:, :], p_auxb[:, 0:128]),
                      reads=("p_aux",), writes=("btok",))
                xw, xwk = xwb[c % 2], f"xw{c % 2}"
                em.op("dve", lambda v, xt=xt, c=c, xw=xw: v.tensor_tensor(
                    xw.ap().rearrange("p (h q) -> p h q", h=64),
                    xt.ap().rearrange("p (h q) -> p h q", h=64),
                    dtok[:, 192 * c + 128:192 * c + 192].unsqueeze(2).broadcast_to([128, 64, 64]),
                    op=OP.mult),
                    reads=(xk, f"dtok3_{c}"), writes=(xwk,))
                if debug and c == KFOR + 2:
                    em.dma("qsp", dbg_c[:, 4672:4864], dtok[:, 192 * c:192 * (c + 1)],
                           reads=(f"dtok1_{c}", f"dtok2_{c}", f"dtok3_{c}"), writes=("dbg_c",))
                    em.dma("qpool", dbg_d[:, 0:2048], xw[:, 0:2048], reads=(xwk,), writes=("dbg_d",))
                if debug and c == KFOR + 3:
                    em.dma("qpool", dbg_d[:, 2048:4096], xw[:, 0:2048], reads=(xwk,), writes=("dbg_d",))
                if own:
                    em.op("dve", lambda v: v.tensor_copy(sbf[:, :], s_st[:, :]),
                          reads=("s_st",), writes=("sbf",))
                    em.op("pe", lambda t, c=c: t.matmul(p_aux[:, :], b_res[:, 128 * c:128 * (c + 1)],
                                                        c_res[:, 128 * c:128 * (c + 1)],
                                                        start=True, stop=True),
                          reads=("b_res", "c_res", "p_aux"), writes=("p_aux",))
                    em.op("act", lambda s: s.copy(cbsb[:, :], p_aux[:, :]),
                          reads=("p_aux",), writes=("cbsb",))
                    em.op("dve", lambda v, xt=xt: v.tensor_tensor(xd[:, :], xt[:, :], dbcr[:, :],
                                                                  op=OP.mult),
                        reads=(xk, "dbcr"), writes=("xd",))
                    em.op("dve", lambda v, c=c: v.tensor_tensor(
                        dtap.ap().rearrange("p (h i) -> p h i", h=64),
                        dtokb[:, 128 * c:128 * c + 64].unsqueeze(2).broadcast_to([128, 64, 128]),
                        trib.ap().unsqueeze(1).broadcast_to([128, 64, 128]), op=OP.mult),
                        reads=(f"dtokb0_{c}", "trib"), writes=("dtap",))
                    for g4 in range(16):
                        pd, pdk = p_dec[g4 % 2], f"p_dec{g4 % 2}"

                        def decmm(t, g4=g4, pd=pd, cc=c):
                            t.matmul(pd[:, :], onesb[:, :], dtap[:, 512 * g4:512 * (g4 + 1)],
                                     start=True, stop=False)
                            t.matmul(pd[:, :], ntrilb[:, :],
                                     dtokb[:, 128 * cc + 4 * g4:128 * cc + 4 * g4 + 4].unsqueeze(2)
                                     .broadcast_to([128, 4, 128]), start=False, stop=False)
                            t.matmul(pd[:, :], ident[:, :],
                                     dtokb[:, 128 * cc + 64 + 4 * g4:128 * cc + 64 + 4 * g4 + 4].unsqueeze(2)
                                     .broadcast_to([128, 4, 128]), start=False, stop=False)
                            return t.matmul(pd[:, :], nbigb[:, :],
                                            ident.ap().unsqueeze(1).broadcast_to([128, 4, 128]),
                                            start=False, stop=True)
                        em.op("pe", decmm, reads=("onesb", "dtap", f"dtokb0_{c}",
                                                  f"dtokb1_{c}", "ntrilb", "ident", "nbigb", pdk),
                              writes=(pdk,))
                        db, dbk = decb[g4 % 2], f"decb{g4 % 2}"
                        em.op("act", lambda s, pd=pd, db=db: s.activation(db[:, :], pd[:, :], AF.Exp),
                              reads=(pdk,), writes=(dbk,))
                        em.op("dve", lambda v, db=db, g4=g4: v.tensor_tensor(
                            mpp[:, 512 * g4:512 * (g4 + 1)].rearrange("p (h i) -> p h i", h=4),
                            db.ap().rearrange("p (h i) -> p h i", h=4),
                            cbsb.ap().unsqueeze(1).broadcast_to([128, 4, 128]), op=OP.mult),
                            reads=(dbk, "cbsb"), writes=("mpp",))
                for g8 in range(8):
                    if own:
                        py1, k1 = p_y1[g8 % 2], f"p_y1{g8 % 2}"
                        py2, k2 = p_y2[g8 % 2], f"p_y2{g8 % 2}"

                        def ymm(t, g8=g8, py1=py1, xt=xt):
                            t.matmul(py1[:, :], ident[:, :], xd[:, 512 * g8:512 * (g8 + 1)],
                                     start=True, stop=False)
                            for hl in range(8):
                                h = 8 * g8 + hl
                                ins = t.matmul(py1[:, 64 * hl:64 * (hl + 1)],
                                               mpp[:, 128 * h:128 * (h + 1)],
                                               xt[:, 64 * h:64 * (h + 1)],
                                               start=False, stop=(hl == 7))
                            return ins
                        em.op("pe", ymm, reads=("ident", "xd", "mpp", xk, k1), writes=(k1,))
                        em.op("pe", lambda t, py2=py2, g8=g8, c=c: t.matmul(
                            py2[:, :], c_res[:, 128 * c:128 * (c + 1)],
                            sbf[:, 512 * g8:512 * (g8 + 1)], start=True, stop=True),
                            reads=("c_res", "sbf", k2), writes=(k2,))
                        yb, ybk = ybuf[0], "ybuf0"
                        em.op("dve", lambda v, py2=py2, g8=g8, c=c: v.tensor_tensor(
                            evtmp.ap().rearrange("p (h q) -> p h q", h=8),
                            py2.ap().rearrange("p (h q) -> p h q", h=8),
                            dtok[:, 192 * c + 8 * g8:192 * c + 8 * g8 + 8]
                            .unsqueeze(2).broadcast_to([128, 8, 64]), op=OP.mult),
                            reads=(k2, f"dtok1_{c}"), writes=("evtmp",))
                        em.op("dve", lambda v, py1=py1, yb=yb, g8=g8: v.tensor_tensor(
                            yb[:, 512 * g8:512 * (g8 + 1)], evtmp[:, :], py1[:, :], op=OP.add),
                            reads=("evtmp", k1), writes=(ybk,))
                    em.op("pe", lambda t, g8=g8, xw=xw: t.matmul(
                        p_s3[:, :], btok[:, :], xw[:, 512 * g8:512 * (g8 + 1)],
                        start=True, stop=True),
                        reads=("btok", xwk, "p_s3"), writes=("p_s3",))
                    em.op("dve", lambda v, g8=g8, c=c: v.tensor_tensor(
                        s_st[:, 512 * g8:512 * (g8 + 1)].rearrange("p (h q) -> p h q", h=8),
                        s_st[:, 512 * g8:512 * (g8 + 1)].rearrange("p (h q) -> p h q", h=8),
                        dtok[:, 192 * c + 64 + 8 * g8:192 * c + 64 + 8 * g8 + 8]
                        .unsqueeze(2).broadcast_to([128, 8, 64]), op=OP.mult),
                        reads=(f"dtok2_{c}", "s_st"), writes=("s_st",))
                    em.op("dve", lambda v, g8=g8: v.tensor_tensor(
                        s_st[:, 512 * g8:512 * (g8 + 1)], s_st[:, 512 * g8:512 * (g8 + 1)],
                        p_s3[:, :], op=OP.add),
                        reads=("p_s3", "s_st"), writes=("s_st",))
                if own:
                    em.dma("qact", y_sp[128 * (c - KFOR):128 * (c - KFOR + 1), :],
                           ybuf[0][:, :], reads=("ybuf0",), writes=("y_sp",))
                if debug and c == KFOR + 3:
                    em.op("dve", lambda v: v.tensor_copy(evtmp[:, 0:64], dbc[:, :]),
                          reads=("dbc",), writes=("evtmp",))
                    em.op("dve", lambda v, c=c: v.tensor_copy(
                        evtmp.ap()[:, 64:256].rearrange("p (h q) -> p h q", q=3),
                        dtok[:, 192 * c + 128:192 * c + 192].unsqueeze(2).broadcast_to([128, 64, 3])),
                        reads=(f"dtok3_{c}",), writes=("evtmp",))
                    em.dma("qsp", dbg_a[:, 256:512], evtmp[:, 0:256], reads=("evtmp",), writes=("dbg_a",))
                    em.op("dve", lambda v, c=c: v.tensor_copy(
                        evtmp[:, 256:448], dtok[:, 192 * c:192 * (c + 1)]),
                        reads=(f"dtok1_{c}", f"dtok2_{c}", f"dtok3_{c}"), writes=("evtmp",))
                    em.dma("qsp", dbg_a[:, 0:192], evtmp[:, 256:448],
                           reads=("evtmp",), writes=("dbg_a",))
                    em.dma("qsp", dbg_a[0:64, 1024:2048], dch[0:64, 1024:2048],
                           reads=("dtf1", "dta1", "lnd1", "dls1", "wdt1", "cum1"), writes=("dbg_a",))
                    em.dma("qsp", dbg_b[:, 0:4096], xtok[c % 2][:, :], reads=(f"xtok{c % 2}",), writes=("dbg_b",))
                    em.dma("qsp", dbg_b[:, 4096:8192], sbf[:, :], reads=("sbf",), writes=("dbg_b",))
                    em.dma("qsp", dbg_c[:, 0:4096], xw[:, :], reads=(xwk,), writes=("dbg_c",))
                    em.dma("qsp", dbg_c[:, 4096:4224], cbsb[:, :], reads=("cbsb",), writes=("dbg_c",))
                    em.dma("qsp", dbg_c[:, 4224:4352], btok[:, :], reads=("btok",), writes=("dbg_c",))
                    em.dma("qsp", dbg_c[:, 4352:4480], dtokb[:, 128 * c:128 * (c + 1)], reads=(f"dtokb0_{c}", f"dtokb1_{c}"), writes=("dbg_c",))
                    em.dma("qsp", dbg_c[:, 4480:4672], dtok[:, 192 * c:192 * (c + 1)],
                           reads=(f"dtok1_{c}", f"dtok2_{c}", f"dtok3_{c}"), writes=("dbg_c",))

                if c == KFOR - 1:
                    em.op("dve", lambda v: v.tensor_scalar_mul(s_st[:, :], s_st[:, :],
                                                               small[:, 8:9]),
                          reads=("small", "s_st"), writes=("s_st",))
                want_s = (c + 1) * len(sign_jobs) // NCH
                while sign_i < want_s:
                    while sign_d < min(sign_i + 2, len(sign_jobs)):
                        sign_dma(sign_d)
                        sign_d += 1
                    sign_emit(sign_i)
                    sign_i += 1

            for part in range(8):
                em.dma("qpool", woutb_sp[512 * part:512 * (part + 1), :],
                       w_out[512 * part:512 * (part + 1), :], writes=("woutb_sp",))
            while sign_i < len(sign_jobs):
                while sign_d < min(sign_i + 2, len(sign_jobs)):
                    sign_dma(sign_d)
                    sign_d += 1
                sign_emit(sign_i)
                sign_i += 1

        scAP3.close()
        em.barrier()

        with ExitStack() as sc4a:
            nwb = sc4a.enter_context(nc.sbuf_tensor("nwb", [128, 4096], BF16))
            nwrow = sc4a.enter_context(nc.sbuf_tensor("nwrow", [1, 512], F32))
            p_nw = sc4a.enter_context(nc.psum_tensor("p_nw", [128, 512], F32))
            for j in range(8):
                em.dma("qsp", nwrow[0:1, :], norm_w[512 * j:512 * (j + 1)].unsqueeze(0),
                       writes=("nwrow",))
                em.op("pe", lambda t, j=j: t.matmul(p_nw[:, :], ones32[0:1, :],
                                                    nwrow[0:1, :], start=True, stop=True),
                      reads=("ones32", "nwrow", "p_nw"), writes=("p_nw",))
                em.op("dve", lambda v, j=j: v.tensor_copy(nwb[:, 512 * j:512 * (j + 1)],
                                                          p_nw[:, :]),
                      reads=("p_nw",), writes=("nwb",))
            yb4 = [sc4a.enter_context(nc.sbuf_tensor(f"y4_{i}", [128, 4096], BF16)) for i in range(2)]
            zb4 = [sc4a.enter_context(nc.sbuf_tensor(f"z4_{i}", [128, 4096], BF16)) for i in range(2)]
            tb4 = [sc4a.enter_context(nc.sbuf_tensor(f"tb_{i}", [128, 4096], BF16)) for i in range(2)]
            tsq4 = [sc4a.enter_context(nc.sbuf_tensor(f"tsq_{i}", [128, 4096], BF16)) for i in range(2)]
            tn4 = [sc4a.enter_context(nc.sbuf_tensor(f"tn_{i}", [128, 4096], BF16)) for i in range(2)]
            cla = sc4a.enter_context(nc.sbuf_tensor("cla", [128, 8], F32))
            for m in range(16):
                i2 = m % 2
                yb, zb, tb, tsq, tn = yb4[i2], zb4[i2], tb4[i2], tsq4[i2], tn4[i2]
                yk, zk, tbk, tqk, tnk = (f"y4_{i2}", f"z4_{i2}", f"tb_{i2}", f"tsq_{i2}", f"tn_{i2}")
                c0 = 4 * i2
                ck = f"cla{i2}"
                em.dma("qsp", yb[:, :], y_sp[128 * m:128 * (m + 1), :],
                       reads=("y_sp",), writes=(yk,))
                for zc in range(8):
                    em.dma("qsp", zb[:, 512 * zc:512 * (zc + 1)],
                           zsil_sp[zc, 128 * m:128 * (m + 1), :],
                           reads=("zsil_sp",), writes=(zk,))
                em.op("dve", lambda v, yb=yb, zb=zb, tb=tb: v.tensor_tensor(
                    tb[:, :], yb[:, :], zb[:, :], op=OP.mult),
                    reads=(yk, zk), writes=(tbk,))
                em.op("act", lambda s, tb=tb, tsq=tsq, c0=c0: s.activation(
                    tsq[:, :], tb[:, :], AF.Square, accum_out=cla[:, c0:c0 + 1]),
                    reads=(tbk,), writes=(tqk, ck))
                em.op("act", lambda s, c0=c0: s.activation(cla[:, c0 + 1:c0 + 2],
                                                           cla[:, c0:c0 + 1], AF.Ln,
                                                           bias=small[:, 7:8],
                                                           scale=1.0 / D_INNER),
                      reads=(ck, "small"), writes=(ck,))
                em.op("dve", lambda v, c0=c0: v.tensor_scalar_mul(cla[:, c0 + 1:c0 + 2],
                                                                  cla[:, c0 + 1:c0 + 2], -0.5),
                      reads=(ck,), writes=(ck,))
                em.op("act", lambda s, c0=c0: s.activation(cla[:, c0 + 2:c0 + 3],
                                                           cla[:, c0 + 1:c0 + 2], AF.Exp),
                      reads=(ck,), writes=(ck,))
                em.op("dve", lambda v, tb=tb, tn=tn, c0=c0: v.scalar_tensor_tensor(
                    tn[:, :], tb[:, :], cla[:, c0 + 2:c0 + 3], nwb[:, :],
                    op0=OP.mult, op1=OP.mult),
                    reads=(tbk, ck, "nwb"), writes=(tnk,))
                em.dma("qact", tn_sp[128 * m:128 * (m + 1), :], tn[:, :],
                       reads=(tnk,), writes=("tn_sp",))

        em.barrier()

        with ExitStack() as sc4:
            woutb = sc4.enter_context(nc.sbuf_tensor("woutb", [128, 32 * 2048], BF16))
            for part in range(8):
                em.dma("qsp",
                       woutb.ap().rearrange("p (k n) -> p k n", k=32)[:, 4 * part:4 * (part + 1), :],
                       woutb_sp.rearrange("(k p) n -> p k n", p=128)[:, 4 * part:4 * (part + 1), :],
                       reads=("woutb_sp",), writes=("woutb",))
            tnl = [sc4.enter_context(nc.sbuf_tensor(f"tnl{i}", [128, 4096], BF16)) for i in range(2)]
            ub4 = [sc4.enter_context(nc.sbuf_tensor(f"u4_{i}", [128, 2048], BF16)) for i in range(2)]
            tT4 = [sc4.enter_context(nc.sbuf_tensor(f"tT{i}", [128, 4096], BF16)) for i in range(2)]
            x1p4 = [sc4.enter_context(nc.sbuf_tensor(f"x1p{i}", [128, 2048], F32)) for i in range(2)]
            sqb = [sc4.enter_context(nc.sbuf_tensor(f"sqb{i}", [128, 2048], BF16)) for i in range(2)]
            x1b4 = [sc4.enter_context(nc.sbuf_tensor(f"x1b{i}", [128, 2048], BF16)) for i in range(2)]
            clb = sc4.enter_context(nc.sbuf_tensor("clb", [128, 12], F32))
            p_w = [sc4.enter_context(nc.psum_tensor(f"p_w{i}", [128, 512], F32)) for i in range(2)]
            p_tr32 = [sc4.enter_context(nc.psum_tensor(f"p_tr{i}", [128, 512], F32)) for i in range(2)]
            p_tr = [p.ap().bitcast(BF16) for p in p_tr32]

            for m in range(16):
                i2 = m % 2
                tn, ub, tT, x1pre, x1b, sq = tnl[i2], ub4[i2], tT4[i2], x1p4[i2], x1b4[i2], sqb[i2]
                tnk, uk, tTk, xpk, xbk, sqk = (f"tnl{i2}", f"u4_{i2}", f"tT{i2}",
                                               f"x1p{i2}", f"x1b{i2}", f"sqb{i2}")
                c0 = 6 * i2
                ck = f"clb{i2}"
                em.dma("qsp", tn[:, :], tn_sp[128 * m:128 * (m + 1), :],
                       reads=("tn_sp",), writes=(tnk,))
                em.dma("qpool", ub[:, :], u_own[128 * m:128 * (m + 1), :], writes=(uk,))
                for tt in range(8):
                    ptr, pk = p_tr[tt % 2], f"p_tr{tt % 2}"

                    def trg(t, tt=tt, ptr=ptr, tn=tn):
                        for qq in range(4):
                            j = 4 * tt + qq
                            ins = t.transpose(ptr[:, 128 * qq:128 * (qq + 1)],
                                              tn[:, 128 * j:128 * (j + 1)], ident[:, :])
                        return ins
                    em.op("pe", trg, reads=(tnk, "ident", pk), writes=(pk,))
                    em.op("act", lambda s, tt=tt, ptr=ptr, tT=tT: s.copy(
                        tT[:, 512 * tt:512 * (tt + 1)], ptr[:, 0:512]),
                        reads=(pk,), writes=(tTk,))
                for nchk in range(4):
                    pw, pk = p_w[nchk % 2], f"p_w{nchk % 2}"

                    def wmm(t, nchk=nchk, pw=pw, tT=tT):
                        for k in range(32):
                            ins = t.matmul(pw[:, :], tT[:, 128 * k:128 * (k + 1)],
                                           woutb[:, 2048 * k + 512 * nchk:2048 * k + 512 * (nchk + 1)],
                                           start=(k == 0), stop=(k == 31))
                        return ins
                    em.op("pe", wmm, reads=(tTk, "woutb", pk), writes=(pk,))
                    em.op("dve", lambda v, nchk=nchk, pw=pw, ub=ub, x1pre=x1pre: v.tensor_tensor(
                        x1pre[:, 512 * nchk:512 * (nchk + 1)], pw[:, :],
                        ub[:, 512 * nchk:512 * (nchk + 1)], op=OP.add),
                        reads=(pk, uk), writes=(xpk,))
                em.op("dve", lambda v, x1pre=x1pre, c0=c0: v.tensor_reduce(
                    clb[:, c0:c0 + 1], x1pre[:, :], axis=AX.X, op=OP.add),
                    reads=(xpk,), writes=(ck,))
                em.op("dve", lambda v, c0=c0: v.tensor_scalar_mul(clb[:, c0:c0 + 1],
                                                                  clb[:, c0:c0 + 1], 1.0 / D_MODEL),
                      reads=(ck,), writes=(ck,))
                em.op("act", lambda s, x1pre=x1pre, sq=sq, c0=c0: s.activation(
                    sq[:, :], x1pre[:, :], AF.Square, accum_out=clb[:, c0 + 1:c0 + 2]),
                    reads=(xpk,), writes=(sqk, ck))
                em.op("dve", lambda v, c0=c0: v.tensor_scalar_mul(clb[:, c0 + 1:c0 + 2],
                                                                  clb[:, c0 + 1:c0 + 2], 1.0 / D_MODEL),
                      reads=(ck,), writes=(ck,))
                em.op("dve", lambda v, c0=c0: v.scalar_tensor_tensor(
                    clb[:, c0 + 2:c0 + 3], clb[:, c0:c0 + 1], clb[:, c0:c0 + 1],
                    clb[:, c0 + 1:c0 + 2], op0=OP.mult, op1=OP.subtract),
                    reads=(ck,), writes=(ck,))
                em.op("dve", lambda v, c0=c0: v.tensor_scalar_mul(clb[:, c0 + 2:c0 + 3],
                                                                  clb[:, c0 + 2:c0 + 3], -1.0),
                      reads=(ck,), writes=(ck,))
                em.op("act", lambda s, c0=c0: s.activation(clb[:, c0 + 3:c0 + 4],
                                                           clb[:, c0 + 2:c0 + 3], AF.Ln,
                                                           bias=small[:, 7:8], scale=1.0),
                      reads=(ck, "small"), writes=(ck,))
                em.op("dve", lambda v, c0=c0: v.tensor_scalar_mul(clb[:, c0 + 3:c0 + 4],
                                                                  clb[:, c0 + 3:c0 + 4], -0.5),
                      reads=(ck,), writes=(ck,))
                em.op("act", lambda s, c0=c0: s.activation(clb[:, c0 + 3:c0 + 4],
                                                           clb[:, c0 + 3:c0 + 4], AF.Exp),
                      reads=(ck,), writes=(ck,))
                em.op("dve", lambda v, c0=c0: v.scalar_tensor_tensor(
                    clb[:, c0 + 4:c0 + 5], clb[:, c0:c0 + 1], -1.0, clb[:, c0 + 3:c0 + 4],
                    op0=OP.mult, op1=OP.mult),
                    reads=(ck,), writes=(ck,))
                em.op("dve", lambda v, x1pre=x1pre, x1b=x1b, c0=c0: v.tensor_scalar(
                    x1b[:, :], x1pre[:, :], clb[:, c0 + 3:c0 + 4], clb[:, c0 + 4:c0 + 5],
                    op0=OP.mult, op1=OP.add),
                    reads=(xpk, ck), writes=(xbk,))
                em.dma("qact", x1_sp[128 * m:128 * (m + 1), :], x1b[:, :], reads=(xbk,),
                       writes=("x1_sp",))

        em.barrier()

        qcols = stack.enter_context(nc.sbuf_tensor("qcols", [128, 64], F32))
        p_f1 = [stack.enter_context(nc.psum_tensor(f"p_f1{i}", [128, 512], F32)) for i in range(4)]
        p_tq32 = [stack.enter_context(nc.psum_tensor(f"p_tq{i}", [128, 512], F32)) for i in range(2)]
        p_tq = [p.ap().bitcast(BF16) for p in p_tq32]

        with ExitStack() as scE1:
            xqT = scE1.enter_context(nc.sbuf_tensor("xqT", [128, 16 * 2048], BF16))
            x1l = [scE1.enter_context(nc.sbuf_tensor(f"x1l{i}", [128, 2048], BF16)) for i in range(2)]
            hgrp = [scE1.enter_context(nc.sbuf_tensor(f"hgrp{i}", [128, 8192], BF16)) for i in range(4)]
            w1qb = [scE1.enter_context(nc.sbuf_tensor(f"w1qb{i}", [128, 8192], BF16)) for i in range(2)]
            y2b = scE1.enter_context(nc.sbuf_tensor("y2b", [128, 2048], F32))
            mfr = scE1.enter_context(nc.sbuf_tensor("mfr", [128, 2048], F32))
            ibuf = scE1.enter_context(nc.sbuf_tensor("ibuf", [128, 2048], mybir.dt.int32))
            qfb = scE1.enter_context(nc.sbuf_tensor("qfb", [128, 2048], F32))
            hqs = scE1.enter_context(nc.sbuf_tensor("hqs", [128, 2048], BF16))


            y2c = y2b.ap().bitcast(BF16)
            mfc = mfr.ap().bitcast(BF16)

            def emit_round(in_ap, in_key, sc_key, sc_ap, dst_ap, dst_key):
                em.op("dve", lambda v: v.tensor_scalar_mul(y2b[:, :], in_ap, sc_ap),
                      reads=(in_key, sc_key), writes=("y2b",))
                em.op("dve", lambda v: v.tensor_copy(ibuf[:, :], y2b[:, :]),
                      reads=("y2b",), writes=("ibuf",))
                em.op("dve", lambda v: v.tensor_copy(qfb[:, :], ibuf[:, :]),
                      reads=("ibuf",), writes=("qfb",))
                em.op("dve", lambda v: v.scalar_tensor_tensor(mfr[:, :], qfb[:, :], -1.0,
                                                              y2b[:, :], op0=OP.mult, op1=OP.add),
                      reads=("qfb", "y2b"), writes=("mfr",))
                em.op("act", lambda s: s.activation(y2c[:, 0:2048], mfr[:, :], AF.Sign,
                                                    bias=small[:, 9:10], scale=1.0),
                      reads=("mfr", "small"), writes=("y2b",))
                em.op("act", lambda s: s.activation(y2c[:, 2048:4096], mfr[:, :], AF.Sign,
                                                    bias=small[:, 5:6], scale=1.0),
                      reads=("mfr", "small"), writes=("y2b",))
                em.op("dve", lambda v: v.tensor_tensor(mfc[:, 0:2048], y2c[:, 0:2048],
                                                       y2c[:, 2048:4096], op=OP.add),
                      reads=("y2b",), writes=("mfr",))
                em.op("dve", lambda v: v.scalar_tensor_tensor(dst_ap, mfc[:, 0:2048], 0.5,
                                                              qfb[:, :], op0=OP.mult, op1=OP.add),
                      reads=("mfr", "qfb"), writes=(dst_key,))

            for q in range(4):
                for mi in range(4):
                    m = 4 * q + mi
                    xb, xk = x1l[mi % 2], f"x1l{mi % 2}"
                    em.dma("qsp", xb[:, :], x1_sp[128 * m:128 * (m + 1), :],
                           reads=("x1_sp",), writes=(xk,))
                    em.op("dve", lambda v, xb=xb, m=m: v.reduce_max(
                        qcols[:, m:m + 1], xb[:, :], axis=AX.X,
                        apply_absolute_value=True),
                        reads=(xk,), writes=("qamax",))
                    em.op("dve", lambda v, m=m: v.tensor_scalar_max(
                        qcols[:, 48:49], qcols[:, m:m + 1], EPS),
                        reads=("qamax",), writes=("q48",))
                    em.op("dve", lambda v: v.reciprocal(qcols[:, 49:50], qcols[:, 48:49]),
                          reads=("q48",), writes=("q49",))
                    em.op("dve", lambda v: v.tensor_scalar_mul(qcols[:, 49:50],
                                                               qcols[:, 49:50], 127.0),
                          reads=("q49",), writes=("q49",))
                    em.op("dve", lambda v, m=m: v.tensor_tensor(
                        qcols[:, 16 + m:17 + m], qcols[:, 48:49], small[:, 12:13], op=OP.mult),
                        reads=("q48", "small"), writes=("qc1",))
                    emit_round(xb[:, :], xk, "q49", qcols[:, 49:50], hqs[:, :],
                               "hqsAB")
                    for tt in range(4):
                        ptq, pk = p_tq[tt % 2], f"p_tq{tt % 2}"

                        def xtr(t, tt=tt, ptq=ptq):
                            for qq in range(4):
                                j = 4 * tt + qq
                                ins = t.transpose(ptq[:, 128 * qq:128 * (qq + 1)],
                                                  hqs[:, 128 * j:128 * (j + 1)], ident[:, :])
                            return ins
                        em.op("pe", xtr, reads=("hqsAB", "ident", pk), writes=(pk,))

                        def xev(v, tt=tt, ptq=ptq, m=m):
                            for qq in range(4):
                                k = 4 * tt + qq
                                ins = v.tensor_copy(
                                    xqT[:, 2048 * k + 128 * m:2048 * k + 128 * (m + 1)],
                                    ptq[:, 128 * qq:128 * (qq + 1)])
                            return ins
                        em.op("dve", xev, reads=(pk,), writes=("xqT",))
                if q == 0:
                    em.dma("qpool", w1qb[0][:, :], w1q_sp[0, :, :], reads=("wq_sp",),
                           writes=("w1qb0",))
                for fc in range(16):
                    wq, wk = w1qb[fc % 2], f"w1qb{fc % 2}"
                    nfc = fc + 1 if fc < 15 else 0
                    em.dma("qpool", w1qb[nfc % 2][:, :], w1q_sp[nfc, :, :], reads=("wq_sp",),
                           writes=(f"w1qb{nfc % 2}",))
                    for mi in range(4):
                        m = 4 * q + mi
                        pf, pk = p_f1[mi], f"p_f1{mi}"

                        def f1mm(t, wq=wq, m=m, pf=pf):
                            for k in range(16):
                                ins = t.matmul(pf[:, :],
                                               xqT[:, 2048 * k + 128 * m:2048 * k + 128 * (m + 1)],
                                               wq[:, 512 * k:512 * (k + 1)],
                                               start=(k == 0), stop=(k == 15))
                            return ins
                        em.op("pe", f1mm, reads=("xqT", wk, pk), writes=(pk,))
                        em.op("act", lambda s, pf=pf, mi=mi, m=m, fc=fc: s.activation(
                            hgrp[mi][:, 512 * fc:512 * (fc + 1)], pf[:, :], AF.Gelu,
                            bias=small[:, 18:19], scale=qcols[:, 16 + m:17 + m]),
                            reads=(pk, "qc1", "small"), writes=(f"hgrp{mi}",))
                if debug and q == 3:
                    em.dma("qsp", dbg_b[:, :], hgrp[0][:, :], reads=("hgrp0",), writes=("dbg_b",))
                    em.dma("qsp", dbg_a[:, 0:64], qcols[:, :], reads=("qamax", "qc1", "qc2", "q48", "q49", "q50", "q51"), writes=("dbg_a",))
                for mi in range(4):
                    m = 4 * q + mi
                    hg, hk = hgrp[mi], f"hgrp{mi}"
                    em.op("dve", lambda v, hg=hg: v.reduce_max(
                        qcols[:, 50:51], hg[:, :], axis=AX.X,
                        apply_absolute_value=True),
                        reads=(hk,), writes=("q50",))
                    em.op("dve", lambda v: v.tensor_scalar_max(qcols[:, 50:51], qcols[:, 50:51],
                                                               EPS),
                          reads=("q50",), writes=("q50",))
                    em.op("dve", lambda v: v.reciprocal(qcols[:, 51:52], qcols[:, 50:51]),
                          reads=("q50",), writes=("q51",))
                    em.op("dve", lambda v: v.tensor_scalar_mul(qcols[:, 51:52],
                                                               qcols[:, 51:52], 127.0),
                          reads=("q51",), writes=("q51",))
                    em.op("dve", lambda v, m=m: v.tensor_tensor(
                        qcols[:, 32 + m:33 + m], qcols[:, 50:51], small[:, 13:14], op=OP.mult),
                        reads=("q50", "small"), writes=("qc2",))
                    for piece in range(4):
                        sl = slice(2048 * piece, 2048 * (piece + 1))
                        emit_round(hg[:, sl], hk, "q51", qcols[:, 51:52], hg[:, sl], hk)
                    for tt in range(16):
                        ptq, pk = p_tq[tt % 2], f"p_tq{tt % 2}"

                        def htr(t, tt=tt, ptq=ptq, hg=hg):
                            for qq in range(4):
                                j = 4 * tt + qq
                                ins = t.transpose(ptq[:, 128 * qq:128 * (qq + 1)],
                                                  hg[:, 128 * j:128 * (j + 1)], ident[:, :])
                            return ins
                        em.op("pe", htr, reads=(hk, "ident", pk), writes=(pk,))
                        st_off = 512 * (tt % 2)
                        em.op("dve", lambda v, ptq=ptq, st_off=st_off: v.tensor_copy(
                            hqs[:, st_off:st_off + 512], ptq[:, 0:512]),
                            reads=(pk,), writes=("hqsAB",))
                        em.dma("qact",
                               hqT_sp.rearrange("(b p) t -> b p t", p=128)
                               [4 * tt:4 * (tt + 1), :, 128 * m:128 * (m + 1)]
                               .transpose([1, 0, 2]),
                               hqs.ap().rearrange("p (b t) -> p b t", b=16)
                               [:, 4 * (tt % 2):4 * (tt % 2 + 1), :],
                               reads=("hqsAB",), writes=("hqT_sp",))

        em.barrier()

        with ExitStack() as scE2:
            hqTl = scE2.enter_context(nc.sbuf_tensor("hqTl", [128, 64 * 512], BF16))
            w2qb = [scE2.enter_context(nc.sbuf_tensor(f"w2qb{i}", [128, 16 * 512], BF16)) for i in range(2)]
            x2pre = [scE2.enter_context(nc.sbuf_tensor(f"x2pre{i}", [128, 2048], F32)) for i in range(4)]
            x1q = [scE2.enter_context(nc.sbuf_tensor(f"x1q{i}", [128, 2048], BF16)) for i in range(4)]
            outb = scE2.enter_context(nc.sbuf_tensor("outb", [128, 2048], F32))
            sq2 = scE2.enter_context(nc.sbuf_tensor("sq2", [128, 2048], BF16))
            for q in range(4):
                for k in range(64):
                    em.dma("qsp", hqTl[:, 512 * k:512 * (k + 1)],
                           hqT_sp[128 * k:128 * (k + 1), 512 * q:512 * (q + 1)],
                           reads=("hqT_sp",), writes=("hqTl",))
                for mi in range(4):
                    m = 4 * q + mi
                    em.dma("qsp", x1q[mi][:, :], x1_sp[128 * m:128 * (m + 1), :],
                           reads=("x1_sp",), writes=(f"x1q{mi}",))
                if q == 0:
                    em.dma("qpool", w2qb[0].ap().rearrange("p (k f) -> p k f", k=16),
                           w2q_sp[0, 0:16, :, :].rearrange("k p f -> p k f"),
                           reads=("wq_sp",), writes=("w2qb0",))
                for d2c in range(4):
                    for ks in range(4):
                        st16 = 4 * d2c + ks
                        wq, wk = w2qb[st16 % 2], f"w2qb{st16 % 2}"
                        nst = (st16 + 1) % 16
                        nd2c, nks = nst // 4, nst % 4
                        em.dma("qpool", w2qb[(st16 + 1) % 2].ap().rearrange("p (k f) -> p k f", k=16),
                               w2q_sp[nd2c, 16 * nks:16 * (nks + 1), :, :].rearrange("k p f -> p k f"),
                               reads=("wq_sp",), writes=(f"w2qb{(st16 + 1) % 2}",))
                        for mi in range(4):
                            pf, pk = p_f1[mi], f"p_f1{mi}"

                            def f2mm(t, wq=wq, ks=ks, mi=mi, pf=pf):
                                for kk in range(16):
                                    k = 16 * ks + kk
                                    ins = t.matmul(
                                        pf[:, :],
                                        hqTl[:, 512 * k + 128 * mi:512 * k + 128 * (mi + 1)],
                                        wq[:, 512 * kk:512 * (kk + 1)],
                                        start=(k == 0), stop=(k == 63))
                                return ins
                            em.op("pe", f2mm, reads=("hqTl", wk, pk), writes=(pk,))
                    for mi in range(4):
                        m = 4 * q + mi
                        pf, pk = p_f1[mi], f"p_f1{mi}"
                        em.op("dve", lambda v, pf=pf, mi=mi, d2c=d2c, m=m: v.scalar_tensor_tensor(
                            x2pre[mi][:, 512 * d2c:512 * (d2c + 1)], pf[:, :],
                            qcols[:, 32 + m:33 + m], x1q[mi][:, 512 * d2c:512 * (d2c + 1)],
                            op0=OP.mult, op1=OP.add),
                            reads=(pk, "qc2", f"x1q{mi}"), writes=(f"x2pre{mi}",))
                for mi in range(4):
                    m = 4 * q + mi
                    xp, xk = x2pre[mi], f"x2pre{mi}"
                    em.op("dve", lambda v, xp=xp: v.tensor_reduce(qcols[:, 52:53], xp[:, :],
                                                                  axis=AX.X, op=OP.add),
                          reads=(xk,), writes=("q52",))
                    em.op("dve", lambda v: v.tensor_scalar_mul(qcols[:, 52:53], qcols[:, 52:53],
                                                               1.0 / D_MODEL),
                          reads=("q52",), writes=("q52",))
                    em.op("act", lambda s, xp=xp: s.activation(sq2[:, :], xp[:, :], AF.Square,
                                                               accum_out=qcols[:, 53:54]),
                          reads=(xk,), writes=("sq2", "q53"))
                    em.op("dve", lambda v: v.tensor_scalar_mul(qcols[:, 53:54], qcols[:, 53:54],
                                                               1.0 / D_MODEL),
                          reads=("q53",), writes=("q53",))
                    em.op("dve", lambda v: v.scalar_tensor_tensor(qcols[:, 54:55], qcols[:, 52:53],
                                                                  qcols[:, 52:53], qcols[:, 53:54],
                                                                  op0=OP.mult, op1=OP.subtract),
                          reads=("q52", "q53"), writes=("q54",))
                    em.op("dve", lambda v: v.tensor_scalar_mul(qcols[:, 54:55], qcols[:, 54:55],
                                                               -1.0),
                          reads=("q54",), writes=("q54",))
                    em.op("act", lambda s: s.activation(qcols[:, 54:55], qcols[:, 54:55], AF.Ln,
                                                        bias=small[:, 7:8], scale=1.0),
                          reads=("q54", "small"), writes=("q54",))
                    em.op("dve", lambda v: v.tensor_scalar_mul(qcols[:, 54:55], qcols[:, 54:55],
                                                               -0.5),
                          reads=("q54",), writes=("q54",))
                    em.op("act", lambda s: s.activation(qcols[:, 54:55], qcols[:, 54:55], AF.Exp),
                          reads=("q54",), writes=("q54",))
                    em.op("dve", lambda v: v.scalar_tensor_tensor(qcols[:, 55:56], qcols[:, 52:53],
                                                                  -1.0, qcols[:, 54:55],
                                                                  op0=OP.mult, op1=OP.mult),
                          reads=("q52", "q54"), writes=("q55",))
                    em.op("dve", lambda v, xp=xp: v.tensor_scalar(outb[:, :], xp[:, :],
                                                                  qcols[:, 54:55], qcols[:, 55:56],
                                                                  op0=OP.mult, op1=OP.add),
                          reads=(xk, "q54", "q55"), writes=("outb",))
                    em.dma("qact", out_t[128 * m:128 * (m + 1), :], outb[:, :], reads=("outb",),
                           writes=("out_t",))

        em.wait_all("sp")
        em.wait_all("pool")
        em.flush(block)
    stack.close()
    return nc


_NC_CACHE = {}


def kernel(**inputs):
    u = np.ascontiguousarray(np.asarray(inputs["L0_attn_u"], np.float32))
    W_in = np.ascontiguousarray(np.asarray(inputs["L0_attn_W_in"], np.float32))
    conv_w = np.asarray(inputs["L0_attn_conv_w"], np.float32)
    conv_b = np.asarray(inputs["L0_attn_conv_b"], np.float32)
    dt_bias = np.asarray(inputs["L0_attn_dt_bias"], np.float32)
    A_log = np.asarray(inputs["L0_attn_A_log"], np.float32)
    Dv = np.asarray(inputs["L0_attn_D"], np.float32)
    norm_w = np.asarray(inputs["L0_attn_norm_w"], np.float32)
    W_out = np.ascontiguousarray(np.asarray(inputs["L0_attn_W_out"], np.float32))
    w1 = np.asarray(inputs["L0_ffn_w1"], np.float32)
    w2 = np.asarray(inputs["L0_ffn_w2"], np.float32)

    wx_pad = np.zeros((D_MODEL, XB * 128), np.float32)
    wx_pad[:, :4416] = W_in[:, D_INNER:]
    w_in_x = np.ascontiguousarray(
        wx_pad.reshape(16, 128, XB, 128).transpose(2, 0, 1, 3))
    w_in_z = np.ascontiguousarray(W_in[:, :D_INNER])
    w1t = np.ascontiguousarray(w1.T)
    w2t = np.ascontiguousarray(w2.T)
    convw_pad = np.zeros((XB * 128, 4), np.float32)
    convw_pad[:4352] = conv_w
    convb_pad = np.zeros((XB * 128,), np.float32)
    convb_pad[:4352] = conv_b
    small_c = np.zeros((64, 4), np.float32)
    small_c[:, 0] = dt_bias
    small_c[:, 1] = A_log
    small_r = np.zeros((4, 64), np.float32)
    small_r[2] = Dv

    in_maps = []
    for c in range(8):
        b, h = c // 2, c % 2
        g0 = h * T_OWN
        ut = np.zeros((D_MODEL, T_ALL), np.float32)
        lo = g0 - (T_FOR + HALO)
        src_lo = max(lo, 0)
        ut[:, src_lo - lo:T_FOR + HALO] = u[b, src_lo:g0].T
        ut[:, T_FOR + HALO:] = u[b, g0:g0 + T_OWN].T
        mask = np.full((128, 1), 0.0 if h == 0 else 1.0, np.float32)
        in_maps.append(dict(
            u_t=np.ascontiguousarray(ut),
            u_own=np.ascontiguousarray(u[b, g0:g0 + T_OWN]),
            w_in_x=w_in_x, w_in_z=w_in_z, w_out=W_out, w1t=w1t, w2t=w2t,
            conv_wt=convw_pad, conv_bt=convb_pad, small_c=small_c,
            small_r=small_r, norm_w=norm_w, seq_mask=mask))

    if "nc" not in _NC_CACHE:
        _NC_CACHE["nc"] = build_program()
    nc = _NC_CACHE["nc"]
    res = run_bass_kernel_spmd(nc, in_maps, core_ids=list(range(8)))
    out = np.zeros((4, 4096, D_MODEL), np.float32)
    for c in range(8):
        b, h = c // 2, c % 2
        out[b, h * T_OWN:(h + 1) * T_OWN] = res.results[c]["out"]
    return out
